# version 10
# baseline (speedup 1.0000x reference)
import sys

import numpy as np

if "/opt/trn_rl_repo" not in sys.path:
    sys.path.insert(0, "/opt/trn_rl_repo")

import ml_dtypes

import concourse.bass as bass
import concourse.mybir as mybir
from concourse.bass_utils import run_bass_kernel_spmd

F32 = mybir.dt.float32
BF16 = mybir.dt.bfloat16
AF = mybir.ActivationFunctionType
ALU = mybir.AluOpType
AX = mybir.AxisListType
BFNP = ml_dtypes.bfloat16

NCORE = 8
D = 2048
NTOK = 8192
TPC = NTOK // NCORE
S = 4096
HD = 128
NHC = 2
DT = D // 128
F = 1408
FT = F // 128
BLK = 512
NB = NTOK // BLK
QBN = S // BLK
SCALE = float(1.0 / np.sqrt(128.0))
KB = 1024

SEM_NAMES = ["sP", "sV", "sA", "sPl",
             "dPre", "dX", "dAgxA", "dAgxB", "dAgx20", "dAgx21", "dAgt", "dMask",
             "dXn0", "dXn1", "dOtW0", "dOtW1", "dStg0", "dStg1", "dStg2", "dStg3",
             "dOtl0", "dOtl1", "dGa00", "dGa01", "dGa10", "dGa11", "dSc",
             "dR0", "dR1", "dRo0", "dRo1", "dRb0", "dRb1", "dRt0", "dRt1", "dRg",
             "dCl0", "dCl1", "dCo0", "dCo1", "dDbg",
             "dQkv", "dGate", "dUp", "dDown", "dWo",
             "cAgx", "cAgxB", "cAgt", "cAgk", "cRs"]
NSEL = 2560
NSB = NSEL // BLK
PADIDX = NTOK


def build_program(debug=False, upto=None):
    nc = bass.Bass("TRN2", target_bir_lowering=False, debug=False)

    def din(name, shape, dt=F32):
        return nc.dram_tensor(name, shape, dt, kind="ExternalInput").ap()

    xT = din("xT", [D, TPC])
    wqT = din("wqT", [D, NHC * HD])
    wkT = din("wkT", [D, NHC * HD])
    wvT = din("wvT", [D, NHC * HD])
    woT = din("woT", [NHC * HD, D])
    bqh = din("bqh", [HD, NHC])
    bkh = din("bkh", [HD, NHC])
    bvr = din("bvr", [1, NHC * HD])
    bor = din("bor", [1, D])
    xTok = din("xTok", [TPC, D])
    tokids = din("tokids", [128, 64], mybir.dt.int32)
    pad8192 = din("pad8192", [128, NSEL // 128], mybir.dt.int32)
    idbf = din("idbf", [128, 128], BF16)
    gwT = din("gwT", [D, 8])
    gTd = din("gTd", [D, F])
    uTd = din("uTd", [D, F])
    dTd = din("dTd", [F, D])
    selb = din("selb", [8, 128])
    dmp512 = din("dmp512", [1, BLK])
    masks = din("masks", [4, 128, BLK], BF16)
    idf32 = din("idf32", [128, 128])

    outTok = nc.dram_tensor("outTok", [TPC, D], F32, kind="ExternalOutput").ap()

    agin_x = [nc.dram_tensor(f"agin_x{h}", [D, BLK], BF16) for h in range(2)]
    agout_x = [nc.dram_tensor(f"agout_x{h}", [NCORE * D, BLK], BF16) for h in range(2)]
    agin_x2 = nc.dram_tensor("agin_x2", [TPC, D], BF16)
    xnTok = nc.dram_tensor("xnTok", [NTOK + 128, D], BF16)
    agin_t = nc.dram_tensor("agin_t", [8, TPC], F32)
    agout_t = nc.dram_tensor("agout_t", [NCORE * 8, TPC], F32)
    twtok = nc.dram_tensor("twtok", [NTOK + 128, 1], F32)
    seld = nc.dram_tensor("seld", [NSEL + 128, 1], mybir.dt.int32)
    tgtd = nc.dram_tensor("tgtd", [NTOK, 1], mybir.dt.int32)
    partT = nc.dram_tensor("partT", [NTOK + 128, D], F32)
    rsoutT = nc.dram_tensor("rsoutT", [TPC, D], F32)

    dbg = {}
    if debug:
        for nm, t in [("d_agx", agout_x[0]), ("d_agt", agout_t),
                      ("d_sel", seld), ("d_twtok", twtok),
                      ("d_part", partT), ("d_rs", rsoutT)]:
            dbg[nm] = (nc.dram_tensor(nm, list(t.ap().shape), t.ap().dtype,
                                      kind="ExternalOutput").ap(), t)

    ARENA = 207 * KB
    arena = nc.alloc_sbuf_tensor("arena", [128, ARENA], mybir.dt.uint8)

    I32 = mybir.dt.int32

    def cv(off, nelem, dt, parts=128):
        nb = {F32: 4, BF16: 2, I32: 4}[dt]
        assert off % 64 == 0 and off + nelem * nb <= ARENA, f"bad carve {off}"
        return arena.ap()[0:parts, off:off + nelem * nb].bitcast(dt)

    o = 0
    onesm = cv(o, 128, F32); o += 512
    ones_col_bf = cv(o, 1, BF16); o += 64
    ones_row = cv(o, 128, F32, parts=1); o += 512
    neg8 = cv(o, 1, F32); o += 64
    eps128 = cv(o, 1, F32); o += 64
    bq_sb = cv(o, 2, F32); o += 64
    bk_sb = cv(o, 2, F32); o += 64
    sel_sb = cv(o, NSEL // 128, I32); o += 128
    twg_sb = cv(o, 8, F32); o += 64
    bv_bc = cv(o, NHC * HD, BF16); o += 512
    bvr_sb = cv(o, NHC * HD, F32, parts=1); o += 1024
    idf32_sb = cv(o, 128, F32); o += 512
    idbf_sb = cv(o, 128, BF16); o += 256
    otl = [cv(o + i * 256, 128, BF16) for i in range(4)]; o += 1024
    gwT_sb = cv(o, DT * 8, F32); o += 512
    selb_sb = cv(o, 128, F32, parts=8); o += 512
    assert o <= 6 * KB

    xnb = [cv(6 * KB + i * 16 * KB, DT * BLK, BF16) for i in range(2)]
    pt_t = [cv(6 * KB + i * KB, BLK, BF16) for i in range(4)]
    pt_pair = [cv(6 * KB + i * 2 * KB, 2 * BLK, BF16) for i in range(2)]
    mask_sb = [cv(10 * KB + i * KB, BLK, BF16) for i in range(4)]
    otv = [cv(14 * KB + i * KB, BLK, BF16) for i in range(2)]
    rec_bc = cv(16 * KB, BLK, BF16)
    rrow = [cv(17 * KB + 2048 * i, BLK, F32, parts=1) for i in range(2)]

    QT_sb = cv(38 * KB, NHC * NTOK, BF16)
    KT_sb = cv(70 * KB, NHC * NTOK, BF16)
    V_sb = cv(102 * KB, 64 * NHC * HD, BF16)
    xblk = cv(38 * KB, DT * BLK, F32)
    xn32 = cv(70 * KB, DT * BLK, F32)
    sqt = [cv(102 * KB + i * 2 * KB, BLK, F32) for i in range(2)]
    rstd_t = cv(106 * KB, BLK, F32)
    twt = [cv(108 * KB + i * 2 * KB, BLK, F32, parts=8) for i in range(2)]
    uT_sb = cv(38 * KB, DT * F, BF16)
    dT_sb = cv(82 * KB, FT * D, BF16)
    wo_sb = cv(126 * KB, NHC * D, BF16)
    xtk = [cv(112 * KB + i * 4 * KB, D, BF16) for i in range(2)]
    ct_rs = [cv(38 * KB + i * 8 * KB, D, F32) for i in range(2)]
    ct_x = [cv(54 * KB + i * 8 * KB, D, F32) for i in range(2)]
    ct_o = [cv(70 * KB + i * 8 * KB, D, F32) for i in range(2)]

    gT_sb = cv(134 * KB, DT * F, BF16)
    twtile2 = [cv(134 * KB + i * 2 * KB, BLK, F32, parts=8) for i in range(2)]
    twseg = [cv(138 * KB + i * 2 * KB, BLK, F32, parts=1) for i in range(2)]
    tgtseg = [cv(142 * KB + i * 2 * KB, BLK, I32, parts=1) for i in range(2)]
    maskseg = cv(146 * KB, BLK, F32, parts=1)
    posseg = cv(148 * KB, BLK, F32, parts=1)
    t1seg = cv(150 * KB, BLK, F32, parts=1)
    u1seg = cv(152 * KB, BLK, F32, parts=1)
    tgtfseg = cv(154 * KB, BLK, F32, parts=1)
    ones512r = cv(156 * KB, BLK, F32, parts=1)
    zeros512r = cv(170 * KB, BLK, F32, parts=1)
    baserow = cv(158 * KB, 1, F32, parts=1)
    zpad = cv(159 * KB, D, BF16)
    zpad_f32 = cv(159 * KB, 1, F32)
    ot_bf = cv(142 * KB, 2 * NTOK, BF16)
    tok_sb = cv(174 * KB, 64, I32)
    tgt_sb = cv(174 * KB + 256, 64, I32)
    dmp_sb = cv(175 * KB, BLK, F32, parts=1)

    wq_sb = cv(178 * KB, DT * NHC * HD, BF16)
    wk_sb = cv(186 * KB, DT * NHC * HD, BF16)
    wv_sb = cv(194 * KB, DT * NHC * HD, BF16)
    borow = cv(178 * KB, D, F32, parts=1)
    bo_bc = cv(186 * KB, D, BF16)
    stg25 = [cv(190 * KB + i * 2 * KB, BLK, F32) for i in range(4)]
    xg = [cv(198 * KB + i * 4 * KB, D, BF16) for i in range(2)]
    hp_t = [cv(178 * KB + i * KB, BLK, BF16) for i in range(FT)]
    sg_t = [cv(188 * KB, BLK, BF16), cv(206 * KB, BLK, BF16)]
    eo_bf = [cv(189 * KB + i * 256, 128, BF16) for i in range(2)]
    eo_stage = cv(126 * KB, D, F32)

    psS = [nc.alloc_psum_tensor(f"psS{i}", [128, 2 * BLK], F32) for i in range(2)]
    pbO = [nc.alloc_psum_tensor(f"pbO{i}", [128, BLK], F32) for i in range(2)]
    pbBc = nc.alloc_psum_tensor("pbBc", [128, BLK], F32)
    pbR = nc.alloc_psum_tensor("pbR", [128, BLK], F32)
    pb = [psS[0].ap()[:, 0:BLK], psS[0].ap()[:, BLK:2 * BLK],
          psS[1].ap()[:, 0:BLK], psS[1].ap()[:, BLK:2 * BLK],
          pbBc.ap(), pbO[0].ap(), pbO[1].ap(), pbR.ap()]

    C = {k: 0 for k in SEM_NAMES}
    sem = {k: nc.semaphore(k).__enter__() for k in SEM_NAMES}

    def inc(name, ins, n=None):
        if n is None:
            n = 1 if name in ("sP", "sV", "sA") or name.startswith("c") else 16
        ins.then_inc(sem[name], n)
        C[name] += n
        return C[name]

    M = {}
    block = nc.Block().__enter__()

    wqv = wq_sb.rearrange("p (t h) -> p t h", t=DT)
    wkv = wk_sb.rearrange("p (t h) -> p t h", t=DT)
    wvv = wv_sb.rearrange("p (t h) -> p t h", t=DT)
    gwv = gwT_sb.rearrange("p (t e) -> p t e", t=DT)
    gv = gT_sb.rearrange("p (t f) -> p t f", t=DT)
    uv = uT_sb.rearrange("p (t f) -> p t f", t=DT)
    dvw = dT_sb.rearrange("p (t d) -> p t d", t=FT)
    wov = wo_sb.rearrange("p (h d) -> p h d", h=NHC)
    xnv = [bb.rearrange("p (t q) -> p t q", t=DT) for bb in xnb]

    def _(q):
        inc("dPre", q.dma_start(out=idf32_sb, in_=idf32))
        inc("dPre", q.dma_start(out=bq_sb, in_=bqh))
        inc("dPre", q.dma_start(out=bk_sb, in_=bkh))
        inc("dPre", q.dma_start(out=idbf_sb, in_=idbf))
        inc("dPre", q.dma_start(out=tok_sb, in_=tokids))
        inc("dPre", q.dma_start(out=bvr_sb, in_=bvr))
        inc("dPre", q.dma_start(out=gwv, in_=gwT.rearrange("(t p) e -> p t e", p=128)))
        inc("dPre", q.dma_start(out=selb_sb, in_=selb))
        inc("dPre", q.dma_start(out=dmp_sb, in_=dmp512))
    block.sync(_)
    PRE_ALL = C["dPre"]

    def _(q):
        inc("dQkv", q.dma_start(out=wqv, in_=wqT.rearrange("(t p) h -> p t h", p=128)))
        inc("dQkv", q.dma_start(out=wkv, in_=wkT.rearrange("(t p) h -> p t h", p=128)))
        inc("dQkv", q.dma_start(out=wvv, in_=wvT.rearrange("(t p) h -> p t h", p=128)))
    block.gpsimd(_)

    def _(v):
        v.memset(onesm, 1.0)
        v.memset(ones_col_bf, 1.0)
        v.memset(ones_row, 1.0)
        v.memset(neg8, -8.0)
        M["consts"] = inc("sV", v.memset(eps128, 1e-5))
    block.vector(_)

    def _(t):
        t.wait_ge(sem["dPre"], PRE_ALL)
        t.wait_ge(sem["sV"], M["consts"])
        M["bv_pe"] = inc("sP", t.matmul(pb[7][:, 0:NHC * HD], ones_row, bvr_sb,
                                        start=True, stop=True))
    block.tensor(_)

    def _(v):
        v.wait_ge(sem["sP"], M["bv_pe"])
        M["bv"] = inc("sV", v.tensor_copy(bv_bc, pb[7][:, 0:NHC * HD]))
    block.vector(_)

    for j in range(2):
        def _(q, j=j):
            if j > 0:
                q.wait_ge(sem["sP"], M[("Arpe", j - 1, 3)])
            inc("dX", q.dma_start(
                out=xblk.rearrange("p (t q) -> p t q", t=DT),
                in_=xT.rearrange("(t p) q -> p t q", p=128)[:, :, j * BLK:(j + 1) * BLK]))
        block.sync(_)

        def _(t, j=j):
            t.wait_ge(sem["dX"], (j + 1) * 16)
            t.wait_ge(sem["sV"], M["consts"])
            if j > 0:
                t.wait_ge(sem["sV"], M[("Axn1", j - 1)])
            for i in range(DT):
                ins = t.matmul(pb[0][:, :], onesm,
                               xblk[:, i * BLK:(i + 1) * BLK],
                               start=(i == 0), stop=(i == DT - 1))
            M[("Asum", j)] = inc("sP", ins)
        block.tensor(_)

        for i in range(DT):
            def _(sc, j=j, i=i):
                if i == 0:
                    sc.wait_ge(sem["dX"], (j + 1) * 16)
                    if j > 0:
                        sc.wait_ge(sem["sV"], M[("Atw", j - 1, 3)])
                        sc.wait_ge(sem["sP"], M[("AtwT", j - 1, 3)])
                if i >= 2:
                    sc.wait_ge(sem["sP"], M[("Asqpe", j, i - 2)])
                M[("Asq", j, i)] = inc("sA", sc.activation(
                    sqt[i % 2], xblk[:, i * BLK:(i + 1) * BLK], AF.Square))
            block.scalar(_)

            def _(t, j=j, i=i):
                t.wait_ge(sem["sA"], M[("Asq", j, i)])
                M[("Asqpe", j, i)] = inc("sP", t.matmul(
                    pb[1][:, :], onesm, sqt[i % 2],
                    start=(i == 0), stop=(i == DT - 1)))
            block.tensor(_)

        def _(v, j=j):
            v.wait_ge(sem["sP"], M[("Asum", j)])
            v.wait_ge(sem["sP"], M[("Asqpe", j, DT - 1)])
            w = inc("sV", v.tensor_scalar_mul(sqt[0], pb[0][:, :], 1.0 / D))
            w = inc("sV", v.tensor_scalar_mul(sqt[1], pb[1][:, :], 1.0 / D))
            v.wait_ge(sem["sV"], w)
            w = inc("sV", v.tensor_mul(sqt[0], sqt[0], sqt[0]))
            v.wait_ge(sem["sV"], w)
            M[("Avar", j)] = inc("sV", v.tensor_sub(sqt[1], sqt[1], sqt[0]))
        block.vector(_)

        def _(sc, j=j):
            sc.wait_ge(sem["sV"], M[("Avar", j)])
            M[("Astd", j)] = inc("sA", sc.activation(rstd_t, sqt[1], AF.Sqrt,
                                                     bias=eps128, scale=1.0))
        block.scalar(_)

        def _(v, j=j):
            v.wait_ge(sem["sA"], M[("Astd", j)])
            M[("Arstd", j)] = inc("sV", v.reciprocal(rstd_t, rstd_t))
            v.wait_ge(sem["sV"], M[("Arstd", j)])
            for i in range(DT):
                ins = v.scalar_tensor_tensor(xn32[:, i * BLK:(i + 1) * BLK],
                                             pb[0][:, :], -1.0 / D,
                                             xblk[:, i * BLK:(i + 1) * BLK],
                                             ALU.mult, ALU.add)
            w = inc("sV", ins)
            v.wait_ge(sem["sV"], w)
            for i in range(DT):
                ins = v.tensor_mul(xn32[:, i * BLK:(i + 1) * BLK],
                                   xn32[:, i * BLK:(i + 1) * BLK], rstd_t)
            M[("Axn1", j)] = inc("sV", ins)
            v.wait_ge(sem["sV"], M[("Axn1", j)])
            for i in range(DT):
                ins = v.tensor_copy(xnb[j][:, i * BLK:(i + 1) * BLK],
                                    xn32[:, i * BLK:(i + 1) * BLK])
            M[("Axncp", j)] = inc("sV", ins)
        block.vector(_)

        def _(q, j=j):
            q.wait_ge(sem["sV"], M[("Axncp", j)])
            inc("dAgxA" if j == 0 else "dAgxB", q.dma_start(
                out=agin_x[j].ap().rearrange("(t p) q -> p t q", p=128),
                in_=xnv[j]))
        block.sync(_)

        pbT = [pbO[1].ap()[:, 0:64].bitcast(BF16), pbR.ap()[:, 0:64].bitcast(BF16)]
        for tt4 in range(4):
            for i in range(DT):
                M[("Axtk", j, tt4, i)] = C["sP"] + i + 1
                M[("Axtke", j, tt4, i)] = C["sV"] + i + 1

            def _(t, j=j, tt4=tt4):
                if tt4 == 0:
                    t.wait_ge(sem["sV"], M[("Axncp", j)])
                for i in range(DT):
                    if j > 0 or tt4 > 0 or i >= 2:
                        pi = (j, tt4, i - 2) if i >= 2 else                              ((j, tt4 - 1, DT - 2 + i) if tt4 > 0 else (j - 1, 3, DT - 2 + i))
                        t.wait_ge(sem["sV"], M[("Axtke",) + pi])
                    ins = t.transpose(pbT[i % 2],
                                      xnv[j][:, i, tt4 * 128:(tt4 + 1) * 128], idbf_sb)
                    assert inc("sP", ins) == M[("Axtk", j, tt4, i)]
            block.tensor(_)

            def _(v, j=j, tt4=tt4):
                if tt4 == 0:
                    v.wait_ge(sem["sV"], M[("Axncp", j)])
                for i in range(DT):
                    v.wait_ge(sem["sP"], M[("Axtk", j, tt4, i)])
                    if i == 0 and (j > 0 or tt4 >= 2):
                        pt_ = (j, tt4 - 2) if tt4 >= 2 else (j - 1, 2 + tt4)
                        v.wait_ge(sem[f"dAgx2{tt4 % 2}"], M[("Axtkd",) + pt_])
                    ins = v.tensor_copy(xtk[tt4 % 2][:, i * 128:(i + 1) * 128], pbT[i % 2])
                    assert inc("sV", ins) == M[("Axtke", j, tt4, i)]
            block.vector(_)

            def _(q, j=j, tt4=tt4):
                q.wait_ge(sem["sV"], M[("Axtke", j, tt4, DT - 1)])
                ins = q.dma_start(out=agin_x2.ap()[j * BLK + tt4 * 128:j * BLK + (tt4 + 1) * 128, :],
                                  in_=xtk[tt4 % 2])
                M[("Axtkd", j, tt4)] = inc(f"dAgx2{tt4 % 2}", ins)
            block.sync(_)

        for tt in range(4):
            def _(t, j=j, tt=tt):
                if tt == 0:
                    t.wait_ge(sem["dPre"], PRE_ALL)
                    t.wait_ge(sem["sV"], M[("Axn1", j)])
                prev = (j, tt - 1) if tt > 0 else ((j - 1, 3) if j > 0 else None)
                if prev is not None:
                    t.wait_ge(sem["sV"], M[("ArL", prev[0], prev[1])])
                for i in range(DT):
                    ins = t.matmul(pb[4][0:128, 0:8],
                                   xn32[:, i * BLK + tt * 128:i * BLK + tt * 128 + 128],
                                   gwv[:, i, :], start=(i == 0), stop=(i == DT - 1))
                M[("Arpe", j, tt)] = inc("sP", ins)
            block.tensor(_)

            def _(v, j=j, tt=tt):
                L = sqt[0][:, 0:8]; m1 = sqt[0][:, 8:9]; k1 = sqt[0][:, 16:24]
                L2 = sqt[0][:, 24:32]; m2 = sqt[0][:, 32:33]; k2 = sqt[0][:, 40:48]
                dd = sqt[0][:, 48:49]
                v.wait_ge(sem["sP"], M[("Arpe", j, tt)])
                M[("ArL", j, tt)] = w = inc("sV", v.tensor_copy(L, pb[4][0:128, 0:8]))
                v.wait_ge(sem["sV"], w)
                w = inc("sV", v.reduce_max(m1, L, axis=AX.X))
                v.wait_ge(sem["sV"], w)
                w = inc("sV", v.tensor_scalar(k1, L, m1, None, ALU.is_equal))
                v.wait_ge(sem["sV"], w)
                w = inc("sV", v.scalar_tensor_tensor(L2, k1, -1e30, L, ALU.mult, ALU.add))
                v.wait_ge(sem["sV"], w)
                w = inc("sV", v.reduce_max(m2, L2, axis=AX.X))
                v.wait_ge(sem["sV"], w)
                v.tensor_scalar(k2, L2, m2, None, ALU.is_equal)
                M[("Ar1", j, tt)] = inc("sV", v.tensor_sub(dd, m2, m1))
            block.vector(_)

            def _(sc, j=j, tt=tt):
                sc.wait_ge(sem["sV"], M[("Ar1", j, tt)])
                M[("Ar2", j, tt)] = inc("sA", sc.activation(
                    sqt[0][:, 49:50], sqt[0][:, 48:49], AF.Exp))
            block.scalar(_)

            def _(v, j=j, tt=tt):
                k1 = sqt[0][:, 16:24]; k2 = sqt[0][:, 40:48]
                ee = sqt[0][:, 49:50]; rr = sqt[0][:, 50:51]; w2 = sqt[0][:, 51:52]
                tw = sqt[1][:, 0:8]
                v.wait_ge(sem["sA"], M[("Ar2", j, tt)])
                w = inc("sV", v.tensor_scalar_add(rr, ee, 1.0))
                v.wait_ge(sem["sV"], w)
                w = inc("sV", v.reciprocal(rr, rr))
                v.wait_ge(sem["sV"], w)
                w = inc("sV", v.tensor_mul(w2, ee, rr))
                v.wait_ge(sem["sV"], w)
                prev = (j, tt - 1) if tt > 0 else ((j - 1, 3) if j > 0 else None)
                if prev is not None:
                    v.wait_ge(sem["sP"], M[("AtwT", prev[0], prev[1])])
                w = inc("sV", v.tensor_scalar(tw, k1, rr, None, ALU.mult))
                v.wait_ge(sem["sV"], w)
                M[("Atw", j, tt)] = inc("sV", v.scalar_tensor_tensor(
                    tw, k2, w2, tw, ALU.mult, ALU.add))
            block.vector(_)

            def _(t, j=j, tt=tt):
                t.wait_ge(sem["sV"], M[("Atw", j, tt)])
                M[("AtwT", j, tt)] = inc("sP", t.transpose(
                    pb[5][0:8, 0:128], sqt[1][:, 0:8], idf32_sb))
            block.tensor(_)

            def _(v, j=j, tt=tt):
                v.wait_ge(sem["sP"], M[("AtwT", j, tt)])
                M[("Atwc", j, tt)] = inc("sV", v.tensor_copy(
                    twt[j][0:8, tt * 128:(tt + 1) * 128], pb[5][0:8, 0:128]))
            block.vector(_)

        def _(q, j=j):
            q.wait_ge(sem["sV"], M[("Atwc", j, 3)])
            inc("dAgt", q.dma_start(out=agin_t.ap()[0:8, j * BLK:(j + 1) * BLK],
                                    in_=twt[j][0:8, :]))
        block.sync(_)

    if upto == "A":
        dbgA = {}
        for nm, shape, dt in [("t_xblk", [128, DT * BLK], F32),
                              ("t_xn32", [128, DT * BLK], F32),
                              ("t_srow", [128, BLK], F32),
                              ("t_agx", [D, TPC], BF16),
                              ("t_agt", [8, TPC], F32)]:
            dbgA[nm] = nc.dram_tensor(nm, shape, dt, kind="ExternalOutput").ap()

        def _(q):
            q.wait_ge(sem["sV"], M[("Axncp", 1)])
            q.wait_ge(sem["sV"], M[("Atwc", 1, 3)])
            q.wait_ge(sem["dAgx"], C["dAgx"])
            q.wait_ge(sem["dAgt"], C["dAgt"])
            inc("dDbg", q.dma_start(out=dbgA["t_xblk"], in_=xblk))
            inc("dDbg", q.dma_start(out=dbgA["t_xn32"], in_=xn32))
            inc("dDbg", q.dma_start(out=dbgA["t_srow"], in_=rstd_t))
            inc("dDbg", q.dma_start(out=dbgA["t_agx"], in_=agin_x.ap()))
            inc("dDbg", q.dma_start(out=dbgA["t_agt"], in_=agin_t.ap()))
            q.wait_ge(sem["dDbg"], C["dDbg"])
            q.wait_ge(sem["dQkv"], C["dQkv"])
            q.wait_ge(sem["dGate"], C["dGate"])
        block.sync(_)
        block.__exit__(None, None, None)
        return nc

    def _(q):
        q.wait_ge(sem["dAgt"], 32)
        inc("cAgt", q.collective_compute(
            "AllGather", ALU.bypass, replica_groups=[list(range(NCORE))],
            ins=[agin_t.ap().opt()], outs=[agout_t.ap().opt()]), 1)
        q.wait_ge(sem["dAgxA"], 16)
        inc("cAgx", q.collective_compute(
            "AllGather", ALU.bypass, replica_groups=[list(range(NCORE))],
            ins=[agin_x[0].ap().opt()], outs=[agout_x[0].ap().opt()]), 1)
        q.wait_ge(sem["dAgxB"], 16)
        inc("cAgxB", q.collective_compute(
            "AllGather", ALU.bypass, replica_groups=[list(range(NCORE))],
            ins=[agin_x[1].ap().opt()], outs=[agout_x[1].ap().opt()]), 1)
        q.wait_ge(sem["dAgx20"], 4 * 16)
        q.wait_ge(sem["dAgx21"], 4 * 16)
        inc("cAgk", q.collective_compute(
            "AllGather", ALU.bypass, replica_groups=[list(range(NCORE))],
            ins=[agin_x2.ap().opt()], outs=[xnTok.ap()[0:NTOK, :].opt()]), 1)
    block.gpsimd(_)

    def xn_load(q, tb):
        p = tb % 2
        r, half = tb // 2, tb % 2
        src = agout_x[half].ap()[r * D:(r + 1) * D, :]
        inc(f"dXn{p}", q.dma_start(out=xnv[p], in_=src.rearrange("(t p) q -> p t q", p=128)))

    def _(q):
        q.wait_ge(sem["cAgx"], 1)
        xn_load(q, 0)
        M[("B1x", 0)] = C["dXn0"]
        q.wait_ge(sem["cAgxB"], 1)
        xn_load(q, 1)
        M[("B1x", 1)] = C["dXn1"]
    block.sync(_)

    for blk in range(NB):
        rp = blk % 2
        r, half = blk // 2, blk % 2

        def _(q, blk=blk, rp=rp, r=r, half=half):
            if blk == 0:
                q.wait_ge(sem["cAgt"], 1)
            if blk >= 2:
                q.wait_ge(sem["sP"], M[("Rpe", blk - 2)])
            ins = q.dma_start(out=twtile2[rp],
                              in_=agout_t.ap()[r * 8:(r + 1) * 8, half * BLK:(half + 1) * BLK])
            M[("Rld", blk)] = inc(f"dR{rp}", ins)
        block.sync(_)

        def _(t, blk=blk, rp=rp):
            t.wait_ge(sem[f"dR{rp}"], M[("Rld", blk)])
            if blk == 0:
                t.wait_ge(sem["sV"], M[("Axtke", 1, 3, DT - 1)])
            if blk >= 1:
                t.wait_ge(sem["sV"], M[("Rcp", blk - 1)])
            M[("Rpe", blk)] = inc("sP", t.matmul(
                pbO[1].ap()[:, :], selb_sb, twtile2[rp], start=True, stop=True))
        block.tensor(_)

        def _(v, blk=blk, rp=rp):
            if blk == 0:
                v.memset(ones512r, 1.0)
                v.memset(zeros512r, 0.0)
                v.memset(baserow, 0.0)
                v.memset(zpad, 0.0)
                v.wait_ge(sem["dPre"], PRE_ALL)
            v.wait_ge(sem["sP"], M[("Rpe", blk)])
            if blk >= 2:
                v.wait_ge(sem[f"dRo{rp}"], M[("Rod", blk - 2)])
            M[("Rcp", blk)] = inc("sV", v.tensor_copy(twseg[rp], pbO[1].ap()[0:1, :]))
            w = M[("Rcp", blk)]
            v.wait_ge(sem["sV"], w)
            w = inc("sV", v.tensor_tensor(maskseg, twseg[rp], zeros512r, ALU.is_gt))
            v.wait_ge(sem["sV"], w)
            w = inc("sV", v.tensor_tensor_scan(posseg, ones512r, maskseg, 0.0,
                                               ALU.mult, ALU.add))
            v.wait_ge(sem["sV"], w)
            w = inc("sV", v.tensor_scalar(posseg, posseg, baserow, None, ALU.add))
            v.wait_ge(sem["sV"], w)
            w = inc("sV", v.tensor_copy(baserow, posseg[0:1, BLK - 1:BLK]))
            v.wait_ge(sem["sV"], w)
            w = inc("sV", v.tensor_sub(t1seg, posseg, ones512r))
            v.wait_ge(sem["sV"], w)
            w = inc("sV", v.tensor_mul(t1seg, t1seg, maskseg))
            v.wait_ge(sem["sV"], w)
            w = inc("sV", v.tensor_sub(u1seg, ones512r, maskseg))
            v.wait_ge(sem["sV"], w)
            w = inc("sV", v.tensor_mul(u1seg, u1seg, dmp_sb))
            v.wait_ge(sem["sV"], w)
            w = inc("sV", v.tensor_add(tgtfseg, u1seg, t1seg))
            v.wait_ge(sem["sV"], w)
            if blk >= 2:
                v.wait_ge(sem[f"dRt{rp}"], M[("Rtd", blk - 2)])
            M[("Rch", blk)] = inc("sV", v.tensor_copy(tgtseg[rp], tgtfseg))
        block.vector(_)

        def _(q, blk=blk, rp=rp):
            q.wait_ge(sem["sV"], M[("Rcp", blk)])
            ins = q.dma_start(out=twtok.ap()[blk * BLK:(blk + 1) * BLK, 0:1],
                              in_=twseg[rp])
            M[("Rod", blk)] = inc(f"dRo{rp}", ins)
            q.wait_ge(sem["sV"], M[("Rch", blk)])
            ins = q.dma_start(out=tgtd.ap()[blk * BLK:(blk + 1) * BLK, 0:1],
                              in_=tgtseg[rp])
            M[("Rtd", blk)] = inc(f"dRt{rp}", ins)
        block.sync(_)

    for tb in range(NB):
        p = tb % 2

        if tb >= 2:
            def _(q, tb=tb, p=p):
                q.wait_ge(sem["sP"], M[("B1pe", tb - 2)])
                xn_load(q, tb)
                M[("B1x", tb)] = C[f"dXn{p}"]
            block.sync(_)

        def _(t, tb=tb, p=p):
            t.wait_ge(sem[f"dXn{p}"], M[("B1x", tb)])
            if tb == 0:
                t.wait_ge(sem["dQkv"], 48)
            if tb >= 1:
                t.wait_ge(sem["sA"], M[("B1evA", tb - 1)])
                t.wait_ge(sem["sV"], M[("B1evV", tb - 1)])
            xv = xnv[p]
            for h in range(NHC):
                for i in range(DT):
                    ins = t.matmul(pb[0 + h][:, :], wqv[:, i, h * HD:(h + 1) * HD],
                                   xv[:, i, :], start=(i == 0), stop=(i == DT - 1))
                M[("B1q", tb, h)] = inc("sP", ins)
                for i in range(DT):
                    ins = t.matmul(pb[2 + h][:, :], wkv[:, i, h * HD:(h + 1) * HD],
                                   xv[:, i, :], start=(i == 0), stop=(i == DT - 1))
                M[("B1k", tb, h)] = inc("sP", ins)
            for s4 in range(4):
                bank, half = 4 + s4 // 2, s4 % 2
                for i in range(DT):
                    ins = t.matmul(pb[bank][:, half * 256:(half + 1) * 256],
                                   xv[:, i, s4 * 128:(s4 + 1) * 128],
                                   wvv[:, i, :], start=(i == 0), stop=(i == DT - 1))
                if s4 % 2 == 1:
                    M[("B1v", tb, s4 // 2)] = inc("sP", ins)
            M[("B1pe", tb)] = C["sP"]
        block.tensor(_)

        def _(sc, tb=tb):
            if tb == 0:
                sc.wait_ge(sem["dPre"], PRE_ALL)
            for h in range(NHC):
                sc.wait_ge(sem["sP"], M[("B1q", tb, h)])
                sc.activation(QT_sb[:, h * NTOK + tb * BLK:h * NTOK + (tb + 1) * BLK],
                              pb[0 + h][:, :], AF.Identity, bias=bq_sb[:, h:h + 1],
                              scale=1.0)
                sc.wait_ge(sem["sP"], M[("B1k", tb, h)])
                ins = sc.activation(KT_sb[:, h * NTOK + tb * BLK:h * NTOK + (tb + 1) * BLK],
                                    pb[2 + h][:, :], AF.Identity, bias=bk_sb[:, h:h + 1],
                                    scale=1.0)
            M[("B1evA", tb)] = inc("sA", ins)
        block.scalar(_)

        def _(v, tb=tb):
            if tb == 0:
                v.wait_ge(sem["sV"], M["bv"])
                v.wait_ge(sem["dAgt"], 32)
                v.wait_ge(sem["dAgx20"], 64)
                v.wait_ge(sem["dAgx21"], 64)
            for s4 in range(4):
                bank, half = 4 + s4 // 2, s4 % 2
                g = tb * 4 + s4
                v.wait_ge(sem["sP"], M[("B1v", tb, s4 // 2)])
                ins = v.tensor_add(V_sb[:, g * 256:(g + 1) * 256],
                                   pb[bank][:, half * 256:(half + 1) * 256], bv_bc)
            M[("B1evV", tb)] = inc("sV", ins)
        block.vector(_)

    def _(q):
        q.wait_ge(sem["sP"], M[("B1pe", 14)])
        for m in range(4):
            inc("dMask", q.dma_start(out=mask_sb[m], in_=masks[m]))
    block.sync(_)

    def _(g):
        from concourse.bass import IndirectOffsetOnAxis
        g.wait_ge(sem["sV"], M[("Rcp", 0)])
        inc("dRb1", g.dma_start(out=xnTok.ap()[NTOK:NTOK + 128, :], in_=zpad))
        inc("dRb1", g.dma_start(out=twtok.ap()[NTOK:NTOK + 128, 0:1], in_=zpad_f32))
        g.wait_ge(sem["sP"], M[("B1pe", NB - 1)])
        inc("dRb0", g.dma_start(out=seld.ap()[0:NSEL, :],
                                in_=pad8192.rearrange("p c -> (p c)").unsqueeze(1)))
        g.wait_ge(sem["dRt0"], 8 * 16)
        g.wait_ge(sem["dRt1"], 8 * 16)
        inc("dRb0", g.dma_start(out=tgt_sb,
                                in_=tgtd.ap().rearrange("(p c) o -> p (c o)", p=128)))
        g.wait_ge(sem["dPre"], PRE_ALL)
        g.wait_ge(sem["dRb0"], 2 * 16)
        for c in range(64):
            ins = g.indirect_dma_start(
                seld.ap(),
                IndirectOffsetOnAxis(tgt_sb[:, c:c + 1], 0),
                tok_sb[:, c:c + 1], None)
            inc("dRg", ins)
    block.gpsimd(_)
    M["RgScat"] = C["dRg"]

    baseP, baseA, baseV = C["sP"], C["sA"], C["sV"]
    sMp, pvMp, expMp = {}, {}, {}
    bcM, maskM = {}, {}
    rrowM, recM, recbcM, otvM = {}, {}, {}, {}
    p_, a_, v_ = baseP, baseA, baseV
    gq = 0
    gp = 0
    for u in range(4):
        for qb in range(QBN):
            nk = 4 * qb + 4
            np_ = nk // 2
            order = [("s", 0)] + ([("s", 1)] if np_ > 1 else [])
            for pk in range(np_):
                if pk + 2 < np_:
                    order.append(("s", pk + 2))
                order.append(("pv", pk))
            for kind, pk in order:
                p_ += 1
                if kind == "s":
                    sMp[gp + pk] = p_
                else:
                    pvMp[gp + pk] = p_
            for pk in range(np_):
                expMp[gp + pk] = a_ + 1
                a_ += 1
            for kt in range(4 * qb, nk):
                maskM[(gq, kt)] = v_ + 1
                v_ += 1
            bcM[gq] = p_ + 1
            p_ += 1
            rrowM[gq] = v_ + 1
            recM[gq] = v_ + 2
            recbcM[gq] = v_ + 3
            otvM[gq] = v_ + 4
            v_ += 4
            gp += np_
            gq += 1

    gq = 0
    gp = 0
    for u in range(4):
        b, hh = u // 2, u % 2
        qoff = hh * NTOK + b * S
        for qb in range(QBN):
            nk = 4 * qb + 4
            np_ = nk // 2
            g0, gq0 = gp, gq

            def _(sc, u=u, qb=qb, nk=nk, np_=np_, g0=g0):
                for pk in range(np_):
                    gg = g0 + pk
                    sc.wait_ge(sem["sP"], sMp[gg])
                    if gg >= 2:
                        sc.wait_ge(sem["sP"], pvMp[gg - 2])
                    ins = sc.activation(pt_pair[gg % 2], psS[gg % 2].ap()[:, :],
                                        AF.Exp, bias=neg8, scale=SCALE)
                    assert inc("sA", ins) == expMp[gg]
            block.scalar(_)

            def _(v, u=u, qb=qb, nk=nk, np_=np_, g0=g0, gq0=gq0):
                if u == 0 and qb == 0:
                    v.wait_ge(sem["dMask"], 64)
                for kt in range(4 * qb, nk):
                    pk = kt // 2
                    gg = g0 + pk
                    v.wait_ge(sem["sA"], expMp[gg])
                    half = pt_pair[gg % 2][:, (kt % 2) * BLK:(kt % 2 + 1) * BLK]
                    ins = v.tensor_mul(half, half, mask_sb[kt - 4 * qb])
                    assert inc("sV", ins) == maskM[(gq0, kt)]
                v.wait_ge(sem["sP"], pvMp[g0 + np_ - 1])
                if gq0 >= 2:
                    v.wait_ge(sem["sP"], bcM[gq0 - 2])
                assert inc("sV", v.tensor_copy(rrow[gq0 % 2], pb[7][0:1, :])) == rrowM[gq0]
                v.wait_ge(sem["sV"], rrowM[gq0])
                assert inc("sV", v.reciprocal(rrow[gq0 % 2], rrow[gq0 % 2])) == recM[gq0]
                v.wait_ge(sem["sP"], bcM[gq0])
                assert inc("sV", v.tensor_copy(rec_bc, pb[4][:, :])) == recbcM[gq0]
                v.wait_ge(sem["sV"], recbcM[gq0])
                if gq0 == 0:
                    v.wait_ge(sem["dRt0"], 8 * 16)
                    v.wait_ge(sem["dRt1"], 8 * 16)
                    v.wait_ge(sem["dRb1"], 2 * 16)
                ocol = (u % 2) * NTOK + (u // 2) * S + qb * BLK
                ins = v.tensor_mul(ot_bf[:, ocol:ocol + BLK],
                                   pbO[gq0 % 2].ap()[:, :], rec_bc)
                assert inc("sV", ins) == otvM[gq0]
            block.vector(_)

            def _(t, u=u, qb=qb, nk=nk, np_=np_, g0=g0, gq0=gq0, hh=hh, b=b, qoff=qoff):
                if u == 0 and qb == 0:
                    t.wait_ge(sem["sA"], M[("B1evA", NB - 1)])
                    t.wait_ge(sem["sV"], M[("B1evV", NB - 1)])
                if gq0 >= 2:
                    t.wait_ge(sem["sV"], otvM[gq0 - 2])
                if gq0 >= 1:
                    t.wait_ge(sem["sV"], rrowM[gq0 - 1])

                def s_pair(pk):
                    gg = g0 + pk
                    if gg >= 2:
                        t.wait_ge(sem["sA"], expMp[gg - 2])
                    for half in range(2):
                        kt = 2 * pk + half
                        ins = t.matmul(psS[gg % 2].ap()[:, half * BLK:(half + 1) * BLK],
                                       KT_sb[:, qoff + kt * HD:qoff + (kt + 1) * HD],
                                       QT_sb[:, qoff + qb * BLK:qoff + (qb + 1) * BLK],
                                       start=True, stop=True)
                    assert inc("sP", ins) == sMp[gg]

                def pv_block(pk):
                    gg = g0 + pk
                    t.wait_ge(sem["sA"], expMp[gg])
                    for half in range(2):
                        kt = 2 * pk + half
                        if kt >= 4 * qb:
                            t.wait_ge(sem["sV"], maskM[(gq0, kt)])
                    for half in range(2):
                        kt = 2 * pk + half
                        pth = pt_pair[gg % 2][:, half * BLK:(half + 1) * BLK]
                        t.matmul(pbO[gq0 % 2].ap()[:, :],
                                 V_sb[:, (b * 32 + kt) * 256 + hh * HD:(b * 32 + kt) * 256 + (hh + 1) * HD],
                                 pth, start=(kt == 0), stop=(kt == nk - 1))
                        ins = t.matmul(pb[7][0:1, :], ones_col_bf, pth,
                                       start=(kt == 0), stop=(kt == nk - 1))
                    assert inc("sP", ins) == pvMp[gg]

                s_pair(0)
                if np_ > 1:
                    s_pair(1)
                for pk in range(np_):
                    if pk + 2 < np_:
                        s_pair(pk + 2)
                    pv_block(pk)
                t.wait_ge(sem["sV"], recM[gq0])
                if gq0 >= 1:
                    t.wait_ge(sem["sV"], recbcM[gq0 - 1])
                ins = t.matmul(pb[4][:, :], ones_row, rrow[gq0 % 2], start=True, stop=True)
                assert inc("sP", ins) == bcM[gq0]
            block.tensor(_)

            gp += np_
            gq += 1

    M["B2peEnd"] = C["sP"]
    M["B2vEnd"] = C["sV"]
    M["B2aEnd"] = C["sA"]

    def _(q):
        q.wait_ge(sem["sP"], M["B2peEnd"])
        inc("dUp", q.dma_start(out=uv, in_=uTd.rearrange("(t p) f -> p t f", p=128)))
        inc("dDown", q.dma_start(out=dvw, in_=dTd.rearrange("(t p) d -> p t d", p=128)))
        inc("dWo", q.dma_start(out=wov, in_=woT.rearrange("(h p) d -> p h d", p=128)))
        inc("dWo", q.dma_start(out=borow, in_=bor))
        q.wait_ge(sem["dRg"], M["RgScat"])
        inc("dRg", q.dma_start(out=sel_sb,
                               in_=seld.ap()[0:NSEL, :].rearrange("(p c) o -> p (c o)", p=128)))
        M["Rg"] = C["dRg"]
    block.gpsimd(_)

    for blk in range(4):
        def _(t, blk=blk):
            if blk == 0:
                t.wait_ge(sem["sV"], recbcM[31])
                t.wait_ge(sem["dWo"], 32)
            if blk > 0:
                t.wait_ge(sem["sV"], M[("bobc2", blk - 1)])
            M[("bobc2pe", blk)] = inc("sP", t.matmul(
                pb[4][:, :], ones_row, borow[0:1, blk * BLK:(blk + 1) * BLK],
                start=True, stop=True))
        block.tensor(_)

        def _(v, blk=blk):
            v.wait_ge(sem["sP"], M[("bobc2pe", blk)])
            M[("bobc2", blk)] = inc("sV", v.tensor_scalar_mul(
                bo_bc[:, blk * BLK:(blk + 1) * BLK], pb[4][:, :], 1.0 / NCORE))
        block.vector(_)

    _stg_base = {i: C[f"dStg{i}"] for i in range(4)}
    for k in range(256):
        _stg_base[k % 4] += 16
        M[("B25st", k)] = _stg_base[k % 4]

    for tt in range(64):
        b = tt // 32
        qb25 = (tt % 32) // 4
        po = tt % 2

        def _(t, tt=tt, b=b, qb25=qb25):
            t.wait_ge(sem["sV"], otvM[(2 * b + 1) * QBN + qb25])
            if tt == 0:
                t.wait_ge(sem["sA"], M["B2aEnd"])
            for blk in range(4):
                if tt >= 1:
                    t.wait_ge(sem["sV"], M[("B25ev", tt - 1, blk)])
                for hh in range(NHC):
                    ins = t.matmul(pb[blk][:, :],
                                   ot_bf[:, hh * NTOK + tt * 128:hh * NTOK + (tt + 1) * 128],
                                   wov[:, hh, blk * BLK:(blk + 1) * BLK],
                                   start=(hh == 0), stop=(hh == NHC - 1))
            M[("B25pe", tt)] = inc("sP", ins)
        block.tensor(_)

        def _(v, tt=tt):
            v.wait_ge(sem["sP"], M[("B25pe", tt)])
            if tt == 0:
                v.wait_ge(sem["sV"], M[("bobc2", 3)])
            for blk in range(4):
                sp = (4 * tt + blk) % 4
                if 4 * tt + blk >= 4:
                    v.wait_ge(sem[f"dStg{sp}"], M[("B25st", 4 * tt + blk - 4)])
                ins = v.tensor_add(stg25[sp], pb[blk][:, :],
                                   bo_bc[:, blk * BLK:(blk + 1) * BLK])
                M[("B25ev", tt, blk)] = inc("sV", ins)
        block.vector(_)

        def _(sc, tt=tt):
            for blk in range(4):
                sp = (4 * tt + blk) % 4
                sc.wait_ge(sem["sV"], M[("B25ev", tt, blk)])
                ins = sc.dma_start(
                    out=partT.ap()[tt * 128:(tt + 1) * 128, blk * BLK:(blk + 1) * BLK],
                    in_=stg25[sp])
                assert inc(f"dStg{sp}", ins) == M[("B25st", 4 * tt + blk)]
        block.scalar(_)

    def _(q):
        q.wait_ge(sem["sP"], M[("B25pe", 63)])
        q.wait_ge(sem["sV"], M["B2vEnd"])
        q.wait_ge(sem["dR0"], 8 * 16)
        q.wait_ge(sem["dR1"], 8 * 16)
        q.wait_ge(sem["dRo0"], 8 * 16)
        q.wait_ge(sem["dRo1"], 8 * 16)
        q.wait_ge(sem["dRb0"], 2 * 16)
        q.wait_ge(sem["dRb1"], 2 * 16)
        inc("dGate", q.dma_start(out=gv, in_=gTd.rearrange("(t p) f -> p t f", p=128)))
    block.gpsimd(_)

    baseP3, baseA3, baseV3 = C["sP"], C["sA"], C["sV"]
    ga_use = {(a, b): C[f"dGa{a}{b}"] for a in range(2) for b in range(2)}
    sc_use = C["dSc"]
    p_, a_, v_ = baseP3, baseA3, baseV3
    for sb in range(NSB):
        for q4 in range(4):
            for i in range(DT):
                M[("Str", sb, q4, i)] = p_ + 1
                p_ += 1
        for ft in range(FT):
            M[("Sgu", sb, ft)] = p_ + 1
            p_ += 1
        for gd in range(64):
            M[("Sdn", sb, gd // DT, gd % DT)] = p_ + 1
            p_ += 1
            if gd > 0:
                M[("Setr", sb, (gd - 1) // DT, (gd - 1) % DT)] = p_ + 1
                p_ += 1
        M[("Setr", sb, 3, DT - 1)] = p_ + 1
        p_ += 1
        for ft in range(FT):
            M[("Ssg", sb, ft)] = a_ + 1
            a_ += 1
        for gd in range(64):
            M[("Ssc", sb, gd // DT, gd % DT)] = a_ + 1
            a_ += 1
        for q4 in range(4):
            for i in range(DT):
                M[("Sxc", sb, q4, i)] = v_ + 1
                v_ += 1
        for ft in range(FT):
            M[("Sh", sb, ft)] = v_ + 1
            v_ += 1
        for gd in range(64):
            M[("Sev", sb, gd // DT, gd % DT)] = v_ + 1
            v_ += 1
        for q4 in range(4):
            ga_use[(sb % 2, q4 % 2)] += 32
            M[("Sga", sb, q4)] = ga_use[(sb % 2, q4 % 2)]
        for q4 in range(4):
            sc_use += 16
            M[("Sscat", sb, q4)] = sc_use

    pbT2 = [psS[0].ap()[:, 0:64].bitcast(BF16), psS[0].ap()[:, BLK:BLK + 64].bitcast(BF16)]
    pbT3 = [pbBc.ap()[:, 0:64].bitcast(BF16), pbR.ap()[:, 0:64].bitcast(BF16)]

    def selv(gg):
        return sel_sb.rearrange("p (c o) -> p c o", o=1)[:, gg, :]

    for sb in range(NSB):
        sp2 = sb % 2

        def _(g, sb=sb, sp2=sp2):
            from concourse.bass import IndirectOffsetOnAxis
            if sb == 0:
                g.wait_ge(sem["dRg"], M["Rg"])
                g.wait_ge(sem["cAgk"], 1)
                g.wait_ge(sem["dRo0"], C["dRo0"])
                g.wait_ge(sem["dRo1"], C["dRo1"])
            for q4 in range(4):
                if sb > 0:
                    g.wait_ge(sem["sA"], M[("Ssc", sb - 1, q4, DT - 1)])
                    assert inc("dSc", g.indirect_dma_start(
                        partT.ap(), IndirectOffsetOnAxis(selv((sb - 1) * 4 + q4), 0),
                        eo_stage, None, compute_op=ALU.add)) == M[("Sscat", sb - 1, q4)]
                gg = sb * 4 + q4
                if gg >= 2:
                    pq = divmod(gg - 2, 4)
                    g.wait_ge(sem["sP"], M[("Str", pq[0], pq[1], DT - 1)])
                if sb >= 2 and q4 == 0:
                    g.wait_ge(sem["sA"], M[("Ssc", sb - 2, 3, DT - 1)])
                inc(f"dGa{sp2}{q4 % 2}", g.indirect_dma_start(
                    xg[q4 % 2], None, xnTok.ap(), IndirectOffsetOnAxis(selv(gg), 0)))
                inc(f"dGa{sp2}{q4 % 2}", g.indirect_dma_start(
                    twg_sb[:, sp2 * 4 + q4:sp2 * 4 + q4 + 1], None,
                    twtok.ap(), IndirectOffsetOnAxis(selv(gg), 0)))
                assert C[f"dGa{sp2}{q4 % 2}"] == M[("Sga", sb, q4)]
        block.gpsimd(_)

        def _(t, sb=sb, sp2=sp2):
            if sb == 0:
                t.wait_ge(sem["sV"], M[("B25ev", 63, 3)])
            for q4 in range(4):
                t.wait_ge(sem[f"dGa{sp2}{q4 % 2}"], M[("Sga", sb, q4)])
                for i in range(DT):
                    if sb > 0 or q4 > 0 or i >= 2:
                        pi = (sb, q4, i - 2) if i >= 2 else \
                             ((sb, q4 - 1, DT - 2 + i) if q4 > 0 else (sb - 1, 3, DT - 2 + i))
                        t.wait_ge(sem["sV"], M[("Sxc",) + pi])
                    assert inc("sP", t.transpose(
                        pbT2[i % 2],
                        xg[q4 % 2].rearrange("p (t c) -> p t c", t=DT)[:, i, :],
                        idbf_sb)) == M[("Str", sb, q4, i)]
            xv = xnv[sb % 2]
            for ft in range(FT):
                gb, ub = 0 + ft % 2, 2 + ft % 2
                if ft == 0:
                    t.wait_ge(sem["sV"], M[("Sxc", sb, 3, DT - 1)])
                if sb > 0 or ft >= 2:
                    pft = (sb, ft - 2) if ft >= 2 else (sb - 1, FT - 2 + ft)
                    t.wait_ge(sem["sA"], M[("Ssg",) + pft])
                    t.wait_ge(sem["sV"], M[("Sh",) + pft])
                if sb == 0 and ft == 0:
                    t.wait_ge(sem["dGate"], 16)
                    t.wait_ge(sem["dUp"], 16)
                for i in range(DT):
                    t.matmul(pb[gb][:, :], gv[:, i, ft * HD:(ft + 1) * HD],
                             xv[:, i, :], start=(i == 0), stop=(i == DT - 1))
                for i in range(DT):
                    ins = t.matmul(pb[ub][:, :], uv[:, i, ft * HD:(ft + 1) * HD],
                                   xv[:, i, :], start=(i == 0), stop=(i == DT - 1))
                assert inc("sP", ins) == M[("Sgu", sb, ft)]
            for gd in range(64):
                q4, d = gd // DT, gd % DT
                bank = 5 + gd % 2
                if gd == 0:
                    t.wait_ge(sem["sV"], M[("Sh", sb, FT - 1)])
                    if sb == 0:
                        t.wait_ge(sem["dDown"], 16)
                if sb > 0 or gd >= 2:
                    pgd = gd - 2 if gd >= 2 else 64 - 2 + gd
                    psb = sb if gd >= 2 else sb - 1
                    t.wait_ge(sem["sV"], M[("Sev", psb, pgd // DT, pgd % DT)])
                for ft in range(FT):
                    ins = t.matmul(
                        pb[bank][:, 0:128],
                        dvw[:, ft, d * HD:(d + 1) * HD],
                        hp_t[ft][:, q4 * 128:(q4 + 1) * 128],
                        start=(ft == 0), stop=(ft == FT - 1))
                assert inc("sP", ins) == M[("Sdn", sb, q4, d)]
                etr = gd - 1
                if etr >= 0:
                    eq4, ed = etr // DT, etr % DT
                    t.wait_ge(sem["sV"], M[("Sev", sb, eq4, ed)])
                    if sb > 0 or etr >= 2:
                        pgd = etr - 2 if etr >= 2 else 64 - 2 + etr
                        psb = sb if etr >= 2 else sb - 1
                        t.wait_ge(sem["sA"], M[("Ssc", psb, pgd // DT, pgd % DT)])
                    assert inc("sP", t.transpose(
                        pbT3[etr % 2], eo_bf[etr % 2], idbf_sb)) == M[("Setr", sb, eq4, ed)]
            t.wait_ge(sem["sV"], M[("Sev", sb, 3, DT - 1)])
            t.wait_ge(sem["sA"], M[("Ssc", sb, 3, DT - 3)])
            assert inc("sP", t.transpose(
                pbT3[63 % 2], eo_bf[63 % 2], idbf_sb)) == M[("Setr", sb, 3, DT - 1)]
        block.tensor(_)

        def _(sc, sb=sb, sp2=sp2):
            for ft in range(FT):
                if sb == 0 and ft == 0:
                    for k in range(4):
                        sc.wait_ge(sem[f"dStg{k}"], C[f"dStg{k}"])
                sc.wait_ge(sem["sP"], M[("Sgu", sb, ft)])
                assert inc("sA", sc.activation(
                    sg_t[ft % 2], pb[0 + ft % 2][:, :], AF.Silu)) == M[("Ssg", sb, ft)]
            for gd in range(64):
                q4, d = gd // DT, gd % DT
                sc.wait_ge(sem["sP"], M[("Setr", sb, q4, d)])
                if gd == 0:
                    if sb == 0:
                        sc.wait_ge(sem["sP"], M[("B25pe", 63)])
                    if sb > 0:
                        sc.wait_ge(sem["dSc"], M[("Sscat", sb - 1, 3)])
                elif d == 0:
                    sc.wait_ge(sem["dSc"], M[("Sscat", sb, q4 - 1)])
                assert inc("sA", sc.activation(
                    eo_stage[:, d * 128:(d + 1) * 128], pbT3[d % 2],
                    AF.Identity,
                    scale=twg_sb[:, sp2 * 4 + q4:sp2 * 4 + q4 + 1])) == M[("Ssc", sb, q4, d)]
        block.scalar(_)

        def _(v, sb=sb, sp2=sp2):
            for q4 in range(4):
                for i in range(DT):
                    v.wait_ge(sem["sP"], M[("Str", sb, q4, i)])
                    if sb >= 2 and q4 == 0 and i == 0:
                        v.wait_ge(sem["sP"], M[("Sgu", sb - 2, FT - 1)])
                    assert inc("sV", v.tensor_copy(
                        xnv[sb % 2][:, i, q4 * 128:(q4 + 1) * 128],
                        pbT2[i % 2])) == M[("Sxc", sb, q4, i)]
            for ft in range(FT):
                v.wait_ge(sem["sA"], M[("Ssg", sb, ft)])
                if sb == 0 and ft == 0:
                    for k in range(4):
                        v.wait_ge(sem[f"dStg{k}"], C[f"dStg{k}"])
                if sb >= 1 and ft == 0:
                    v.wait_ge(sem["sP"], M[("Sdn", sb - 1, 3, DT - 1)])
                assert inc("sV", v.tensor_mul(
                    hp_t[ft], pb[2 + ft % 2][:, :], sg_t[ft % 2])) == M[("Sh", sb, ft)]
            for gd in range(64):
                q4, d = gd // DT, gd % DT
                bank = 5 + gd % 2
                v.wait_ge(sem["sP"], M[("Sdn", sb, q4, d)])
                if sb > 0 or gd >= 2:
                    pgd = gd - 2 if gd >= 2 else 64 - 2 + gd
                    psb = sb if gd >= 2 else sb - 1
                    v.wait_ge(sem["sP"], M[("Setr", psb, pgd // DT, pgd % DT)])
                assert inc("sV", v.tensor_copy(
                    eo_bf[gd % 2],
                    pb[bank][:, 0:128])) == M[("Sev", sb, q4, d)]
        block.vector(_)

    def _(g):
        from concourse.bass import IndirectOffsetOnAxis
        for q4 in range(4):
            g.wait_ge(sem["sA"], M[("Ssc", NSB - 1, q4, DT - 1)])
            assert inc("dSc", g.indirect_dma_start(
                partT.ap(), IndirectOffsetOnAxis(selv((NSB - 1) * 4 + q4), 0),
                eo_stage, None, compute_op=ALU.add)) == M[("Sscat", NSB - 1, q4)]
    block.gpsimd(_)

    def _(q):
        for k in range(4):
            q.wait_ge(sem[f"dStg{k}"], C[f"dStg{k}"])
        q.wait_ge(sem["dSc"], C["dSc"])
        inc("cRs", q.collective_compute(
            "ReduceScatter", ALU.add, replica_groups=[list(range(NCORE))],
            ins=[partT.ap()[0:NTOK, :].opt()], outs=[rsoutT.ap().opt()]), 1)
    block.gpsimd(_)

    for it in range(8):
        p = it % 2

        def _(q, it=it, p=p):
            if it == 0:
                q.wait_ge(sem["cRs"], 1)
            if it >= 2:
                q.wait_ge(sem["sV"], M[("Ca", it - 2)])
                q.wait_ge(sem[f"dCo{p}"], M[("Co", it - 2)])
            inc(f"dCl{p}", q.dma_start(
                out=ct_rs[p], in_=rsoutT.ap()[it * 128:(it + 1) * 128, :]))
            ins = q.dma_start(
                out=ct_x[p], in_=xTok[it * 128:(it + 1) * 128, :])
            M[("Cl", it)] = inc(f"dCl{p}", ins)
        block.sync(_)

        def _(v, it=it, p=p):
            v.wait_ge(sem[f"dCl{p}"], M[("Cl", it)])
            if it >= 2:
                v.wait_ge(sem[f"dCo{p}"], M[("Co", it - 2)])
            M[("Ca", it)] = inc("sV", v.tensor_add(ct_o[p], ct_rs[p], ct_x[p]))
        block.vector(_)

        def _(sc, it=it, p=p):
            sc.wait_ge(sem["sV"], M[("Ca", it)])
            ins = sc.dma_start(out=outTok[it * 128:(it + 1) * 128, :], in_=ct_o[p])
            M[("Co", it)] = inc(f"dCo{p}", ins)
        block.scalar(_)

    if debug:
        def _(q):
            for nm, (dst, src) in dbg.items():
                inc("dDbg", q.dma_start(out=dst, in_=src.ap()))
        block.sync(_)

    def _(q):
        for k in SEM_NAMES:
            if (k.startswith("d") or k.startswith("c")) and C[k]:
                q.wait_ge(sem[k], C[k])
    block.sync(_)

    block.__exit__(None, None, None)
    return nc


def prep_inputs(L0_attn_h, L0_attn_Wq, L0_attn_bq, L0_attn_Wk, L0_attn_bk,
                L0_attn_Wv, L0_attn_bv, L0_attn_Wo, L0_attn_bo,
                L0_ffn_gate_w, L0_ffn_gate_proj, L0_ffn_up_proj, L0_ffn_down_proj):
    x = np.ascontiguousarray(np.asarray(L0_attn_h, dtype=np.float32).reshape(NTOK, D).T)
    wqT = np.ascontiguousarray(np.asarray(L0_attn_Wq).T)
    wkT = np.ascontiguousarray(np.asarray(L0_attn_Wk).T)
    wvT = np.ascontiguousarray(np.asarray(L0_attn_Wv).T)
    woT = np.ascontiguousarray(np.asarray(L0_attn_Wo).T)
    gwT = np.ascontiguousarray(np.asarray(L0_ffn_gate_w).T)
    bq = np.asarray(L0_attn_bq); bk = np.asarray(L0_attn_bk)
    bv = np.asarray(L0_attn_bv); bo = np.asarray(L0_attn_bo)
    gp = np.asarray(L0_ffn_gate_proj); up = np.asarray(L0_ffn_up_proj)
    dp = np.asarray(L0_ffn_down_proj)

    tokid_arr = np.ascontiguousarray(
        (np.arange(128)[:, None] * 64 + np.arange(64)[None, :]).astype(np.int32))
    dmp_arr = (NSEL + (np.arange(BLK) % 64)).astype(np.float32).reshape(1, BLK)
    padarr = np.full((128, NSEL // 128), NTOK, np.int32)
    idbf_arr = np.eye(128, dtype=BFNP)
    mk = np.zeros((4, 128, BLK), dtype=BFNP)
    for m in range(4):
        i = np.arange(128)[:, None]
        jj = np.arange(BLK)[None, :]
        mk[m] = (128 * m + i <= jj).astype(BFNP)
    idf = np.eye(128, dtype=np.float32)

    in_maps = []
    for c in range(NCORE):
        hsl = slice(2 * c * HD, (2 * c + 2) * HD)
        sel = np.zeros((8, 128), np.float32)
        sel[c, :] = 1.0
        in_maps.append({
            "xT": np.ascontiguousarray(x[:, c * TPC:(c + 1) * TPC]),
            "wqT": np.ascontiguousarray(wqT[:, hsl]),
            "wkT": np.ascontiguousarray(wkT[:, hsl]),
            "wvT": np.ascontiguousarray(wvT[:, hsl]),
            "woT": np.ascontiguousarray(woT[hsl, :]),
            "bqh": np.ascontiguousarray(bq[hsl].reshape(NHC, HD).T),
            "bkh": np.ascontiguousarray(bk[hsl].reshape(NHC, HD).T),
            "bvr": np.ascontiguousarray(bv[hsl].reshape(1, NHC * HD)),
            "bor": np.ascontiguousarray(bo.reshape(1, D)),
            "xTok": np.ascontiguousarray(
                np.asarray(L0_attn_h, dtype=np.float32).reshape(NTOK, D)[c * TPC:(c + 1) * TPC]),
            "tokids": tokid_arr,
            "dmp512": dmp_arr,
            "pad8192": padarr,
            "idbf": idbf_arr,
            "gwT": gwT,
            "gTd": np.ascontiguousarray(gp[c].T),
            "uTd": np.ascontiguousarray(up[c].T),
            "dTd": np.ascontiguousarray(dp[c].T),
            "selb": sel,
            "masks": mk,
            "idf32": idf,
        })
    return in_maps


_CACHE = {}


def kernel(**inputs):
    if "nc" not in _CACHE:
        _CACHE["nc"] = build_program(debug=False)
    nc = _CACHE["nc"]
    in_maps = prep_inputs(**inputs)
    res = run_bass_kernel_spmd(nc, in_maps, core_ids=list(range(NCORE)))
    shards = [r["outTok"] for r in res.results]
    out = np.concatenate(shards, axis=0)
    return np.ascontiguousarray(out).reshape(2, S, D).astype(np.float32)


# version 11
# speedup vs baseline: 1.0106x; 1.0106x over previous
import sys

import numpy as np

if "/opt/trn_rl_repo" not in sys.path:
    sys.path.insert(0, "/opt/trn_rl_repo")

import ml_dtypes

import concourse.bass as bass
import concourse.mybir as mybir
from concourse.bass_utils import run_bass_kernel_spmd

F32 = mybir.dt.float32
BF16 = mybir.dt.bfloat16
AF = mybir.ActivationFunctionType
ALU = mybir.AluOpType
AX = mybir.AxisListType
BFNP = ml_dtypes.bfloat16

NCORE = 8
D = 2048
NTOK = 8192
TPC = NTOK // NCORE
S = 4096
HD = 128
NHC = 2
DT = D // 128
F = 1408
FT = F // 128
BLK = 512
NB = NTOK // BLK
QBN = S // BLK
SCALE = float(1.0 / np.sqrt(128.0))
KB = 1024

SEM_NAMES = ["sP", "sV", "sA", "sPl",
             "dPre", "dX", "dAgxA", "dAgxB", "dAgx20", "dAgx21", "dAgt", "dMask",
             "dXn0", "dXn1", "dOtW0", "dOtW1", "dStg0", "dStg1", "dStg2", "dStg3",
             "dOtl0", "dOtl1", "dGa00", "dGa01", "dGa10", "dGa11", "dSc",
             "dR0", "dR1", "dRo0", "dRo1", "dRb0", "dRb1", "dRt0", "dRt1", "dRg",
             "dCl0", "dCl1", "dCo0", "dCo1", "dDbg",
             "dQkv", "dGate", "dUp", "dDown", "dWo",
             "cAgx", "cAgxB", "cAgt", "cAgk", "cRs"]
NSEL = 2560
NSB = NSEL // BLK
PADIDX = NTOK


def build_program(debug=False, upto=None):
    nc = bass.Bass("TRN2", target_bir_lowering=False, debug=False)

    def din(name, shape, dt=F32):
        return nc.dram_tensor(name, shape, dt, kind="ExternalInput").ap()

    xT = din("xT", [D, TPC])
    wqT = din("wqT", [D, NHC * HD])
    wkT = din("wkT", [D, NHC * HD])
    wvT = din("wvT", [D, NHC * HD])
    woT = din("woT", [NHC * HD, D])
    bqh = din("bqh", [HD, NHC])
    bkh = din("bkh", [HD, NHC])
    bvr = din("bvr", [1, NHC * HD])
    bor = din("bor", [1, D])
    xTok = din("xTok", [TPC, D])
    tokids = din("tokids", [128, 64], mybir.dt.int32)
    pad8192 = din("pad8192", [128, NSEL // 128], mybir.dt.int32)
    idbf = din("idbf", [128, 128], BF16)
    gwT = din("gwT", [D, 8])
    gTd = din("gTd", [D, F])
    uTd = din("uTd", [D, F])
    dTd = din("dTd", [F, D])
    selb = din("selb", [8, 128])
    dmp512 = din("dmp512", [1, BLK])
    masks = din("masks", [4, 128, BLK], BF16)
    idf32 = din("idf32", [128, 128])

    outTok = nc.dram_tensor("outTok", [TPC, D], F32, kind="ExternalOutput").ap()

    agin_x = [nc.dram_tensor(f"agin_x{h}", [D, BLK], BF16) for h in range(2)]
    agout_x = [nc.dram_tensor(f"agout_x{h}", [NCORE * D, BLK], BF16) for h in range(2)]
    agin_x2 = nc.dram_tensor("agin_x2", [TPC, D], BF16)
    xnTok = nc.dram_tensor("xnTok", [NTOK + 128, D], BF16)
    agin_t = nc.dram_tensor("agin_t", [8, TPC], F32)
    agout_t = nc.dram_tensor("agout_t", [NCORE * 8, TPC], F32)
    twtok = nc.dram_tensor("twtok", [NTOK + 128, 1], F32)
    seld = nc.dram_tensor("seld", [NSEL + 128, 1], mybir.dt.int32)
    tgtd = nc.dram_tensor("tgtd", [NTOK, 1], mybir.dt.int32)
    partT = nc.dram_tensor("partT", [NTOK + 128, D], F32)
    rsoutT = nc.dram_tensor("rsoutT", [TPC, D], F32)

    dbg = {}
    if debug:
        for nm, t in [("d_agx", agout_x[0]), ("d_agt", agout_t),
                      ("d_sel", seld), ("d_twtok", twtok),
                      ("d_part", partT), ("d_rs", rsoutT)]:
            dbg[nm] = (nc.dram_tensor(nm, list(t.ap().shape), t.ap().dtype,
                                      kind="ExternalOutput").ap(), t)

    ARENA = 207 * KB
    arena = nc.alloc_sbuf_tensor("arena", [128, ARENA], mybir.dt.uint8)

    I32 = mybir.dt.int32

    def cv(off, nelem, dt, parts=128):
        nb = {F32: 4, BF16: 2, I32: 4}[dt]
        assert off % 64 == 0 and off + nelem * nb <= ARENA, f"bad carve {off}"
        return arena.ap()[0:parts, off:off + nelem * nb].bitcast(dt)

    o = 0
    onesm = cv(o, 128, F32); o += 512
    ones_col_bf = cv(o, 1, BF16); o += 64
    ones_row = cv(o, 128, F32, parts=1); o += 512
    neg8 = cv(o, 1, F32); o += 64
    eps128 = cv(o, 1, F32); o += 64
    bq_sb = cv(o, 2, F32); o += 64
    bk_sb = cv(o, 2, F32); o += 64
    sel_sb = cv(o, NSEL // 128, I32); o += 128
    twg_sb = cv(o, 8, F32); o += 64
    bv_bc = cv(o, NHC * HD, BF16); o += 512
    bvr_sb = cv(o, NHC * HD, F32, parts=1); o += 1024
    idf32_sb = cv(o, 128, F32); o += 512
    idbf_sb = cv(o, 128, BF16); o += 256
    otl = [cv(o + i * 256, 128, BF16) for i in range(4)]; o += 1024
    gwT_sb = cv(o, DT * 8, F32); o += 512
    selb_sb = cv(o, 128, F32, parts=8); o += 512
    assert o <= 6 * KB

    xnb = [cv(6 * KB + i * 16 * KB, DT * BLK, BF16) for i in range(2)]
    pt_t = [cv(6 * KB + i * KB, BLK, BF16) for i in range(4)]
    pt_pair = [cv(6 * KB + i * 2 * KB, 2 * BLK, BF16) for i in range(2)]
    mask_sb = [cv(10 * KB + i * KB, BLK, BF16) for i in range(4)]
    otv = [cv(14 * KB + i * KB, BLK, BF16) for i in range(2)]
    rec_bc = cv(16 * KB, BLK, BF16)
    rrow = [cv(17 * KB + 2048 * i, BLK, F32, parts=1) for i in range(2)]

    QT_sb = cv(38 * KB, NHC * NTOK, BF16)
    KT_sb = cv(70 * KB, NHC * NTOK, BF16)
    V_sb = cv(102 * KB, 64 * NHC * HD, BF16)
    xblk = cv(38 * KB, DT * BLK, F32)
    xn32 = cv(70 * KB, DT * BLK, F32)
    sqt = [cv(102 * KB + i * 2 * KB, BLK, F32) for i in range(2)]
    rstd_t = cv(106 * KB, BLK, F32)
    twt = [cv(108 * KB + i * 2 * KB, BLK, F32, parts=8) for i in range(2)]
    uT_sb = cv(38 * KB, DT * F, BF16)
    dT_sb = cv(82 * KB, FT * D, BF16)
    wo_sb = cv(126 * KB, NHC * D, BF16)
    xtk = [cv(112 * KB + i * 4 * KB, D, BF16) for i in range(2)]
    ct_rs = [cv(38 * KB + i * 8 * KB, D, F32) for i in range(2)]
    ct_x = [cv(54 * KB + i * 8 * KB, D, F32) for i in range(2)]
    ct_o = [cv(70 * KB + i * 8 * KB, D, F32) for i in range(2)]

    gT_sb = cv(134 * KB, DT * F, BF16)
    twtile2 = [cv(134 * KB + i * 2 * KB, BLK, F32, parts=8) for i in range(2)]
    twseg = [cv(138 * KB + i * 2 * KB, BLK, F32, parts=1) for i in range(2)]
    tgtseg = [cv(142 * KB + i * 2 * KB, BLK, I32, parts=1) for i in range(2)]
    maskseg = cv(146 * KB, BLK, F32, parts=1)
    posseg = cv(148 * KB, BLK, F32, parts=1)
    t1seg = cv(150 * KB, BLK, F32, parts=1)
    u1seg = cv(152 * KB, BLK, F32, parts=1)
    tgtfseg = cv(154 * KB, BLK, F32, parts=1)
    ones512r = cv(156 * KB, BLK, F32, parts=1)
    zeros512r = cv(170 * KB, BLK, F32, parts=1)
    baserow = cv(158 * KB, 1, F32, parts=1)
    zpad = cv(159 * KB, D, BF16)
    zpad_f32 = cv(159 * KB, 1, F32)
    ot_bf = cv(142 * KB, 2 * NTOK, BF16)
    tok_sb = cv(174 * KB, 64, I32)
    tgt_sb = cv(174 * KB + 256, 64, I32)
    dmp_sb = cv(175 * KB, BLK, F32, parts=1)

    wq_sb = cv(178 * KB, DT * NHC * HD, BF16)
    wk_sb = cv(186 * KB, DT * NHC * HD, BF16)
    wv_sb = cv(194 * KB, DT * NHC * HD, BF16)
    borow = cv(178 * KB, D, F32, parts=1)
    bo_bc = cv(186 * KB, D, BF16)
    stg25 = [cv(190 * KB + i * 2 * KB, BLK, F32) for i in range(4)]
    xg = [cv(198 * KB + i * 4 * KB, D, BF16) for i in range(2)]
    hp_t = [cv(178 * KB + i * KB, BLK, BF16) for i in range(FT)]
    sg_t = [cv(188 * KB, BLK, BF16), cv(206 * KB, BLK, BF16)]
    eo_bf = [cv(189 * KB + i * 256, 128, BF16) for i in range(2)]
    eo_stage = cv(126 * KB, D, F32)

    psS = [nc.alloc_psum_tensor(f"psS{i}", [128, 2 * BLK], F32) for i in range(2)]
    pbO = [nc.alloc_psum_tensor(f"pbO{i}", [128, BLK], F32) for i in range(2)]
    pbBc = nc.alloc_psum_tensor("pbBc", [128, BLK], F32)
    pbR = nc.alloc_psum_tensor("pbR", [128, BLK], F32)
    pb = [psS[0].ap()[:, 0:BLK], psS[0].ap()[:, BLK:2 * BLK],
          psS[1].ap()[:, 0:BLK], psS[1].ap()[:, BLK:2 * BLK],
          pbBc.ap(), pbO[0].ap(), pbO[1].ap(), pbR.ap()]

    C = {k: 0 for k in SEM_NAMES}
    sem = {k: nc.semaphore(k).__enter__() for k in SEM_NAMES}

    def inc(name, ins, n=None):
        if n is None:
            n = 1 if name in ("sP", "sV", "sA") or name.startswith("c") else 16
        ins.then_inc(sem[name], n)
        C[name] += n
        return C[name]

    M = {}
    block = nc.Block().__enter__()

    wqv = wq_sb.rearrange("p (t h) -> p t h", t=DT)
    wkv = wk_sb.rearrange("p (t h) -> p t h", t=DT)
    wvv = wv_sb.rearrange("p (t h) -> p t h", t=DT)
    gwv = gwT_sb.rearrange("p (t e) -> p t e", t=DT)
    gv = gT_sb.rearrange("p (t f) -> p t f", t=DT)
    uv = uT_sb.rearrange("p (t f) -> p t f", t=DT)
    dvw = dT_sb.rearrange("p (t d) -> p t d", t=FT)
    wov = wo_sb.rearrange("p (h d) -> p h d", h=NHC)
    xnv = [bb.rearrange("p (t q) -> p t q", t=DT) for bb in xnb]

    def _(q):
        inc("dPre", q.dma_start(out=idf32_sb, in_=idf32))
        inc("dPre", q.dma_start(out=bq_sb, in_=bqh))
        inc("dPre", q.dma_start(out=bk_sb, in_=bkh))
        inc("dPre", q.dma_start(out=idbf_sb, in_=idbf))
        inc("dPre", q.dma_start(out=tok_sb, in_=tokids))
        inc("dPre", q.dma_start(out=bvr_sb, in_=bvr))
        inc("dPre", q.dma_start(out=gwv, in_=gwT.rearrange("(t p) e -> p t e", p=128)))
        inc("dPre", q.dma_start(out=selb_sb, in_=selb))
        inc("dPre", q.dma_start(out=dmp_sb, in_=dmp512))
    block.sync(_)
    PRE_ALL = C["dPre"]

    def _(q):
        inc("dQkv", q.dma_start(out=wqv, in_=wqT.rearrange("(t p) h -> p t h", p=128)))
        inc("dQkv", q.dma_start(out=wkv, in_=wkT.rearrange("(t p) h -> p t h", p=128)))
        inc("dQkv", q.dma_start(out=wvv, in_=wvT.rearrange("(t p) h -> p t h", p=128)))
    block.gpsimd(_)

    def _(v):
        v.memset(onesm, 1.0)
        v.memset(ones_col_bf, 1.0)
        v.memset(ones_row, 1.0)
        v.memset(neg8, -8.0)
        M["consts"] = inc("sV", v.memset(eps128, 1e-5))
    block.vector(_)

    def _(t):
        t.wait_ge(sem["dPre"], PRE_ALL)
        t.wait_ge(sem["sV"], M["consts"])
        M["bv_pe"] = inc("sP", t.matmul(pb[7][:, 0:NHC * HD], ones_row, bvr_sb,
                                        start=True, stop=True))
    block.tensor(_)

    def _(v):
        v.wait_ge(sem["sP"], M["bv_pe"])
        M["bv"] = inc("sV", v.tensor_copy(bv_bc, pb[7][:, 0:NHC * HD]))
    block.vector(_)

    for j in range(2):
        def _(q, j=j):
            if j > 0:
                q.wait_ge(sem["sP"], M[("Arpe", j - 1, 3)])
            inc("dX", q.dma_start(
                out=xblk.rearrange("p (t q) -> p t q", t=DT),
                in_=xT.rearrange("(t p) q -> p t q", p=128)[:, :, j * BLK:(j + 1) * BLK]))
        block.sync(_)

        def _(t, j=j):
            t.wait_ge(sem["dX"], (j + 1) * 16)
            t.wait_ge(sem["sV"], M["consts"])
            if j > 0:
                t.wait_ge(sem["sV"], M[("Axn1", j - 1)])
            for i in range(DT):
                ins = t.matmul(pb[0][:, :], onesm,
                               xblk[:, i * BLK:(i + 1) * BLK],
                               start=(i == 0), stop=(i == DT - 1))
            M[("Asum", j)] = inc("sP", ins)
        block.tensor(_)

        for i in range(DT):
            def _(sc, j=j, i=i):
                if i == 0:
                    sc.wait_ge(sem["dX"], (j + 1) * 16)
                    if j > 0:
                        sc.wait_ge(sem["sV"], M[("Atw", j - 1, 3)])
                        sc.wait_ge(sem["sP"], M[("AtwT", j - 1, 3)])
                if i >= 2:
                    sc.wait_ge(sem["sP"], M[("Asqpe", j, i - 2)])
                M[("Asq", j, i)] = inc("sA", sc.activation(
                    sqt[i % 2], xblk[:, i * BLK:(i + 1) * BLK], AF.Square))
            block.scalar(_)

            def _(t, j=j, i=i):
                t.wait_ge(sem["sA"], M[("Asq", j, i)])
                M[("Asqpe", j, i)] = inc("sP", t.matmul(
                    pb[1][:, :], onesm, sqt[i % 2],
                    start=(i == 0), stop=(i == DT - 1)))
            block.tensor(_)

        def _(v, j=j):
            v.wait_ge(sem["sP"], M[("Asum", j)])
            v.wait_ge(sem["sP"], M[("Asqpe", j, DT - 1)])
            w = inc("sV", v.tensor_scalar_mul(sqt[0], pb[0][:, :], 1.0 / D))
            w = inc("sV", v.tensor_scalar_mul(sqt[1], pb[1][:, :], 1.0 / D))
            v.wait_ge(sem["sV"], w)
            w = inc("sV", v.tensor_mul(sqt[0], sqt[0], sqt[0]))
            v.wait_ge(sem["sV"], w)
            M[("Avar", j)] = inc("sV", v.tensor_sub(sqt[1], sqt[1], sqt[0]))
        block.vector(_)

        def _(sc, j=j):
            sc.wait_ge(sem["sV"], M[("Avar", j)])
            M[("Astd", j)] = inc("sA", sc.activation(rstd_t, sqt[1], AF.Sqrt,
                                                     bias=eps128, scale=1.0))
        block.scalar(_)

        def _(v, j=j):
            v.wait_ge(sem["sA"], M[("Astd", j)])
            M[("Arstd", j)] = inc("sV", v.reciprocal(rstd_t, rstd_t))
            v.wait_ge(sem["sV"], M[("Arstd", j)])
            for i in range(DT):
                ins = v.scalar_tensor_tensor(xn32[:, i * BLK:(i + 1) * BLK],
                                             pb[0][:, :], -1.0 / D,
                                             xblk[:, i * BLK:(i + 1) * BLK],
                                             ALU.mult, ALU.add)
            w = inc("sV", ins)
            v.wait_ge(sem["sV"], w)
            for i in range(DT):
                ins = v.tensor_mul(xn32[:, i * BLK:(i + 1) * BLK],
                                   xn32[:, i * BLK:(i + 1) * BLK], rstd_t)
            M[("Axn1", j)] = inc("sV", ins)
            v.wait_ge(sem["sV"], M[("Axn1", j)])
            for i in range(DT):
                ins = v.tensor_copy(xnb[j][:, i * BLK:(i + 1) * BLK],
                                    xn32[:, i * BLK:(i + 1) * BLK])
            M[("Axncp", j)] = inc("sV", ins)
        block.vector(_)

        def _(q, j=j):
            q.wait_ge(sem["sV"], M[("Axncp", j)])
            inc("dAgxA" if j == 0 else "dAgxB", q.dma_start(
                out=agin_x[j].ap().rearrange("(t p) q -> p t q", p=128),
                in_=xnv[j]))
        block.sync(_)

        pbT = [pbO[1].ap()[:, 0:64].bitcast(BF16), pbR.ap()[:, 0:64].bitcast(BF16)]
        for tt4 in range(4):
            for i in range(DT):
                M[("Axtk", j, tt4, i)] = C["sP"] + i + 1
                M[("Axtke", j, tt4, i)] = C["sV"] + i + 1

            def _(t, j=j, tt4=tt4):
                if tt4 == 0:
                    t.wait_ge(sem["sV"], M[("Axncp", j)])
                for i in range(DT):
                    if j > 0 or tt4 > 0 or i >= 2:
                        pi = (j, tt4, i - 2) if i >= 2 else                              ((j, tt4 - 1, DT - 2 + i) if tt4 > 0 else (j - 1, 3, DT - 2 + i))
                        t.wait_ge(sem["sV"], M[("Axtke",) + pi])
                    ins = t.transpose(pbT[i % 2],
                                      xnv[j][:, i, tt4 * 128:(tt4 + 1) * 128], idbf_sb)
                    assert inc("sP", ins) == M[("Axtk", j, tt4, i)]
            block.tensor(_)

            def _(v, j=j, tt4=tt4):
                if tt4 == 0:
                    v.wait_ge(sem["sV"], M[("Axncp", j)])
                for i in range(DT):
                    v.wait_ge(sem["sP"], M[("Axtk", j, tt4, i)])
                    if i == 0 and (j > 0 or tt4 >= 2):
                        pt_ = (j, tt4 - 2) if tt4 >= 2 else (j - 1, 2 + tt4)
                        v.wait_ge(sem[f"dAgx2{tt4 % 2}"], M[("Axtkd",) + pt_])
                    ins = v.tensor_copy(xtk[tt4 % 2][:, i * 128:(i + 1) * 128], pbT[i % 2])
                    assert inc("sV", ins) == M[("Axtke", j, tt4, i)]
            block.vector(_)

            def _(q, j=j, tt4=tt4):
                q.wait_ge(sem["sV"], M[("Axtke", j, tt4, DT - 1)])
                ins = q.dma_start(out=agin_x2.ap()[j * BLK + tt4 * 128:j * BLK + (tt4 + 1) * 128, :],
                                  in_=xtk[tt4 % 2])
                M[("Axtkd", j, tt4)] = inc(f"dAgx2{tt4 % 2}", ins)
            block.sync(_)

        for tt in range(4):
            def _(t, j=j, tt=tt):
                if tt == 0:
                    t.wait_ge(sem["dPre"], PRE_ALL)
                    t.wait_ge(sem["sV"], M[("Axn1", j)])
                prev = (j, tt - 1) if tt > 0 else ((j - 1, 3) if j > 0 else None)
                if prev is not None:
                    t.wait_ge(sem["sV"], M[("ArL", prev[0], prev[1])])
                for i in range(DT):
                    ins = t.matmul(pb[4][0:128, 0:8],
                                   xn32[:, i * BLK + tt * 128:i * BLK + tt * 128 + 128],
                                   gwv[:, i, :], start=(i == 0), stop=(i == DT - 1))
                M[("Arpe", j, tt)] = inc("sP", ins)
            block.tensor(_)

            def _(v, j=j, tt=tt):
                L = sqt[0][:, 0:8]; m1 = sqt[0][:, 8:9]; k1 = sqt[0][:, 16:24]
                L2 = sqt[0][:, 24:32]; m2 = sqt[0][:, 32:33]; k2 = sqt[0][:, 40:48]
                dd = sqt[0][:, 48:49]
                v.wait_ge(sem["sP"], M[("Arpe", j, tt)])
                M[("ArL", j, tt)] = w = inc("sV", v.tensor_copy(L, pb[4][0:128, 0:8]))
                v.wait_ge(sem["sV"], w)
                w = inc("sV", v.reduce_max(m1, L, axis=AX.X))
                v.wait_ge(sem["sV"], w)
                w = inc("sV", v.tensor_scalar(k1, L, m1, None, ALU.is_equal))
                v.wait_ge(sem["sV"], w)
                w = inc("sV", v.scalar_tensor_tensor(L2, k1, -1e30, L, ALU.mult, ALU.add))
                v.wait_ge(sem["sV"], w)
                w = inc("sV", v.reduce_max(m2, L2, axis=AX.X))
                v.wait_ge(sem["sV"], w)
                v.tensor_scalar(k2, L2, m2, None, ALU.is_equal)
                M[("Ar1", j, tt)] = inc("sV", v.tensor_sub(dd, m2, m1))
            block.vector(_)

            def _(sc, j=j, tt=tt):
                sc.wait_ge(sem["sV"], M[("Ar1", j, tt)])
                M[("Ar2", j, tt)] = inc("sA", sc.activation(
                    sqt[0][:, 49:50], sqt[0][:, 48:49], AF.Exp))
            block.scalar(_)

            def _(v, j=j, tt=tt):
                k1 = sqt[0][:, 16:24]; k2 = sqt[0][:, 40:48]
                ee = sqt[0][:, 49:50]; rr = sqt[0][:, 50:51]; w2 = sqt[0][:, 51:52]
                tw = sqt[1][:, 0:8]
                v.wait_ge(sem["sA"], M[("Ar2", j, tt)])
                w = inc("sV", v.tensor_scalar_add(rr, ee, 1.0))
                v.wait_ge(sem["sV"], w)
                w = inc("sV", v.reciprocal(rr, rr))
                v.wait_ge(sem["sV"], w)
                w = inc("sV", v.tensor_mul(w2, ee, rr))
                v.wait_ge(sem["sV"], w)
                prev = (j, tt - 1) if tt > 0 else ((j - 1, 3) if j > 0 else None)
                if prev is not None:
                    v.wait_ge(sem["sP"], M[("AtwT", prev[0], prev[1])])
                w = inc("sV", v.tensor_scalar(tw, k1, rr, None, ALU.mult))
                v.wait_ge(sem["sV"], w)
                M[("Atw", j, tt)] = inc("sV", v.scalar_tensor_tensor(
                    tw, k2, w2, tw, ALU.mult, ALU.add))
            block.vector(_)

            def _(t, j=j, tt=tt):
                t.wait_ge(sem["sV"], M[("Atw", j, tt)])
                M[("AtwT", j, tt)] = inc("sP", t.transpose(
                    pb[5][0:8, 0:128], sqt[1][:, 0:8], idf32_sb))
            block.tensor(_)

            def _(v, j=j, tt=tt):
                v.wait_ge(sem["sP"], M[("AtwT", j, tt)])
                M[("Atwc", j, tt)] = inc("sV", v.tensor_copy(
                    twt[j][0:8, tt * 128:(tt + 1) * 128], pb[5][0:8, 0:128]))
            block.vector(_)

        def _(q, j=j):
            q.wait_ge(sem["sV"], M[("Atwc", j, 3)])
            inc("dAgt", q.dma_start(out=agin_t.ap()[0:8, j * BLK:(j + 1) * BLK],
                                    in_=twt[j][0:8, :]))
        block.sync(_)

    if upto == "A":
        dbgA = {}
        for nm, shape, dt in [("t_xblk", [128, DT * BLK], F32),
                              ("t_xn32", [128, DT * BLK], F32),
                              ("t_srow", [128, BLK], F32),
                              ("t_agx", [D, TPC], BF16),
                              ("t_agt", [8, TPC], F32)]:
            dbgA[nm] = nc.dram_tensor(nm, shape, dt, kind="ExternalOutput").ap()

        def _(q):
            q.wait_ge(sem["sV"], M[("Axncp", 1)])
            q.wait_ge(sem["sV"], M[("Atwc", 1, 3)])
            q.wait_ge(sem["dAgx"], C["dAgx"])
            q.wait_ge(sem["dAgt"], C["dAgt"])
            inc("dDbg", q.dma_start(out=dbgA["t_xblk"], in_=xblk))
            inc("dDbg", q.dma_start(out=dbgA["t_xn32"], in_=xn32))
            inc("dDbg", q.dma_start(out=dbgA["t_srow"], in_=rstd_t))
            inc("dDbg", q.dma_start(out=dbgA["t_agx"], in_=agin_x.ap()))
            inc("dDbg", q.dma_start(out=dbgA["t_agt"], in_=agin_t.ap()))
            q.wait_ge(sem["dDbg"], C["dDbg"])
            q.wait_ge(sem["dQkv"], C["dQkv"])
            q.wait_ge(sem["dGate"], C["dGate"])
        block.sync(_)
        block.__exit__(None, None, None)
        return nc

    def _(q):
        q.wait_ge(sem["dAgt"], 32)
        inc("cAgt", q.collective_compute(
            "AllGather", ALU.bypass, replica_groups=[list(range(NCORE))],
            ins=[agin_t.ap().opt()], outs=[agout_t.ap().opt()]), 1)
        q.wait_ge(sem["dAgxA"], 16)
        inc("cAgx", q.collective_compute(
            "AllGather", ALU.bypass, replica_groups=[list(range(NCORE))],
            ins=[agin_x[0].ap().opt()], outs=[agout_x[0].ap().opt()]), 1)
        q.wait_ge(sem["dAgxB"], 16)
        inc("cAgxB", q.collective_compute(
            "AllGather", ALU.bypass, replica_groups=[list(range(NCORE))],
            ins=[agin_x[1].ap().opt()], outs=[agout_x[1].ap().opt()]), 1)
        q.wait_ge(sem["dAgx20"], 4 * 16)
        q.wait_ge(sem["dAgx21"], 4 * 16)
        inc("cAgk", q.collective_compute(
            "AllGather", ALU.bypass, replica_groups=[list(range(NCORE))],
            ins=[agin_x2.ap().opt()], outs=[xnTok.ap()[0:NTOK, :].opt()]), 1)
    block.gpsimd(_)

    def xn_load(q, tb):
        p = tb % 2
        r, half = tb // 2, tb % 2
        src = agout_x[half].ap()[r * D:(r + 1) * D, :]
        inc(f"dXn{p}", q.dma_start(out=xnv[p], in_=src.rearrange("(t p) q -> p t q", p=128)))

    def _(q):
        q.wait_ge(sem["cAgx"], 1)
        xn_load(q, 0)
        M[("B1x", 0)] = C["dXn0"]
        q.wait_ge(sem["cAgxB"], 1)
        xn_load(q, 1)
        M[("B1x", 1)] = C["dXn1"]
    block.sync(_)

    for blk in range(NB):
        rp = blk % 2
        r, half = blk // 2, blk % 2

        def _(q, blk=blk, rp=rp, r=r, half=half):
            if blk == 0:
                q.wait_ge(sem["cAgt"], 1)
            if blk >= 2:
                q.wait_ge(sem["sP"], M[("Rpe", blk - 2)])
            ins = q.dma_start(out=twtile2[rp],
                              in_=agout_t.ap()[r * 8:(r + 1) * 8, half * BLK:(half + 1) * BLK])
            M[("Rld", blk)] = inc(f"dR{rp}", ins)
        block.sync(_)

        def _(t, blk=blk, rp=rp):
            t.wait_ge(sem[f"dR{rp}"], M[("Rld", blk)])
            if blk == 0:
                t.wait_ge(sem["sV"], M[("Axtke", 1, 3, DT - 1)])
            if blk >= 1:
                t.wait_ge(sem["sV"], M[("Rcp", blk - 1)])
            M[("Rpe", blk)] = inc("sP", t.matmul(
                pbO[1].ap()[:, :], selb_sb, twtile2[rp], start=True, stop=True))
        block.tensor(_)

        def _(v, blk=blk, rp=rp):
            if blk == 0:
                v.memset(ones512r, 1.0)
                v.memset(zeros512r, 0.0)
                v.memset(baserow, 0.0)
                v.memset(zpad, 0.0)
                v.wait_ge(sem["dPre"], PRE_ALL)
            v.wait_ge(sem["sP"], M[("Rpe", blk)])
            if blk >= 2:
                v.wait_ge(sem[f"dRo{rp}"], M[("Rod", blk - 2)])
            M[("Rcp", blk)] = inc("sV", v.tensor_copy(twseg[rp], pbO[1].ap()[0:1, :]))
            w = M[("Rcp", blk)]
            v.wait_ge(sem["sV"], w)
            w = inc("sV", v.tensor_tensor(maskseg, twseg[rp], zeros512r, ALU.is_gt))
            v.wait_ge(sem["sV"], w)
            w = inc("sV", v.tensor_tensor_scan(posseg, ones512r, maskseg, 0.0,
                                               ALU.mult, ALU.add))
            v.wait_ge(sem["sV"], w)
            w = inc("sV", v.tensor_scalar(posseg, posseg, baserow, None, ALU.add))
            v.wait_ge(sem["sV"], w)
            w = inc("sV", v.tensor_copy(baserow, posseg[0:1, BLK - 1:BLK]))
            v.wait_ge(sem["sV"], w)
            w = inc("sV", v.tensor_sub(t1seg, posseg, ones512r))
            v.wait_ge(sem["sV"], w)
            w = inc("sV", v.tensor_mul(t1seg, t1seg, maskseg))
            v.wait_ge(sem["sV"], w)
            w = inc("sV", v.tensor_sub(u1seg, ones512r, maskseg))
            v.wait_ge(sem["sV"], w)
            w = inc("sV", v.tensor_mul(u1seg, u1seg, dmp_sb))
            v.wait_ge(sem["sV"], w)
            w = inc("sV", v.tensor_add(tgtfseg, u1seg, t1seg))
            v.wait_ge(sem["sV"], w)
            if blk >= 2:
                v.wait_ge(sem[f"dRt{rp}"], M[("Rtd", blk - 2)])
            M[("Rch", blk)] = inc("sV", v.tensor_copy(tgtseg[rp], tgtfseg))
        block.vector(_)

        def _(q, blk=blk, rp=rp):
            q.wait_ge(sem["sV"], M[("Rcp", blk)])
            ins = q.dma_start(out=twtok.ap()[blk * BLK:(blk + 1) * BLK, 0:1],
                              in_=twseg[rp])
            M[("Rod", blk)] = inc(f"dRo{rp}", ins)
            q.wait_ge(sem["sV"], M[("Rch", blk)])
            ins = q.dma_start(out=tgtd.ap()[blk * BLK:(blk + 1) * BLK, 0:1],
                              in_=tgtseg[rp])
            M[("Rtd", blk)] = inc(f"dRt{rp}", ins)
        block.sync(_)

    for tb in range(NB):
        p = tb % 2

        if tb >= 2:
            def _(q, tb=tb, p=p):
                q.wait_ge(sem["sP"], M[("B1pe", tb - 2)])
                xn_load(q, tb)
                M[("B1x", tb)] = C[f"dXn{p}"]
            block.sync(_)

        def _(t, tb=tb, p=p):
            t.wait_ge(sem[f"dXn{p}"], M[("B1x", tb)])
            if tb == 0:
                t.wait_ge(sem["dQkv"], 48)
            if tb >= 1:
                t.wait_ge(sem["sA"], M[("B1evA", tb - 1)])
                t.wait_ge(sem["sV"], M[("B1evV", tb - 1)])
            xv = xnv[p]
            for h in range(NHC):
                for i in range(DT):
                    ins = t.matmul(pb[0 + h][:, :], wqv[:, i, h * HD:(h + 1) * HD],
                                   xv[:, i, :], start=(i == 0), stop=(i == DT - 1))
                M[("B1q", tb, h)] = inc("sP", ins)
                for i in range(DT):
                    ins = t.matmul(pb[2 + h][:, :], wkv[:, i, h * HD:(h + 1) * HD],
                                   xv[:, i, :], start=(i == 0), stop=(i == DT - 1))
                M[("B1k", tb, h)] = inc("sP", ins)
            for s4 in range(4):
                bank, half = 4 + s4 // 2, s4 % 2
                for i in range(DT):
                    ins = t.matmul(pb[bank][:, half * 256:(half + 1) * 256],
                                   xv[:, i, s4 * 128:(s4 + 1) * 128],
                                   wvv[:, i, :], start=(i == 0), stop=(i == DT - 1))
                if s4 % 2 == 1:
                    M[("B1v", tb, s4 // 2)] = inc("sP", ins)
            M[("B1pe", tb)] = C["sP"]
        block.tensor(_)

        def _(sc, tb=tb):
            if tb == 0:
                sc.wait_ge(sem["dPre"], PRE_ALL)
            for h in range(NHC):
                sc.wait_ge(sem["sP"], M[("B1q", tb, h)])
                sc.activation(QT_sb[:, h * NTOK + tb * BLK:h * NTOK + (tb + 1) * BLK],
                              pb[0 + h][:, :], AF.Identity, bias=bq_sb[:, h:h + 1],
                              scale=1.0)
                sc.wait_ge(sem["sP"], M[("B1k", tb, h)])
                ins = sc.activation(KT_sb[:, h * NTOK + tb * BLK:h * NTOK + (tb + 1) * BLK],
                                    pb[2 + h][:, :], AF.Identity, bias=bk_sb[:, h:h + 1],
                                    scale=1.0)
            M[("B1evA", tb)] = inc("sA", ins)
        block.scalar(_)

        def _(v, tb=tb):
            if tb == 0:
                v.wait_ge(sem["sV"], M["bv"])
                v.wait_ge(sem["dAgt"], 32)
                v.wait_ge(sem["dAgx20"], 64)
                v.wait_ge(sem["dAgx21"], 64)
            for s4 in range(4):
                bank, half = 4 + s4 // 2, s4 % 2
                g = tb * 4 + s4
                v.wait_ge(sem["sP"], M[("B1v", tb, s4 // 2)])
                ins = v.tensor_add(V_sb[:, g * 256:(g + 1) * 256],
                                   pb[bank][:, half * 256:(half + 1) * 256], bv_bc)
            M[("B1evV", tb)] = inc("sV", ins)
        block.vector(_)

    def _(q):
        q.wait_ge(sem["sP"], M[("B1pe", 14)])
        for m in range(4):
            inc("dMask", q.dma_start(out=mask_sb[m], in_=masks[m]))
    block.sync(_)

    def _(g):
        from concourse.bass import IndirectOffsetOnAxis
        g.wait_ge(sem["sV"], M[("Rcp", 0)])
        inc("dRb1", g.dma_start(out=xnTok.ap()[NTOK:NTOK + 128, :], in_=zpad))
        inc("dRb1", g.dma_start(out=twtok.ap()[NTOK:NTOK + 128, 0:1], in_=zpad_f32))
        g.wait_ge(sem["sP"], M[("B1pe", NB - 1)])
        g.wait_ge(sem["cAgk"], 1)
        inc("dRb0", g.dma_start(out=seld.ap()[0:NSEL, :],
                                in_=pad8192.rearrange("p c -> (p c)").unsqueeze(1)))
        g.wait_ge(sem["dRt0"], 8 * 16)
        g.wait_ge(sem["dRt1"], 8 * 16)
        inc("dRb0", g.dma_start(out=tgt_sb,
                                in_=tgtd.ap().rearrange("(p c) o -> p (c o)", p=128)))
        g.wait_ge(sem["dPre"], PRE_ALL)
        g.wait_ge(sem["dRb0"], 2 * 16)
        for c in range(64):
            ins = g.indirect_dma_start(
                seld.ap(),
                IndirectOffsetOnAxis(tgt_sb[:, c:c + 1], 0),
                tok_sb[:, c:c + 1], None)
            inc("dRg", ins)
    block.gpsimd(_)
    M["RgScat"] = C["dRg"]

    baseP, baseA, baseV = C["sP"], C["sA"], C["sV"]
    sMp, pvMp, expMp = {}, {}, {}
    bcM, maskM = {}, {}
    rrowM, recM, recbcM, otvM = {}, {}, {}, {}
    p_, a_, v_ = baseP, baseA, baseV
    gq = 0
    gp = 0
    for u in range(4):
        for qb in range(QBN):
            nk = 4 * qb + 4
            np_ = nk // 2
            order = [("s", 0)] + ([("s", 1)] if np_ > 1 else [])
            for pk in range(np_):
                if pk + 2 < np_:
                    order.append(("s", pk + 2))
                order.append(("pv", pk))
            for kind, pk in order:
                p_ += 1
                if kind == "s":
                    sMp[gp + pk] = p_
                else:
                    pvMp[gp + pk] = p_
            for pk in range(np_):
                expMp[gp + pk] = a_ + 1
                a_ += 1
            for kt in range(4 * qb, nk):
                maskM[(gq, kt)] = v_ + 1
                v_ += 1
            bcM[gq] = p_ + 1
            p_ += 1
            rrowM[gq] = v_ + 1
            recM[gq] = v_ + 2
            recbcM[gq] = v_ + 3
            otvM[gq] = v_ + 4
            v_ += 4
            gp += np_
            gq += 1

    gq = 0
    gp = 0
    for u in range(4):
        b, hh = u // 2, u % 2
        qoff = hh * NTOK + b * S
        for qb in range(QBN):
            nk = 4 * qb + 4
            np_ = nk // 2
            g0, gq0 = gp, gq

            def _(sc, u=u, qb=qb, nk=nk, np_=np_, g0=g0):
                for pk in range(np_):
                    gg = g0 + pk
                    sc.wait_ge(sem["sP"], sMp[gg])
                    if gg >= 2:
                        sc.wait_ge(sem["sP"], pvMp[gg - 2])
                    ins = sc.activation(pt_pair[gg % 2], psS[gg % 2].ap()[:, :],
                                        AF.Exp, bias=neg8, scale=SCALE)
                    assert inc("sA", ins) == expMp[gg]
            block.scalar(_)

            def _(v, u=u, qb=qb, nk=nk, np_=np_, g0=g0, gq0=gq0):
                if u == 0 and qb == 0:
                    v.wait_ge(sem["dMask"], 64)
                for kt in range(4 * qb, nk):
                    pk = kt // 2
                    gg = g0 + pk
                    v.wait_ge(sem["sA"], expMp[gg])
                    half = pt_pair[gg % 2][:, (kt % 2) * BLK:(kt % 2 + 1) * BLK]
                    ins = v.tensor_mul(half, half, mask_sb[kt - 4 * qb])
                    assert inc("sV", ins) == maskM[(gq0, kt)]
                v.wait_ge(sem["sP"], pvMp[g0 + np_ - 1])
                if gq0 >= 2:
                    v.wait_ge(sem["sP"], bcM[gq0 - 2])
                assert inc("sV", v.tensor_copy(rrow[gq0 % 2], pb[7][0:1, :])) == rrowM[gq0]
                v.wait_ge(sem["sV"], rrowM[gq0])
                assert inc("sV", v.reciprocal(rrow[gq0 % 2], rrow[gq0 % 2])) == recM[gq0]
                v.wait_ge(sem["sP"], bcM[gq0])
                assert inc("sV", v.tensor_copy(rec_bc, pb[4][:, :])) == recbcM[gq0]
                v.wait_ge(sem["sV"], recbcM[gq0])
                if gq0 == 0:
                    v.wait_ge(sem["dRt0"], 8 * 16)
                    v.wait_ge(sem["dRt1"], 8 * 16)
                    v.wait_ge(sem["dRb1"], 2 * 16)
                ocol = (u % 2) * NTOK + (u // 2) * S + qb * BLK
                ins = v.tensor_mul(ot_bf[:, ocol:ocol + BLK],
                                   pbO[gq0 % 2].ap()[:, :], rec_bc)
                assert inc("sV", ins) == otvM[gq0]
            block.vector(_)

            def _(t, u=u, qb=qb, nk=nk, np_=np_, g0=g0, gq0=gq0, hh=hh, b=b, qoff=qoff):
                if u == 0 and qb == 0:
                    t.wait_ge(sem["sA"], M[("B1evA", NB - 1)])
                    t.wait_ge(sem["sV"], M[("B1evV", NB - 1)])
                if gq0 >= 2:
                    t.wait_ge(sem["sV"], otvM[gq0 - 2])
                if gq0 >= 1:
                    t.wait_ge(sem["sV"], rrowM[gq0 - 1])

                def s_pair(pk):
                    gg = g0 + pk
                    if gg >= 2:
                        t.wait_ge(sem["sA"], expMp[gg - 2])
                    for half in range(2):
                        kt = 2 * pk + half
                        ins = t.matmul(psS[gg % 2].ap()[:, half * BLK:(half + 1) * BLK],
                                       KT_sb[:, qoff + kt * HD:qoff + (kt + 1) * HD],
                                       QT_sb[:, qoff + qb * BLK:qoff + (qb + 1) * BLK],
                                       start=True, stop=True)
                    assert inc("sP", ins) == sMp[gg]

                def pv_block(pk):
                    gg = g0 + pk
                    t.wait_ge(sem["sA"], expMp[gg])
                    for half in range(2):
                        kt = 2 * pk + half
                        if kt >= 4 * qb:
                            t.wait_ge(sem["sV"], maskM[(gq0, kt)])
                    for half in range(2):
                        kt = 2 * pk + half
                        pth = pt_pair[gg % 2][:, half * BLK:(half + 1) * BLK]
                        t.matmul(pbO[gq0 % 2].ap()[:, :],
                                 V_sb[:, (b * 32 + kt) * 256 + hh * HD:(b * 32 + kt) * 256 + (hh + 1) * HD],
                                 pth, start=(kt == 0), stop=(kt == nk - 1))
                        ins = t.matmul(pb[7][0:1, :], ones_col_bf, pth,
                                       start=(kt == 0), stop=(kt == nk - 1))
                    assert inc("sP", ins) == pvMp[gg]

                s_pair(0)
                if np_ > 1:
                    s_pair(1)
                for pk in range(np_):
                    if pk + 2 < np_:
                        s_pair(pk + 2)
                    pv_block(pk)
                t.wait_ge(sem["sV"], recM[gq0])
                if gq0 >= 1:
                    t.wait_ge(sem["sV"], recbcM[gq0 - 1])
                ins = t.matmul(pb[4][:, :], ones_row, rrow[gq0 % 2], start=True, stop=True)
                assert inc("sP", ins) == bcM[gq0]
            block.tensor(_)

            gp += np_
            gq += 1

    M["B2peEnd"] = C["sP"]
    M["B2vEnd"] = C["sV"]
    M["B2aEnd"] = C["sA"]

    def _(q):
        q.wait_ge(sem["sP"], M["B2peEnd"])
        inc("dUp", q.dma_start(out=uv, in_=uTd.rearrange("(t p) f -> p t f", p=128)))
        inc("dDown", q.dma_start(out=dvw, in_=dTd.rearrange("(t p) d -> p t d", p=128)))
        inc("dWo", q.dma_start(out=wov, in_=woT.rearrange("(h p) d -> p h d", p=128)))
        inc("dWo", q.dma_start(out=borow, in_=bor))
        q.wait_ge(sem["dRg"], M["RgScat"])
        inc("dRg", q.dma_start(out=sel_sb,
                               in_=seld.ap()[0:NSEL, :].rearrange("(p c) o -> p (c o)", p=128)))
        M["Rg"] = C["dRg"]
    block.gpsimd(_)

    for blk in range(4):
        def _(t, blk=blk):
            if blk == 0:
                t.wait_ge(sem["sV"], recbcM[31])
                t.wait_ge(sem["dWo"], 32)
            if blk > 0:
                t.wait_ge(sem["sV"], M[("bobc2", blk - 1)])
            M[("bobc2pe", blk)] = inc("sP", t.matmul(
                pb[4][:, :], ones_row, borow[0:1, blk * BLK:(blk + 1) * BLK],
                start=True, stop=True))
        block.tensor(_)

        def _(v, blk=blk):
            v.wait_ge(sem["sP"], M[("bobc2pe", blk)])
            M[("bobc2", blk)] = inc("sV", v.tensor_scalar_mul(
                bo_bc[:, blk * BLK:(blk + 1) * BLK], pb[4][:, :], 1.0 / NCORE))
        block.vector(_)

    _stg_base = {i: C[f"dStg{i}"] for i in range(4)}
    for k in range(256):
        _stg_base[k % 4] += 16
        M[("B25st", k)] = _stg_base[k % 4]

    for tt in range(64):
        b = tt // 32
        qb25 = (tt % 32) // 4
        po = tt % 2

        def _(t, tt=tt, b=b, qb25=qb25):
            t.wait_ge(sem["sV"], otvM[(2 * b + 1) * QBN + qb25])
            if tt == 0:
                t.wait_ge(sem["sA"], M["B2aEnd"])
            for blk in range(4):
                if tt >= 1:
                    t.wait_ge(sem["sV"], M[("B25ev", tt - 1, blk)])
                for hh in range(NHC):
                    ins = t.matmul(pb[blk][:, :],
                                   ot_bf[:, hh * NTOK + tt * 128:hh * NTOK + (tt + 1) * 128],
                                   wov[:, hh, blk * BLK:(blk + 1) * BLK],
                                   start=(hh == 0), stop=(hh == NHC - 1))
            M[("B25pe", tt)] = inc("sP", ins)
        block.tensor(_)

        def _(v, tt=tt):
            v.wait_ge(sem["sP"], M[("B25pe", tt)])
            if tt == 0:
                v.wait_ge(sem["sV"], M[("bobc2", 3)])
            for blk in range(4):
                sp = (4 * tt + blk) % 4
                if 4 * tt + blk >= 4:
                    v.wait_ge(sem[f"dStg{sp}"], M[("B25st", 4 * tt + blk - 4)])
                ins = v.tensor_add(stg25[sp], pb[blk][:, :],
                                   bo_bc[:, blk * BLK:(blk + 1) * BLK])
                M[("B25ev", tt, blk)] = inc("sV", ins)
        block.vector(_)

        def _(sc, tt=tt):
            for blk in range(4):
                sp = (4 * tt + blk) % 4
                sc.wait_ge(sem["sV"], M[("B25ev", tt, blk)])
                ins = sc.dma_start(
                    out=partT.ap()[tt * 128:(tt + 1) * 128, blk * BLK:(blk + 1) * BLK],
                    in_=stg25[sp])
                assert inc(f"dStg{sp}", ins) == M[("B25st", 4 * tt + blk)]
        block.scalar(_)

    def _(q):
        q.wait_ge(sem["sP"], M[("B25pe", 63)])
        q.wait_ge(sem["sV"], M["B2vEnd"])
        q.wait_ge(sem["dR0"], 8 * 16)
        q.wait_ge(sem["dR1"], 8 * 16)
        q.wait_ge(sem["dRo0"], 8 * 16)
        q.wait_ge(sem["dRo1"], 8 * 16)
        q.wait_ge(sem["dRb0"], 2 * 16)
        q.wait_ge(sem["dRb1"], 2 * 16)
        inc("dGate", q.dma_start(out=gv, in_=gTd.rearrange("(t p) f -> p t f", p=128)))
    block.gpsimd(_)

    baseP3, baseA3, baseV3 = C["sP"], C["sA"], C["sV"]
    ga_use = {(a, b): C[f"dGa{a}{b}"] for a in range(2) for b in range(2)}
    sc_use = C["dSc"]
    p_, a_, v_ = baseP3, baseA3, baseV3
    for sb in range(NSB):
        for q4 in range(4):
            for i in range(DT):
                M[("Str", sb, q4, i)] = p_ + 1
                p_ += 1
        for ft in range(FT):
            M[("Sgu", sb, ft)] = p_ + 1
            p_ += 1
        for gd in range(64):
            M[("Sdn", sb, gd // DT, gd % DT)] = p_ + 1
            p_ += 1
            if gd > 0:
                M[("Setr", sb, (gd - 1) // DT, (gd - 1) % DT)] = p_ + 1
                p_ += 1
        M[("Setr", sb, 3, DT - 1)] = p_ + 1
        p_ += 1
        for ft in range(FT):
            M[("Ssg", sb, ft)] = a_ + 1
            a_ += 1
        for gd in range(64):
            M[("Ssc", sb, gd // DT, gd % DT)] = a_ + 1
            a_ += 1
        for q4 in range(4):
            for i in range(DT):
                M[("Sxc", sb, q4, i)] = v_ + 1
                v_ += 1
        for ft in range(FT):
            M[("Sh", sb, ft)] = v_ + 1
            v_ += 1
        for gd in range(64):
            M[("Sev", sb, gd // DT, gd % DT)] = v_ + 1
            v_ += 1
        for q4 in range(4):
            ga_use[(sb % 2, q4 % 2)] += 32
            M[("Sga", sb, q4)] = ga_use[(sb % 2, q4 % 2)]
        for q4 in range(4):
            sc_use += 16
            M[("Sscat", sb, q4)] = sc_use

    pbT2 = [psS[0].ap()[:, 0:64].bitcast(BF16), psS[0].ap()[:, BLK:BLK + 64].bitcast(BF16)]
    pbT3 = [pbBc.ap()[:, 0:64].bitcast(BF16), pbR.ap()[:, 0:64].bitcast(BF16)]

    def selv(gg):
        return sel_sb.rearrange("p (c o) -> p c o", o=1)[:, gg, :]

    for sb in range(NSB):
        sp2 = sb % 2

        def _(g, sb=sb, sp2=sp2):
            from concourse.bass import IndirectOffsetOnAxis
            if sb == 0:
                g.wait_ge(sem["dRg"], M["Rg"])
                g.wait_ge(sem["cAgk"], 1)
                g.wait_ge(sem["dRo0"], C["dRo0"])
                g.wait_ge(sem["dRo1"], C["dRo1"])
            for q4 in range(4):
                if sb > 0:
                    g.wait_ge(sem["sA"], M[("Ssc", sb - 1, q4, DT - 1)])
                    assert inc("dSc", g.indirect_dma_start(
                        partT.ap(), IndirectOffsetOnAxis(selv((sb - 1) * 4 + q4), 0),
                        eo_stage, None, compute_op=ALU.add)) == M[("Sscat", sb - 1, q4)]
                gg = sb * 4 + q4
                if gg >= 2:
                    pq = divmod(gg - 2, 4)
                    g.wait_ge(sem["sP"], M[("Str", pq[0], pq[1], DT - 1)])
                if sb >= 2 and q4 == 0:
                    g.wait_ge(sem["sA"], M[("Ssc", sb - 2, 3, DT - 1)])
                inc(f"dGa{sp2}{q4 % 2}", g.indirect_dma_start(
                    xg[q4 % 2], None, xnTok.ap(), IndirectOffsetOnAxis(selv(gg), 0)))
                inc(f"dGa{sp2}{q4 % 2}", g.indirect_dma_start(
                    twg_sb[:, sp2 * 4 + q4:sp2 * 4 + q4 + 1], None,
                    twtok.ap(), IndirectOffsetOnAxis(selv(gg), 0)))
                assert C[f"dGa{sp2}{q4 % 2}"] == M[("Sga", sb, q4)]
        block.gpsimd(_)

        def _(t, sb=sb, sp2=sp2):
            if sb == 0:
                t.wait_ge(sem["sV"], M[("B25ev", 63, 3)])
            for q4 in range(4):
                t.wait_ge(sem[f"dGa{sp2}{q4 % 2}"], M[("Sga", sb, q4)])
                for i in range(DT):
                    if sb > 0 or q4 > 0 or i >= 2:
                        pi = (sb, q4, i - 2) if i >= 2 else \
                             ((sb, q4 - 1, DT - 2 + i) if q4 > 0 else (sb - 1, 3, DT - 2 + i))
                        t.wait_ge(sem["sV"], M[("Sxc",) + pi])
                    assert inc("sP", t.transpose(
                        pbT2[i % 2],
                        xg[q4 % 2].rearrange("p (t c) -> p t c", t=DT)[:, i, :],
                        idbf_sb)) == M[("Str", sb, q4, i)]
            xv = xnv[sb % 2]
            for ft in range(FT):
                gb, ub = 0 + ft % 2, 2 + ft % 2
                if ft == 0:
                    t.wait_ge(sem["sV"], M[("Sxc", sb, 3, DT - 1)])
                if sb > 0 or ft >= 2:
                    pft = (sb, ft - 2) if ft >= 2 else (sb - 1, FT - 2 + ft)
                    t.wait_ge(sem["sA"], M[("Ssg",) + pft])
                    t.wait_ge(sem["sV"], M[("Sh",) + pft])
                if sb == 0 and ft == 0:
                    t.wait_ge(sem["dGate"], 16)
                    t.wait_ge(sem["dUp"], 16)
                for i in range(DT):
                    t.matmul(pb[gb][:, :], gv[:, i, ft * HD:(ft + 1) * HD],
                             xv[:, i, :], start=(i == 0), stop=(i == DT - 1))
                for i in range(DT):
                    ins = t.matmul(pb[ub][:, :], uv[:, i, ft * HD:(ft + 1) * HD],
                                   xv[:, i, :], start=(i == 0), stop=(i == DT - 1))
                assert inc("sP", ins) == M[("Sgu", sb, ft)]
            for gd in range(64):
                q4, d = gd // DT, gd % DT
                bank = 5 + gd % 2
                if gd == 0:
                    t.wait_ge(sem["sV"], M[("Sh", sb, FT - 1)])
                    if sb == 0:
                        t.wait_ge(sem["dDown"], 16)
                if sb > 0 or gd >= 2:
                    pgd = gd - 2 if gd >= 2 else 64 - 2 + gd
                    psb = sb if gd >= 2 else sb - 1
                    t.wait_ge(sem["sV"], M[("Sev", psb, pgd // DT, pgd % DT)])
                for ft in range(FT):
                    ins = t.matmul(
                        pb[bank][:, 0:128],
                        dvw[:, ft, d * HD:(d + 1) * HD],
                        hp_t[ft][:, q4 * 128:(q4 + 1) * 128],
                        start=(ft == 0), stop=(ft == FT - 1))
                assert inc("sP", ins) == M[("Sdn", sb, q4, d)]
                etr = gd - 1
                if etr >= 0:
                    eq4, ed = etr // DT, etr % DT
                    t.wait_ge(sem["sV"], M[("Sev", sb, eq4, ed)])
                    if sb > 0 or etr >= 2:
                        pgd = etr - 2 if etr >= 2 else 64 - 2 + etr
                        psb = sb if etr >= 2 else sb - 1
                        t.wait_ge(sem["sA"], M[("Ssc", psb, pgd // DT, pgd % DT)])
                    assert inc("sP", t.transpose(
                        pbT3[etr % 2], eo_bf[etr % 2], idbf_sb)) == M[("Setr", sb, eq4, ed)]
            t.wait_ge(sem["sV"], M[("Sev", sb, 3, DT - 1)])
            t.wait_ge(sem["sA"], M[("Ssc", sb, 3, DT - 3)])
            assert inc("sP", t.transpose(
                pbT3[63 % 2], eo_bf[63 % 2], idbf_sb)) == M[("Setr", sb, 3, DT - 1)]
        block.tensor(_)

        def _(sc, sb=sb, sp2=sp2):
            for ft in range(FT):
                if sb == 0 and ft == 0:
                    for k in range(4):
                        sc.wait_ge(sem[f"dStg{k}"], C[f"dStg{k}"])
                sc.wait_ge(sem["sP"], M[("Sgu", sb, ft)])
                assert inc("sA", sc.activation(
                    sg_t[ft % 2], pb[0 + ft % 2][:, :], AF.Silu)) == M[("Ssg", sb, ft)]
            for gd in range(64):
                q4, d = gd // DT, gd % DT
                sc.wait_ge(sem["sP"], M[("Setr", sb, q4, d)])
                if gd == 0:
                    if sb == 0:
                        sc.wait_ge(sem["sP"], M[("B25pe", 63)])
                    if sb > 0:
                        sc.wait_ge(sem["dSc"], M[("Sscat", sb - 1, 3)])
                elif d == 0:
                    sc.wait_ge(sem["dSc"], M[("Sscat", sb, q4 - 1)])
                assert inc("sA", sc.activation(
                    eo_stage[:, d * 128:(d + 1) * 128], pbT3[d % 2],
                    AF.Identity,
                    scale=twg_sb[:, sp2 * 4 + q4:sp2 * 4 + q4 + 1])) == M[("Ssc", sb, q4, d)]
        block.scalar(_)

        def _(v, sb=sb, sp2=sp2):
            for q4 in range(4):
                for i in range(DT):
                    v.wait_ge(sem["sP"], M[("Str", sb, q4, i)])
                    if sb >= 2 and q4 == 0 and i == 0:
                        v.wait_ge(sem["sP"], M[("Sgu", sb - 2, FT - 1)])
                    assert inc("sV", v.tensor_copy(
                        xnv[sb % 2][:, i, q4 * 128:(q4 + 1) * 128],
                        pbT2[i % 2])) == M[("Sxc", sb, q4, i)]
            for ft in range(FT):
                v.wait_ge(sem["sA"], M[("Ssg", sb, ft)])
                if sb == 0 and ft == 0:
                    for k in range(4):
                        v.wait_ge(sem[f"dStg{k}"], C[f"dStg{k}"])
                if sb >= 1 and ft == 0:
                    v.wait_ge(sem["sP"], M[("Sdn", sb - 1, 3, DT - 1)])
                assert inc("sV", v.tensor_mul(
                    hp_t[ft], pb[2 + ft % 2][:, :], sg_t[ft % 2])) == M[("Sh", sb, ft)]
            for gd in range(64):
                q4, d = gd // DT, gd % DT
                bank = 5 + gd % 2
                v.wait_ge(sem["sP"], M[("Sdn", sb, q4, d)])
                if sb > 0 or gd >= 2:
                    pgd = gd - 2 if gd >= 2 else 64 - 2 + gd
                    psb = sb if gd >= 2 else sb - 1
                    v.wait_ge(sem["sP"], M[("Setr", psb, pgd // DT, pgd % DT)])
                assert inc("sV", v.tensor_copy(
                    eo_bf[gd % 2],
                    pb[bank][:, 0:128])) == M[("Sev", sb, q4, d)]
        block.vector(_)

    def _(g):
        from concourse.bass import IndirectOffsetOnAxis
        for q4 in range(4):
            g.wait_ge(sem["sA"], M[("Ssc", NSB - 1, q4, DT - 1)])
            assert inc("dSc", g.indirect_dma_start(
                partT.ap(), IndirectOffsetOnAxis(selv((NSB - 1) * 4 + q4), 0),
                eo_stage, None, compute_op=ALU.add)) == M[("Sscat", NSB - 1, q4)]
    block.gpsimd(_)

    def _(q):
        for k in range(4):
            q.wait_ge(sem[f"dStg{k}"], C[f"dStg{k}"])
        q.wait_ge(sem["dSc"], C["dSc"])
        inc("cRs", q.collective_compute(
            "ReduceScatter", ALU.add, replica_groups=[list(range(NCORE))],
            ins=[partT.ap()[0:NTOK, :].opt()], outs=[rsoutT.ap().opt()]), 1)
    block.gpsimd(_)

    for it in range(8):
        p = it % 2

        def _(q, it=it, p=p):
            if it == 0:
                q.wait_ge(sem["cRs"], 1)
            if it >= 2:
                q.wait_ge(sem["sV"], M[("Ca", it - 2)])
                q.wait_ge(sem[f"dCo{p}"], M[("Co", it - 2)])
            inc(f"dCl{p}", q.dma_start(
                out=ct_rs[p], in_=rsoutT.ap()[it * 128:(it + 1) * 128, :]))
            ins = q.dma_start(
                out=ct_x[p], in_=xTok[it * 128:(it + 1) * 128, :])
            M[("Cl", it)] = inc(f"dCl{p}", ins)
        block.sync(_)

        def _(v, it=it, p=p):
            v.wait_ge(sem[f"dCl{p}"], M[("Cl", it)])
            if it >= 2:
                v.wait_ge(sem[f"dCo{p}"], M[("Co", it - 2)])
            M[("Ca", it)] = inc("sV", v.tensor_add(ct_o[p], ct_rs[p], ct_x[p]))
        block.vector(_)

        def _(sc, it=it, p=p):
            sc.wait_ge(sem["sV"], M[("Ca", it)])
            ins = sc.dma_start(out=outTok[it * 128:(it + 1) * 128, :], in_=ct_o[p])
            M[("Co", it)] = inc(f"dCo{p}", ins)
        block.scalar(_)

    if debug:
        def _(q):
            for nm, (dst, src) in dbg.items():
                inc("dDbg", q.dma_start(out=dst, in_=src.ap()))
        block.sync(_)

    def _(q):
        for k in SEM_NAMES:
            if (k.startswith("d") or k.startswith("c")) and C[k]:
                q.wait_ge(sem[k], C[k])
    block.sync(_)

    block.__exit__(None, None, None)
    return nc


def prep_inputs(L0_attn_h, L0_attn_Wq, L0_attn_bq, L0_attn_Wk, L0_attn_bk,
                L0_attn_Wv, L0_attn_bv, L0_attn_Wo, L0_attn_bo,
                L0_ffn_gate_w, L0_ffn_gate_proj, L0_ffn_up_proj, L0_ffn_down_proj):
    x = np.ascontiguousarray(np.asarray(L0_attn_h, dtype=np.float32).reshape(NTOK, D).T)
    wqT = np.ascontiguousarray(np.asarray(L0_attn_Wq).T)
    wkT = np.ascontiguousarray(np.asarray(L0_attn_Wk).T)
    wvT = np.ascontiguousarray(np.asarray(L0_attn_Wv).T)
    woT = np.ascontiguousarray(np.asarray(L0_attn_Wo).T)
    gwT = np.ascontiguousarray(np.asarray(L0_ffn_gate_w).T)
    bq = np.asarray(L0_attn_bq); bk = np.asarray(L0_attn_bk)
    bv = np.asarray(L0_attn_bv); bo = np.asarray(L0_attn_bo)
    gp = np.asarray(L0_ffn_gate_proj); up = np.asarray(L0_ffn_up_proj)
    dp = np.asarray(L0_ffn_down_proj)

    tokid_arr = np.ascontiguousarray(
        (np.arange(128)[:, None] * 64 + np.arange(64)[None, :]).astype(np.int32))
    dmp_arr = (NSEL + (np.arange(BLK) % 64)).astype(np.float32).reshape(1, BLK)
    padarr = np.full((128, NSEL // 128), NTOK, np.int32)
    idbf_arr = np.eye(128, dtype=BFNP)
    mk = np.zeros((4, 128, BLK), dtype=BFNP)
    for m in range(4):
        i = np.arange(128)[:, None]
        jj = np.arange(BLK)[None, :]
        mk[m] = (128 * m + i <= jj).astype(BFNP)
    idf = np.eye(128, dtype=np.float32)

    in_maps = []
    for c in range(NCORE):
        hsl = slice(2 * c * HD, (2 * c + 2) * HD)
        sel = np.zeros((8, 128), np.float32)
        sel[c, :] = 1.0
        in_maps.append({
            "xT": np.ascontiguousarray(x[:, c * TPC:(c + 1) * TPC]),
            "wqT": np.ascontiguousarray(wqT[:, hsl]),
            "wkT": np.ascontiguousarray(wkT[:, hsl]),
            "wvT": np.ascontiguousarray(wvT[:, hsl]),
            "woT": np.ascontiguousarray(woT[hsl, :]),
            "bqh": np.ascontiguousarray(bq[hsl].reshape(NHC, HD).T),
            "bkh": np.ascontiguousarray(bk[hsl].reshape(NHC, HD).T),
            "bvr": np.ascontiguousarray(bv[hsl].reshape(1, NHC * HD)),
            "bor": np.ascontiguousarray(bo.reshape(1, D)),
            "xTok": np.ascontiguousarray(
                np.asarray(L0_attn_h, dtype=np.float32).reshape(NTOK, D)[c * TPC:(c + 1) * TPC]),
            "tokids": tokid_arr,
            "dmp512": dmp_arr,
            "pad8192": padarr,
            "idbf": idbf_arr,
            "gwT": gwT,
            "gTd": np.ascontiguousarray(gp[c].T),
            "uTd": np.ascontiguousarray(up[c].T),
            "dTd": np.ascontiguousarray(dp[c].T),
            "selb": sel,
            "masks": mk,
            "idf32": idf,
        })
    return in_maps


_CACHE = {}


def kernel(**inputs):
    if "nc" not in _CACHE:
        _CACHE["nc"] = build_program(debug=False)
    nc = _CACHE["nc"]
    in_maps = prep_inputs(**inputs)
    res = run_bass_kernel_spmd(nc, in_maps, core_ids=list(range(NCORE)))
    shards = [r["outTok"] for r in res.results]
    out = np.concatenate(shards, axis=0)
    return np.ascontiguousarray(out).reshape(2, S, D).astype(np.float32)


# version 12
# speedup vs baseline: 1.2850x; 1.2714x over previous
import sys

import numpy as np

if "/opt/trn_rl_repo" not in sys.path:
    sys.path.insert(0, "/opt/trn_rl_repo")

import ml_dtypes

import concourse.bass as bass
import concourse.mybir as mybir
from concourse.bass_utils import run_bass_kernel_spmd

F32 = mybir.dt.float32
BF16 = mybir.dt.bfloat16
AF = mybir.ActivationFunctionType
ALU = mybir.AluOpType
AX = mybir.AxisListType
BFNP = ml_dtypes.bfloat16

NCORE = 8
D = 2048
NTOK = 8192
TPC = NTOK // NCORE
S = 4096
HD = 128
NHC = 2
DT = D // 128
F = 1408
FT = F // 128
BLK = 512
NB = NTOK // BLK
QBN = S // BLK
SCALE = float(1.0 / np.sqrt(128.0))
KB = 1024

SEM_NAMES = ["sP", "sV", "sA", "sPl",
             "dPre", "dX", "dAgxA", "dAgxB", "dAgx20", "dAgx21", "dAgt", "dMask",
             "dXn0", "dXn1", "dOtW0", "dOtW1", "dStg0", "dStg1", "dStg2", "dStg3",
             "dOtl0", "dOtl1", "dGa00", "dGa01", "dGa10", "dGa11", "dSc",
             "dR0", "dR1", "dRo0", "dRo1", "dRb0", "dRb1", "dRt0", "dRt1", "dRg",
             "dCl0", "dCl1", "dCo0", "dCo1", "dDbg",
             "dQkv", "dGate", "dUp", "dDown", "dWo",
             "cAgx", "cAgxB", "cAgt", "cAgk", "cRs"]
NSEL = 2560
NSB = NSEL // BLK
PADIDX = NTOK


def build_program(debug=False, upto=None):
    nc = bass.Bass("TRN2", target_bir_lowering=False, debug=False)

    def din(name, shape, dt=F32):
        return nc.dram_tensor(name, shape, dt, kind="ExternalInput").ap()

    xT = din("xT", [D, TPC])
    wqT = din("wqT", [D, NHC * HD])
    wkT = din("wkT", [D, NHC * HD])
    wvT = din("wvT", [D, NHC * HD])
    woT = din("woT", [NHC * HD, D])
    bqh = din("bqh", [HD, NHC])
    bkh = din("bkh", [HD, NHC])
    bvr = din("bvr", [1, NHC * HD])
    bor = din("bor", [1, D])
    xTok = din("xTok", [TPC, D])
    tokids = din("tokids", [128, 64], mybir.dt.int32)
    pad8192 = din("pad8192", [128, NSEL // 128], mybir.dt.int32)
    idbf = din("idbf", [128, 128], BF16)
    gwT = din("gwT", [D, 8])
    gTd = din("gTd", [D, F])
    uTd = din("uTd", [D, F])
    dTd = din("dTd", [F, D])
    selb = din("selb", [8, 128])
    dmp512 = din("dmp512", [1, BLK])
    masks = din("masks", [4, 128, BLK], BF16)
    idf32 = din("idf32", [128, 128])

    outTok = nc.dram_tensor("outTok", [TPC, D], F32, kind="ExternalOutput").ap()

    agin_x = [nc.dram_tensor(f"agin_x{h}", [D, BLK], BF16) for h in range(2)]
    agout_x = [nc.dram_tensor(f"agout_x{h}", [NCORE * D, BLK], BF16) for h in range(2)]
    agin_x2 = nc.dram_tensor("agin_x2", [TPC, D], BF16)
    xnTok = nc.dram_tensor("xnTok", [NTOK + 128, D], BF16)
    agin_t = nc.dram_tensor("agin_t", [8, TPC], F32)
    agout_t = nc.dram_tensor("agout_t", [NCORE * 8, TPC], F32)
    twtok = nc.dram_tensor("twtok", [NTOK + 128, 1], F32)
    seld = nc.dram_tensor("seld", [NSEL + NTOK, 1], mybir.dt.int32)
    tgtd = nc.dram_tensor("tgtd", [NTOK, 1], mybir.dt.int32)
    partT = nc.dram_tensor("partT", [NTOK + 128, D], F32)
    rsoutT = nc.dram_tensor("rsoutT", [TPC, D], F32)

    dbg = {}
    if debug:
        for nm, t in [("d_agx", agout_x[0]), ("d_agt", agout_t),
                      ("d_sel", seld), ("d_twtok", twtok),
                      ("d_part", partT), ("d_rs", rsoutT)]:
            dbg[nm] = (nc.dram_tensor(nm, list(t.ap().shape), t.ap().dtype,
                                      kind="ExternalOutput").ap(), t)

    ARENA = 207 * KB
    arena = nc.alloc_sbuf_tensor("arena", [128, ARENA], mybir.dt.uint8)

    I32 = mybir.dt.int32

    def cv(off, nelem, dt, parts=128):
        nb = {F32: 4, BF16: 2, I32: 4}[dt]
        assert off % 64 == 0 and off + nelem * nb <= ARENA, f"bad carve {off}"
        return arena.ap()[0:parts, off:off + nelem * nb].bitcast(dt)

    o = 0
    onesm = cv(o, 128, F32); o += 512
    ones_col_bf = cv(o, 1, BF16); o += 64
    ones_row = cv(o, 128, F32, parts=1); o += 512
    neg8 = cv(o, 1, F32); o += 64
    eps128 = cv(o, 1, F32); o += 64
    bq_sb = cv(o, 2, F32); o += 64
    bk_sb = cv(o, 2, F32); o += 64
    sel_sb = cv(o, NSEL // 128, I32); o += 128
    twg_sb = cv(o, 8, F32); o += 64
    bv_bc = cv(o, NHC * HD, BF16); o += 512
    bvr_sb = cv(o, NHC * HD, F32, parts=1); o += 1024
    idf32_sb = cv(o, 128, F32); o += 512
    idbf_sb = cv(o, 128, BF16); o += 256
    otl = [cv(o + i * 256, 128, BF16) for i in range(4)]; o += 1024
    gwT_sb = cv(o, DT * 8, F32); o += 512
    selb_sb = cv(o, 128, F32, parts=8); o += 512
    assert o <= 6 * KB

    xnb = [cv(6 * KB + i * 16 * KB, DT * BLK, BF16) for i in range(2)]
    pt_t = [cv(6 * KB + i * KB, BLK, BF16) for i in range(4)]
    pt_pair = [cv(6 * KB + i * 2 * KB, 2 * BLK, BF16) for i in range(2)]
    mask_sb = [cv(10 * KB + i * KB, BLK, BF16) for i in range(4)]
    otv = [cv(14 * KB + i * KB, BLK, BF16) for i in range(2)]
    rec_bc = cv(16 * KB, BLK, BF16)
    rrow = [cv(17 * KB + 2048 * i, BLK, F32, parts=1) for i in range(2)]

    QT_sb = cv(38 * KB, NHC * NTOK, BF16)
    KT_sb = cv(70 * KB, NHC * NTOK, BF16)
    V_sb = cv(102 * KB, 64 * NHC * HD, BF16)
    xblk = cv(38 * KB, DT * BLK, F32)
    xn32 = cv(70 * KB, DT * BLK, F32)
    sqt = [cv(102 * KB + i * 2 * KB, BLK, F32) for i in range(2)]
    rstd_t = cv(106 * KB, BLK, F32)
    twt = [cv(108 * KB + i * 2 * KB, BLK, F32, parts=8) for i in range(2)]
    uT_sb = cv(38 * KB, DT * F, BF16)
    dT_sb = cv(82 * KB, FT * D, BF16)
    wo_sb = cv(126 * KB, NHC * D, BF16)
    xtk = [cv(112 * KB + i * 4 * KB, D, BF16) for i in range(2)]
    ct_rs = [cv(38 * KB + i * 8 * KB, D, F32) for i in range(2)]
    ct_x = [cv(54 * KB + i * 8 * KB, D, F32) for i in range(2)]
    ct_o = [cv(70 * KB + i * 8 * KB, D, F32) for i in range(2)]

    gT_sb = cv(134 * KB, DT * F, BF16)
    twtile2 = [cv(134 * KB + i * 2 * KB, BLK, F32, parts=8) for i in range(2)]
    twseg = [cv(138 * KB + i * 2 * KB, BLK, F32, parts=1) for i in range(2)]
    tgtseg = [cv(142 * KB + i * 2 * KB, BLK, I32, parts=1) for i in range(2)]
    maskseg = cv(146 * KB, BLK, F32, parts=1)
    posseg = cv(148 * KB, BLK, F32, parts=1)
    t1seg = cv(150 * KB, BLK, F32, parts=1)
    u1seg = cv(152 * KB, BLK, F32, parts=1)
    tgtfseg = cv(154 * KB, BLK, F32, parts=1)
    ones512r = cv(156 * KB, BLK, F32, parts=1)
    zeros512r = cv(170 * KB, BLK, F32, parts=1)
    baserow = cv(158 * KB, 1, F32, parts=1)
    zpad = cv(159 * KB, D, BF16)
    zpad_f32 = cv(159 * KB, 1, F32)
    ot_bf = cv(142 * KB, 2 * NTOK, BF16)
    tok_sb = cv(174 * KB, 64, I32)
    tgt_sb = cv(174 * KB + 256, 64, I32)
    dmp_sb = cv(175 * KB, BLK, F32, parts=1)

    wq_sb = cv(178 * KB, DT * NHC * HD, BF16)
    wk_sb = cv(186 * KB, DT * NHC * HD, BF16)
    wv_sb = cv(194 * KB, DT * NHC * HD, BF16)
    borow = cv(178 * KB, D, F32, parts=1)
    bo_bc = cv(186 * KB, D, BF16)
    stg25 = [cv(190 * KB + i * 2 * KB, BLK, F32) for i in range(4)]
    xg = [cv(198 * KB + i * 4 * KB, D, BF16) for i in range(2)]
    hp_t = [cv(178 * KB + i * KB, BLK, BF16) for i in range(FT)]
    sg_t = [cv(188 * KB, BLK, BF16), cv(206 * KB, BLK, BF16)]
    eo_bf = [cv(189 * KB + i * 256, 128, BF16) for i in range(2)]
    eo_stage = cv(126 * KB, D, F32)

    psS = [nc.alloc_psum_tensor(f"psS{i}", [128, 2 * BLK], F32) for i in range(2)]
    pbO = [nc.alloc_psum_tensor(f"pbO{i}", [128, BLK], F32) for i in range(2)]
    pbBc = nc.alloc_psum_tensor("pbBc", [128, BLK], F32)
    pbR = nc.alloc_psum_tensor("pbR", [128, BLK], F32)
    pb = [psS[0].ap()[:, 0:BLK], psS[0].ap()[:, BLK:2 * BLK],
          psS[1].ap()[:, 0:BLK], psS[1].ap()[:, BLK:2 * BLK],
          pbBc.ap(), pbO[0].ap(), pbO[1].ap(), pbR.ap()]

    C = {k: 0 for k in SEM_NAMES}
    sem = {k: nc.semaphore(k).__enter__() for k in SEM_NAMES}

    def inc(name, ins, n=None):
        if n is None:
            n = 1 if name in ("sP", "sV", "sA") or name.startswith("c") else 16
        ins.then_inc(sem[name], n)
        C[name] += n
        return C[name]

    M = {}
    block = nc.Block().__enter__()

    wqv = wq_sb.rearrange("p (t h) -> p t h", t=DT)
    wkv = wk_sb.rearrange("p (t h) -> p t h", t=DT)
    wvv = wv_sb.rearrange("p (t h) -> p t h", t=DT)
    gwv = gwT_sb.rearrange("p (t e) -> p t e", t=DT)
    gv = gT_sb.rearrange("p (t f) -> p t f", t=DT)
    uv = uT_sb.rearrange("p (t f) -> p t f", t=DT)
    dvw = dT_sb.rearrange("p (t d) -> p t d", t=FT)
    wov = wo_sb.rearrange("p (h d) -> p h d", h=NHC)
    xnv = [bb.rearrange("p (t q) -> p t q", t=DT) for bb in xnb]

    def _(q):
        inc("dPre", q.dma_start(out=idf32_sb, in_=idf32))
        inc("dPre", q.dma_start(out=bq_sb, in_=bqh))
        inc("dPre", q.dma_start(out=bk_sb, in_=bkh))
        inc("dPre", q.dma_start(out=idbf_sb, in_=idbf))
        inc("dPre", q.dma_start(out=tok_sb, in_=tokids))
        inc("dPre", q.dma_start(out=bvr_sb, in_=bvr))
        inc("dPre", q.dma_start(out=gwv, in_=gwT.rearrange("(t p) e -> p t e", p=128)))
        inc("dPre", q.dma_start(out=selb_sb, in_=selb))
        inc("dPre", q.dma_start(out=dmp_sb, in_=dmp512))
    block.sync(_)
    PRE_ALL = C["dPre"]

    def _(q):
        inc("dQkv", q.dma_start(out=wqv, in_=wqT.rearrange("(t p) h -> p t h", p=128)))
        inc("dQkv", q.dma_start(out=wkv, in_=wkT.rearrange("(t p) h -> p t h", p=128)))
        inc("dQkv", q.dma_start(out=wvv, in_=wvT.rearrange("(t p) h -> p t h", p=128)))
    block.gpsimd(_)

    def _(v):
        v.memset(onesm, 1.0)
        v.memset(ones_col_bf, 1.0)
        v.memset(ones_row, 1.0)
        v.memset(neg8, -8.0)
        M["consts"] = inc("sV", v.memset(eps128, 1e-5))
    block.vector(_)

    def _(t):
        t.wait_ge(sem["dPre"], PRE_ALL)
        t.wait_ge(sem["sV"], M["consts"])
        M["bv_pe"] = inc("sP", t.matmul(pb[7][:, 0:NHC * HD], ones_row, bvr_sb,
                                        start=True, stop=True))
    block.tensor(_)

    def _(v):
        v.wait_ge(sem["sP"], M["bv_pe"])
        M["bv"] = inc("sV", v.tensor_copy(bv_bc, pb[7][:, 0:NHC * HD]))
    block.vector(_)

    for j in range(2):
        def _(q, j=j):
            if j > 0:
                q.wait_ge(sem["sP"], M[("Arpe", j - 1, 3)])
            inc("dX", q.dma_start(
                out=xblk.rearrange("p (t q) -> p t q", t=DT),
                in_=xT.rearrange("(t p) q -> p t q", p=128)[:, :, j * BLK:(j + 1) * BLK]))
        block.sync(_)

        def _(t, j=j):
            t.wait_ge(sem["dX"], (j + 1) * 16)
            t.wait_ge(sem["sV"], M["consts"])
            if j > 0:
                t.wait_ge(sem["sV"], M[("Axn1", j - 1)])
            for i in range(DT):
                ins = t.matmul(pb[0][:, :], onesm,
                               xblk[:, i * BLK:(i + 1) * BLK],
                               start=(i == 0), stop=(i == DT - 1))
            M[("Asum", j)] = inc("sP", ins)
        block.tensor(_)

        for i in range(DT):
            def _(sc, j=j, i=i):
                if i == 0:
                    sc.wait_ge(sem["dX"], (j + 1) * 16)
                    if j > 0:
                        sc.wait_ge(sem["sV"], M[("Atw", j - 1, 3)])
                        sc.wait_ge(sem["sP"], M[("AtwT", j - 1, 3)])
                if i >= 2:
                    sc.wait_ge(sem["sP"], M[("Asqpe", j, i - 2)])
                M[("Asq", j, i)] = inc("sA", sc.activation(
                    sqt[i % 2], xblk[:, i * BLK:(i + 1) * BLK], AF.Square))
            block.scalar(_)

            def _(t, j=j, i=i):
                t.wait_ge(sem["sA"], M[("Asq", j, i)])
                M[("Asqpe", j, i)] = inc("sP", t.matmul(
                    pb[1][:, :], onesm, sqt[i % 2],
                    start=(i == 0), stop=(i == DT - 1)))
            block.tensor(_)

        def _(v, j=j):
            v.wait_ge(sem["sP"], M[("Asum", j)])
            v.wait_ge(sem["sP"], M[("Asqpe", j, DT - 1)])
            w = inc("sV", v.tensor_scalar_mul(sqt[0], pb[0][:, :], 1.0 / D))
            w = inc("sV", v.tensor_scalar_mul(sqt[1], pb[1][:, :], 1.0 / D))
            v.wait_ge(sem["sV"], w)
            w = inc("sV", v.tensor_mul(sqt[0], sqt[0], sqt[0]))
            v.wait_ge(sem["sV"], w)
            M[("Avar", j)] = inc("sV", v.tensor_sub(sqt[1], sqt[1], sqt[0]))
        block.vector(_)

        def _(sc, j=j):
            sc.wait_ge(sem["sV"], M[("Avar", j)])
            M[("Astd", j)] = inc("sA", sc.activation(rstd_t, sqt[1], AF.Sqrt,
                                                     bias=eps128, scale=1.0))
        block.scalar(_)

        def _(v, j=j):
            v.wait_ge(sem["sA"], M[("Astd", j)])
            M[("Arstd", j)] = inc("sV", v.reciprocal(rstd_t, rstd_t))
            v.wait_ge(sem["sV"], M[("Arstd", j)])
            for i in range(DT):
                ins = v.scalar_tensor_tensor(xn32[:, i * BLK:(i + 1) * BLK],
                                             pb[0][:, :], -1.0 / D,
                                             xblk[:, i * BLK:(i + 1) * BLK],
                                             ALU.mult, ALU.add)
            w = inc("sV", ins)
            v.wait_ge(sem["sV"], w)
            for i in range(DT):
                ins = v.tensor_mul(xn32[:, i * BLK:(i + 1) * BLK],
                                   xn32[:, i * BLK:(i + 1) * BLK], rstd_t)
            M[("Axn1", j)] = inc("sV", ins)
            v.wait_ge(sem["sV"], M[("Axn1", j)])
            for i in range(DT):
                ins = v.tensor_copy(xnb[j][:, i * BLK:(i + 1) * BLK],
                                    xn32[:, i * BLK:(i + 1) * BLK])
            M[("Axncp", j)] = inc("sV", ins)
        block.vector(_)

        def _(q, j=j):
            q.wait_ge(sem["sV"], M[("Axncp", j)])
            inc("dAgxA" if j == 0 else "dAgxB", q.dma_start(
                out=agin_x[j].ap().rearrange("(t p) q -> p t q", p=128),
                in_=xnv[j]))
        block.sync(_)

        pbT = [pbO[1].ap()[:, 0:64].bitcast(BF16), pbR.ap()[:, 0:64].bitcast(BF16)]
        for tt4 in range(4):
            for i in range(DT):
                M[("Axtk", j, tt4, i)] = C["sP"] + i + 1
                M[("Axtke", j, tt4, i)] = C["sV"] + i + 1

            def _(t, j=j, tt4=tt4):
                if tt4 == 0:
                    t.wait_ge(sem["sV"], M[("Axncp", j)])
                for i in range(DT):
                    if j > 0 or tt4 > 0 or i >= 2:
                        pi = (j, tt4, i - 2) if i >= 2 else                              ((j, tt4 - 1, DT - 2 + i) if tt4 > 0 else (j - 1, 3, DT - 2 + i))
                        t.wait_ge(sem["sV"], M[("Axtke",) + pi])
                    ins = t.transpose(pbT[i % 2],
                                      xnv[j][:, i, tt4 * 128:(tt4 + 1) * 128], idbf_sb)
                    assert inc("sP", ins) == M[("Axtk", j, tt4, i)]
            block.tensor(_)

            def _(v, j=j, tt4=tt4):
                if tt4 == 0:
                    v.wait_ge(sem["sV"], M[("Axncp", j)])
                for i in range(DT):
                    v.wait_ge(sem["sP"], M[("Axtk", j, tt4, i)])
                    if i == 0 and (j > 0 or tt4 >= 2):
                        pt_ = (j, tt4 - 2) if tt4 >= 2 else (j - 1, 2 + tt4)
                        v.wait_ge(sem[f"dAgx2{tt4 % 2}"], M[("Axtkd",) + pt_])
                    ins = v.tensor_copy(xtk[tt4 % 2][:, i * 128:(i + 1) * 128], pbT[i % 2])
                    assert inc("sV", ins) == M[("Axtke", j, tt4, i)]
            block.vector(_)

            def _(q, j=j, tt4=tt4):
                q.wait_ge(sem["sV"], M[("Axtke", j, tt4, DT - 1)])
                ins = q.dma_start(out=agin_x2.ap()[j * BLK + tt4 * 128:j * BLK + (tt4 + 1) * 128, :],
                                  in_=xtk[tt4 % 2])
                M[("Axtkd", j, tt4)] = inc(f"dAgx2{tt4 % 2}", ins)
            block.sync(_)

        for tt in range(4):
            def _(t, j=j, tt=tt):
                if tt == 0:
                    t.wait_ge(sem["dPre"], PRE_ALL)
                    t.wait_ge(sem["sV"], M[("Axn1", j)])
                prev = (j, tt - 1) if tt > 0 else ((j - 1, 3) if j > 0 else None)
                if prev is not None:
                    t.wait_ge(sem["sV"], M[("ArL", prev[0], prev[1])])
                for i in range(DT):
                    ins = t.matmul(pb[4][0:128, 0:8],
                                   xn32[:, i * BLK + tt * 128:i * BLK + tt * 128 + 128],
                                   gwv[:, i, :], start=(i == 0), stop=(i == DT - 1))
                M[("Arpe", j, tt)] = inc("sP", ins)
            block.tensor(_)

            def _(v, j=j, tt=tt):
                L = sqt[0][:, 0:8]; m1 = sqt[0][:, 8:9]; k1 = sqt[0][:, 16:24]
                L2 = sqt[0][:, 24:32]; m2 = sqt[0][:, 32:33]; k2 = sqt[0][:, 40:48]
                dd = sqt[0][:, 48:49]
                v.wait_ge(sem["sP"], M[("Arpe", j, tt)])
                M[("ArL", j, tt)] = w = inc("sV", v.tensor_copy(L, pb[4][0:128, 0:8]))
                v.wait_ge(sem["sV"], w)
                w = inc("sV", v.reduce_max(m1, L, axis=AX.X))
                v.wait_ge(sem["sV"], w)
                w = inc("sV", v.tensor_scalar(k1, L, m1, None, ALU.is_equal))
                v.wait_ge(sem["sV"], w)
                w = inc("sV", v.scalar_tensor_tensor(L2, k1, -1e30, L, ALU.mult, ALU.add))
                v.wait_ge(sem["sV"], w)
                w = inc("sV", v.reduce_max(m2, L2, axis=AX.X))
                v.wait_ge(sem["sV"], w)
                v.tensor_scalar(k2, L2, m2, None, ALU.is_equal)
                M[("Ar1", j, tt)] = inc("sV", v.tensor_sub(dd, m2, m1))
            block.vector(_)

            def _(sc, j=j, tt=tt):
                sc.wait_ge(sem["sV"], M[("Ar1", j, tt)])
                M[("Ar2", j, tt)] = inc("sA", sc.activation(
                    sqt[0][:, 49:50], sqt[0][:, 48:49], AF.Exp))
            block.scalar(_)

            def _(v, j=j, tt=tt):
                k1 = sqt[0][:, 16:24]; k2 = sqt[0][:, 40:48]
                ee = sqt[0][:, 49:50]; rr = sqt[0][:, 50:51]; w2 = sqt[0][:, 51:52]
                tw = sqt[1][:, 0:8]
                v.wait_ge(sem["sA"], M[("Ar2", j, tt)])
                w = inc("sV", v.tensor_scalar_add(rr, ee, 1.0))
                v.wait_ge(sem["sV"], w)
                w = inc("sV", v.reciprocal(rr, rr))
                v.wait_ge(sem["sV"], w)
                w = inc("sV", v.tensor_mul(w2, ee, rr))
                v.wait_ge(sem["sV"], w)
                prev = (j, tt - 1) if tt > 0 else ((j - 1, 3) if j > 0 else None)
                if prev is not None:
                    v.wait_ge(sem["sP"], M[("AtwT", prev[0], prev[1])])
                w = inc("sV", v.tensor_scalar(tw, k1, rr, None, ALU.mult))
                v.wait_ge(sem["sV"], w)
                M[("Atw", j, tt)] = inc("sV", v.scalar_tensor_tensor(
                    tw, k2, w2, tw, ALU.mult, ALU.add))
            block.vector(_)

            def _(t, j=j, tt=tt):
                t.wait_ge(sem["sV"], M[("Atw", j, tt)])
                M[("AtwT", j, tt)] = inc("sP", t.transpose(
                    pb[5][0:8, 0:128], sqt[1][:, 0:8], idf32_sb))
            block.tensor(_)

            def _(v, j=j, tt=tt):
                v.wait_ge(sem["sP"], M[("AtwT", j, tt)])
                M[("Atwc", j, tt)] = inc("sV", v.tensor_copy(
                    twt[j][0:8, tt * 128:(tt + 1) * 128], pb[5][0:8, 0:128]))
            block.vector(_)

        def _(q, j=j):
            q.wait_ge(sem["sV"], M[("Atwc", j, 3)])
            inc("dAgt", q.dma_start(out=agin_t.ap()[0:8, j * BLK:(j + 1) * BLK],
                                    in_=twt[j][0:8, :]))
        block.sync(_)

    if upto == "A":
        dbgA = {}
        for nm, shape, dt in [("t_xblk", [128, DT * BLK], F32),
                              ("t_xn32", [128, DT * BLK], F32),
                              ("t_srow", [128, BLK], F32),
                              ("t_agx", [D, TPC], BF16),
                              ("t_agt", [8, TPC], F32)]:
            dbgA[nm] = nc.dram_tensor(nm, shape, dt, kind="ExternalOutput").ap()

        def _(q):
            q.wait_ge(sem["sV"], M[("Axncp", 1)])
            q.wait_ge(sem["sV"], M[("Atwc", 1, 3)])
            q.wait_ge(sem["dAgx"], C["dAgx"])
            q.wait_ge(sem["dAgt"], C["dAgt"])
            inc("dDbg", q.dma_start(out=dbgA["t_xblk"], in_=xblk))
            inc("dDbg", q.dma_start(out=dbgA["t_xn32"], in_=xn32))
            inc("dDbg", q.dma_start(out=dbgA["t_srow"], in_=rstd_t))
            inc("dDbg", q.dma_start(out=dbgA["t_agx"], in_=agin_x.ap()))
            inc("dDbg", q.dma_start(out=dbgA["t_agt"], in_=agin_t.ap()))
            q.wait_ge(sem["dDbg"], C["dDbg"])
            q.wait_ge(sem["dQkv"], C["dQkv"])
            q.wait_ge(sem["dGate"], C["dGate"])
        block.sync(_)
        block.__exit__(None, None, None)
        return nc

    def _(q):
        q.wait_ge(sem["dAgt"], 32)
        inc("cAgt", q.collective_compute(
            "AllGather", ALU.bypass, replica_groups=[list(range(NCORE))],
            ins=[agin_t.ap().opt()], outs=[agout_t.ap().opt()]), 1)
        q.wait_ge(sem["dAgxA"], 16)
        inc("cAgx", q.collective_compute(
            "AllGather", ALU.bypass, replica_groups=[list(range(NCORE))],
            ins=[agin_x[0].ap().opt()], outs=[agout_x[0].ap().opt()]), 1)
        q.wait_ge(sem["dAgxB"], 16)
        inc("cAgxB", q.collective_compute(
            "AllGather", ALU.bypass, replica_groups=[list(range(NCORE))],
            ins=[agin_x[1].ap().opt()], outs=[agout_x[1].ap().opt()]), 1)
        q.wait_ge(sem["dAgx20"], 4 * 16)
        q.wait_ge(sem["dAgx21"], 4 * 16)
        inc("cAgk", q.collective_compute(
            "AllGather", ALU.bypass, replica_groups=[list(range(NCORE))],
            ins=[agin_x2.ap().opt()], outs=[xnTok.ap()[0:NTOK, :].opt()]), 1)
    block.gpsimd(_)

    def xn_load(q, tb):
        p = tb % 2
        r, half = tb // 2, tb % 2
        src = agout_x[half].ap()[r * D:(r + 1) * D, :]
        inc(f"dXn{p}", q.dma_start(out=xnv[p], in_=src.rearrange("(t p) q -> p t q", p=128)))

    def _(q):
        q.wait_ge(sem["cAgx"], 1)
        xn_load(q, 0)
        M[("B1x", 0)] = C["dXn0"]
        q.wait_ge(sem["cAgxB"], 1)
        xn_load(q, 1)
        M[("B1x", 1)] = C["dXn1"]
    block.sync(_)

    for blk in range(NB):
        rp = blk % 2
        r, half = blk // 2, blk % 2

        def _(q, blk=blk, rp=rp, r=r, half=half):
            if blk == 0:
                q.wait_ge(sem["cAgt"], 1)
            if blk >= 2:
                q.wait_ge(sem["sP"], M[("Rpe", blk - 2)])
            ins = q.dma_start(out=twtile2[rp],
                              in_=agout_t.ap()[r * 8:(r + 1) * 8, half * BLK:(half + 1) * BLK])
            M[("Rld", blk)] = inc(f"dR{rp}", ins)
        block.sync(_)

        def _(t, blk=blk, rp=rp):
            t.wait_ge(sem[f"dR{rp}"], M[("Rld", blk)])
            if blk == 0:
                t.wait_ge(sem["sV"], M[("Axtke", 1, 3, DT - 1)])
            if blk >= 1:
                t.wait_ge(sem["sV"], M[("Rcp", blk - 1)])
            M[("Rpe", blk)] = inc("sP", t.matmul(
                pbO[1].ap()[:, :], selb_sb, twtile2[rp], start=True, stop=True))
        block.tensor(_)

        def _(v, blk=blk, rp=rp):
            if blk == 0:
                v.memset(ones512r, 1.0)
                v.memset(zeros512r, 0.0)
                v.memset(baserow, 0.0)
                v.memset(zpad, 0.0)
                v.wait_ge(sem["dPre"], PRE_ALL)
            v.wait_ge(sem["sP"], M[("Rpe", blk)])
            if blk >= 2:
                v.wait_ge(sem[f"dRo{rp}"], M[("Rod", blk - 2)])
            M[("Rcp", blk)] = inc("sV", v.tensor_copy(twseg[rp], pbO[1].ap()[0:1, :]))
            w = M[("Rcp", blk)]
            v.wait_ge(sem["sV"], w)
            w = inc("sV", v.tensor_tensor(maskseg, twseg[rp], zeros512r, ALU.is_gt))
            v.wait_ge(sem["sV"], w)
            w = inc("sV", v.tensor_tensor_scan(posseg, ones512r, maskseg, 0.0,
                                               ALU.mult, ALU.add))
            v.wait_ge(sem["sV"], w)
            w = inc("sV", v.tensor_scalar(posseg, posseg, baserow, None, ALU.add))
            v.wait_ge(sem["sV"], w)
            w = inc("sV", v.tensor_copy(baserow, posseg[0:1, BLK - 1:BLK]))
            v.wait_ge(sem["sV"], w)
            w = inc("sV", v.tensor_sub(t1seg, posseg, ones512r))
            v.wait_ge(sem["sV"], w)
            w = inc("sV", v.tensor_mul(t1seg, t1seg, maskseg))
            v.wait_ge(sem["sV"], w)
            w = inc("sV", v.tensor_sub(u1seg, ones512r, maskseg))
            v.wait_ge(sem["sV"], w)
            w = inc("sV", v.tensor_scalar(posseg, dmp_sb, float(blk * 8), None,
                                          ALU.add))
            v.wait_ge(sem["sV"], w)
            w = inc("sV", v.tensor_mul(u1seg, u1seg, posseg))
            v.wait_ge(sem["sV"], w)
            w = inc("sV", v.tensor_add(tgtfseg, u1seg, t1seg))
            v.wait_ge(sem["sV"], w)
            if blk >= 2:
                v.wait_ge(sem[f"dRt{rp}"], M[("Rtd", blk - 2)])
            M[("Rch", blk)] = inc("sV", v.tensor_copy(tgtseg[rp], tgtfseg))
        block.vector(_)

        def _(q, blk=blk, rp=rp):
            q.wait_ge(sem["sV"], M[("Rcp", blk)])
            ins = q.dma_start(out=twtok.ap()[blk * BLK:(blk + 1) * BLK, 0:1],
                              in_=twseg[rp])
            M[("Rod", blk)] = inc(f"dRo{rp}", ins)
            q.wait_ge(sem["sV"], M[("Rch", blk)])
            ins = q.dma_start(out=tgtd.ap()[blk * BLK:(blk + 1) * BLK, 0:1],
                              in_=tgtseg[rp])
            M[("Rtd", blk)] = inc(f"dRt{rp}", ins)
        block.sync(_)

    for tb in range(NB):
        p = tb % 2

        if tb >= 2:
            def _(q, tb=tb, p=p):
                q.wait_ge(sem["sP"], M[("B1pe", tb - 2)])
                xn_load(q, tb)
                M[("B1x", tb)] = C[f"dXn{p}"]
            block.sync(_)

        def _(t, tb=tb, p=p):
            t.wait_ge(sem[f"dXn{p}"], M[("B1x", tb)])
            if tb == 0:
                t.wait_ge(sem["dQkv"], 48)
            if tb >= 1:
                t.wait_ge(sem["sA"], M[("B1evA", tb - 1)])
                t.wait_ge(sem["sV"], M[("B1evV", tb - 1)])
            xv = xnv[p]
            for h in range(NHC):
                for i in range(DT):
                    ins = t.matmul(pb[0 + h][:, :], wqv[:, i, h * HD:(h + 1) * HD],
                                   xv[:, i, :], start=(i == 0), stop=(i == DT - 1))
                M[("B1q", tb, h)] = inc("sP", ins)
                for i in range(DT):
                    ins = t.matmul(pb[2 + h][:, :], wkv[:, i, h * HD:(h + 1) * HD],
                                   xv[:, i, :], start=(i == 0), stop=(i == DT - 1))
                M[("B1k", tb, h)] = inc("sP", ins)
            for s4 in range(4):
                bank, half = 4 + s4 // 2, s4 % 2
                for i in range(DT):
                    ins = t.matmul(pb[bank][:, half * 256:(half + 1) * 256],
                                   xv[:, i, s4 * 128:(s4 + 1) * 128],
                                   wvv[:, i, :], start=(i == 0), stop=(i == DT - 1))
                if s4 % 2 == 1:
                    M[("B1v", tb, s4 // 2)] = inc("sP", ins)
            M[("B1pe", tb)] = C["sP"]
        block.tensor(_)

        def _(sc, tb=tb):
            if tb == 0:
                sc.wait_ge(sem["dPre"], PRE_ALL)
            for h in range(NHC):
                sc.wait_ge(sem["sP"], M[("B1q", tb, h)])
                sc.activation(QT_sb[:, h * NTOK + tb * BLK:h * NTOK + (tb + 1) * BLK],
                              pb[0 + h][:, :], AF.Identity, bias=bq_sb[:, h:h + 1],
                              scale=1.0)
                sc.wait_ge(sem["sP"], M[("B1k", tb, h)])
                ins = sc.activation(KT_sb[:, h * NTOK + tb * BLK:h * NTOK + (tb + 1) * BLK],
                                    pb[2 + h][:, :], AF.Identity, bias=bk_sb[:, h:h + 1],
                                    scale=1.0)
            M[("B1evA", tb)] = inc("sA", ins)
        block.scalar(_)

        def _(v, tb=tb):
            if tb == 0:
                v.wait_ge(sem["sV"], M["bv"])
                v.wait_ge(sem["dAgt"], 32)
                v.wait_ge(sem["dAgx20"], 64)
                v.wait_ge(sem["dAgx21"], 64)
            for s4 in range(4):
                bank, half = 4 + s4 // 2, s4 % 2
                g = tb * 4 + s4
                v.wait_ge(sem["sP"], M[("B1v", tb, s4 // 2)])
                ins = v.tensor_add(V_sb[:, g * 256:(g + 1) * 256],
                                   pb[bank][:, half * 256:(half + 1) * 256], bv_bc)
            M[("B1evV", tb)] = inc("sV", ins)
        block.vector(_)

    def _(q):
        q.wait_ge(sem["sP"], M[("B1pe", 14)])
        for m in range(4):
            inc("dMask", q.dma_start(out=mask_sb[m], in_=masks[m]))
    block.sync(_)

    def _(g):
        from concourse.bass import IndirectOffsetOnAxis
        g.wait_ge(sem["sV"], M[("Rcp", 0)])
        inc("dRb1", g.dma_start(out=xnTok.ap()[NTOK:NTOK + 128, :], in_=zpad))
        inc("dRb1", g.dma_start(out=twtok.ap()[NTOK:NTOK + 128, 0:1], in_=zpad_f32))
        g.wait_ge(sem["sP"], M[("B1pe", NB - 1)])
        g.wait_ge(sem["cAgk"], 1)
        inc("dRb0", g.dma_start(out=seld.ap()[0:NSEL, :],
                                in_=pad8192.rearrange("p c -> (p c)").unsqueeze(1)))
        g.wait_ge(sem["dRt0"], 8 * 16)
        g.wait_ge(sem["dRt1"], 8 * 16)
        inc("dRb0", g.dma_start(out=tgt_sb,
                                in_=tgtd.ap().rearrange("(p c) o -> p (c o)", p=128)))
        g.wait_ge(sem["dPre"], PRE_ALL)
        g.wait_ge(sem["dRb0"], 2 * 16)
        for c in range(64):
            ins = g.indirect_dma_start(
                seld.ap(),
                IndirectOffsetOnAxis(tgt_sb[:, c:c + 1], 0),
                tok_sb[:, c:c + 1], None)
            inc("dRg", ins)
    block.gpsimd(_)
    M["RgScat"] = C["dRg"]

    baseP, baseA, baseV = C["sP"], C["sA"], C["sV"]
    sMp, pvMp, expMp = {}, {}, {}
    bcM, maskM = {}, {}
    rrowM, recM, recbcM, otvM = {}, {}, {}, {}
    p_, a_, v_ = baseP, baseA, baseV
    gq = 0
    gp = 0
    for u in range(4):
        for qb in range(QBN):
            nk = 4 * qb + 4
            np_ = nk // 2
            order = [("s", 0)] + ([("s", 1)] if np_ > 1 else [])
            for pk in range(np_):
                if pk + 2 < np_:
                    order.append(("s", pk + 2))
                order.append(("pv", pk))
            for kind, pk in order:
                p_ += 1
                if kind == "s":
                    sMp[gp + pk] = p_
                else:
                    pvMp[gp + pk] = p_
            for pk in range(np_):
                expMp[gp + pk] = a_ + 1
                a_ += 1
            for kt in range(4 * qb, nk):
                maskM[(gq, kt)] = v_ + 1
                v_ += 1
            bcM[gq] = p_ + 1
            p_ += 1
            rrowM[gq] = v_ + 1
            recM[gq] = v_ + 2
            recbcM[gq] = v_ + 3
            otvM[gq] = v_ + 4
            v_ += 4
            gp += np_
            gq += 1

    gq = 0
    gp = 0
    for u in range(4):
        b, hh = u // 2, u % 2
        qoff = hh * NTOK + b * S
        for qb in range(QBN):
            nk = 4 * qb + 4
            np_ = nk // 2
            g0, gq0 = gp, gq

            def _(sc, u=u, qb=qb, nk=nk, np_=np_, g0=g0):
                for pk in range(np_):
                    gg = g0 + pk
                    sc.wait_ge(sem["sP"], sMp[gg])
                    if gg >= 2:
                        sc.wait_ge(sem["sP"], pvMp[gg - 2])
                    ins = sc.activation(pt_pair[gg % 2], psS[gg % 2].ap()[:, :],
                                        AF.Exp, bias=neg8, scale=SCALE)
                    assert inc("sA", ins) == expMp[gg]
            block.scalar(_)

            def _(v, u=u, qb=qb, nk=nk, np_=np_, g0=g0, gq0=gq0):
                if u == 0 and qb == 0:
                    v.wait_ge(sem["dMask"], 64)
                for kt in range(4 * qb, nk):
                    pk = kt // 2
                    gg = g0 + pk
                    v.wait_ge(sem["sA"], expMp[gg])
                    half = pt_pair[gg % 2][:, (kt % 2) * BLK:(kt % 2 + 1) * BLK]
                    ins = v.tensor_mul(half, half, mask_sb[kt - 4 * qb])
                    assert inc("sV", ins) == maskM[(gq0, kt)]
                v.wait_ge(sem["sP"], pvMp[g0 + np_ - 1])
                if gq0 >= 2:
                    v.wait_ge(sem["sP"], bcM[gq0 - 2])
                assert inc("sV", v.tensor_copy(rrow[gq0 % 2], pb[7][0:1, :])) == rrowM[gq0]
                v.wait_ge(sem["sV"], rrowM[gq0])
                assert inc("sV", v.reciprocal(rrow[gq0 % 2], rrow[gq0 % 2])) == recM[gq0]
                v.wait_ge(sem["sP"], bcM[gq0])
                assert inc("sV", v.tensor_copy(rec_bc, pb[4][:, :])) == recbcM[gq0]
                v.wait_ge(sem["sV"], recbcM[gq0])
                if gq0 == 0:
                    v.wait_ge(sem["dRt0"], 8 * 16)
                    v.wait_ge(sem["dRt1"], 8 * 16)
                    v.wait_ge(sem["dRb1"], 2 * 16)
                ocol = (u % 2) * NTOK + (u // 2) * S + qb * BLK
                ins = v.tensor_mul(ot_bf[:, ocol:ocol + BLK],
                                   pbO[gq0 % 2].ap()[:, :], rec_bc)
                assert inc("sV", ins) == otvM[gq0]
            block.vector(_)

            def _(t, u=u, qb=qb, nk=nk, np_=np_, g0=g0, gq0=gq0, hh=hh, b=b, qoff=qoff):
                if u == 0 and qb == 0:
                    t.wait_ge(sem["sA"], M[("B1evA", NB - 1)])
                    t.wait_ge(sem["sV"], M[("B1evV", NB - 1)])
                if gq0 >= 2:
                    t.wait_ge(sem["sV"], otvM[gq0 - 2])
                if gq0 >= 1:
                    t.wait_ge(sem["sV"], rrowM[gq0 - 1])

                def s_pair(pk):
                    gg = g0 + pk
                    if gg >= 2:
                        t.wait_ge(sem["sA"], expMp[gg - 2])
                    for half in range(2):
                        kt = 2 * pk + half
                        ins = t.matmul(psS[gg % 2].ap()[:, half * BLK:(half + 1) * BLK],
                                       KT_sb[:, qoff + kt * HD:qoff + (kt + 1) * HD],
                                       QT_sb[:, qoff + qb * BLK:qoff + (qb + 1) * BLK],
                                       start=True, stop=True)
                    assert inc("sP", ins) == sMp[gg]

                def pv_block(pk):
                    gg = g0 + pk
                    t.wait_ge(sem["sA"], expMp[gg])
                    for half in range(2):
                        kt = 2 * pk + half
                        if kt >= 4 * qb:
                            t.wait_ge(sem["sV"], maskM[(gq0, kt)])
                    for half in range(2):
                        kt = 2 * pk + half
                        pth = pt_pair[gg % 2][:, half * BLK:(half + 1) * BLK]
                        t.matmul(pbO[gq0 % 2].ap()[:, :],
                                 V_sb[:, (b * 32 + kt) * 256 + hh * HD:(b * 32 + kt) * 256 + (hh + 1) * HD],
                                 pth, start=(kt == 0), stop=(kt == nk - 1))
                        ins = t.matmul(pb[7][0:1, :], ones_col_bf, pth,
                                       start=(kt == 0), stop=(kt == nk - 1))
                    assert inc("sP", ins) == pvMp[gg]

                s_pair(0)
                if np_ > 1:
                    s_pair(1)
                for pk in range(np_):
                    if pk + 2 < np_:
                        s_pair(pk + 2)
                    pv_block(pk)
                t.wait_ge(sem["sV"], recM[gq0])
                if gq0 >= 1:
                    t.wait_ge(sem["sV"], recbcM[gq0 - 1])
                ins = t.matmul(pb[4][:, :], ones_row, rrow[gq0 % 2], start=True, stop=True)
                assert inc("sP", ins) == bcM[gq0]
            block.tensor(_)

            gp += np_
            gq += 1

    M["B2peEnd"] = C["sP"]
    M["B2vEnd"] = C["sV"]
    M["B2aEnd"] = C["sA"]

    def _(q):
        q.wait_ge(sem["sP"], M["B2peEnd"])
        inc("dUp", q.dma_start(out=uv, in_=uTd.rearrange("(t p) f -> p t f", p=128)))
        inc("dDown", q.dma_start(out=dvw, in_=dTd.rearrange("(t p) d -> p t d", p=128)))
        inc("dWo", q.dma_start(out=wov, in_=woT.rearrange("(h p) d -> p h d", p=128)))
        inc("dWo", q.dma_start(out=borow, in_=bor))
        q.wait_ge(sem["dRg"], M["RgScat"])
        inc("dRg", q.dma_start(out=sel_sb,
                               in_=seld.ap()[0:NSEL, :].rearrange("(p c) o -> p (c o)", p=128)))
        M["Rg"] = C["dRg"]
    block.gpsimd(_)

    for blk in range(4):
        def _(t, blk=blk):
            if blk == 0:
                t.wait_ge(sem["sV"], recbcM[31])
                t.wait_ge(sem["dWo"], 32)
            if blk > 0:
                t.wait_ge(sem["sV"], M[("bobc2", blk - 1)])
            M[("bobc2pe", blk)] = inc("sP", t.matmul(
                pb[4][:, :], ones_row, borow[0:1, blk * BLK:(blk + 1) * BLK],
                start=True, stop=True))
        block.tensor(_)

        def _(v, blk=blk):
            v.wait_ge(sem["sP"], M[("bobc2pe", blk)])
            M[("bobc2", blk)] = inc("sV", v.tensor_scalar_mul(
                bo_bc[:, blk * BLK:(blk + 1) * BLK], pb[4][:, :], 1.0 / NCORE))
        block.vector(_)

    _stg_base = {i: C[f"dStg{i}"] for i in range(4)}
    for k in range(256):
        _stg_base[k % 4] += 16
        M[("B25st", k)] = _stg_base[k % 4]

    for tt in range(64):
        b = tt // 32
        qb25 = (tt % 32) // 4
        po = tt % 2

        def _(t, tt=tt, b=b, qb25=qb25):
            t.wait_ge(sem["sV"], otvM[(2 * b + 1) * QBN + qb25])
            if tt == 0:
                t.wait_ge(sem["sA"], M["B2aEnd"])
            for blk in range(4):
                if tt >= 1:
                    t.wait_ge(sem["sV"], M[("B25ev", tt - 1, blk)])
                for hh in range(NHC):
                    ins = t.matmul(pb[blk][:, :],
                                   ot_bf[:, hh * NTOK + tt * 128:hh * NTOK + (tt + 1) * 128],
                                   wov[:, hh, blk * BLK:(blk + 1) * BLK],
                                   start=(hh == 0), stop=(hh == NHC - 1))
            M[("B25pe", tt)] = inc("sP", ins)
        block.tensor(_)

        def _(v, tt=tt):
            v.wait_ge(sem["sP"], M[("B25pe", tt)])
            if tt == 0:
                v.wait_ge(sem["sV"], M[("bobc2", 3)])
            for blk in range(4):
                sp = (4 * tt + blk) % 4
                if 4 * tt + blk >= 4:
                    v.wait_ge(sem[f"dStg{sp}"], M[("B25st", 4 * tt + blk - 4)])
                ins = v.tensor_add(stg25[sp], pb[blk][:, :],
                                   bo_bc[:, blk * BLK:(blk + 1) * BLK])
                M[("B25ev", tt, blk)] = inc("sV", ins)
        block.vector(_)

        def _(sc, tt=tt):
            for blk in range(4):
                sp = (4 * tt + blk) % 4
                sc.wait_ge(sem["sV"], M[("B25ev", tt, blk)])
                ins = sc.dma_start(
                    out=partT.ap()[tt * 128:(tt + 1) * 128, blk * BLK:(blk + 1) * BLK],
                    in_=stg25[sp])
                assert inc(f"dStg{sp}", ins) == M[("B25st", 4 * tt + blk)]
        block.scalar(_)

    def _(q):
        q.wait_ge(sem["sP"], M[("B25pe", 63)])
        q.wait_ge(sem["sV"], M["B2vEnd"])
        q.wait_ge(sem["dR0"], 8 * 16)
        q.wait_ge(sem["dR1"], 8 * 16)
        q.wait_ge(sem["dRo0"], 8 * 16)
        q.wait_ge(sem["dRo1"], 8 * 16)
        q.wait_ge(sem["dRb0"], 2 * 16)
        q.wait_ge(sem["dRb1"], 2 * 16)
        inc("dGate", q.dma_start(out=gv, in_=gTd.rearrange("(t p) f -> p t f", p=128)))
    block.gpsimd(_)

    baseP3, baseA3, baseV3 = C["sP"], C["sA"], C["sV"]
    ga_use = {(a, b): C[f"dGa{a}{b}"] for a in range(2) for b in range(2)}
    sc_use = C["dSc"]
    p_, a_, v_ = baseP3, baseA3, baseV3
    for sb in range(NSB):
        for q4 in range(4):
            for i in range(DT):
                M[("Str", sb, q4, i)] = p_ + 1
                p_ += 1
        for ft in range(FT):
            M[("Sgu", sb, ft)] = p_ + 1
            p_ += 1
        for gd in range(64):
            M[("Sdn", sb, gd // DT, gd % DT)] = p_ + 1
            p_ += 1
            if gd > 0:
                M[("Setr", sb, (gd - 1) // DT, (gd - 1) % DT)] = p_ + 1
                p_ += 1
        M[("Setr", sb, 3, DT - 1)] = p_ + 1
        p_ += 1
        for ft in range(FT):
            M[("Ssg", sb, ft)] = a_ + 1
            a_ += 1
        for gd in range(64):
            M[("Ssc", sb, gd // DT, gd % DT)] = a_ + 1
            a_ += 1
        for q4 in range(4):
            for i in range(DT):
                M[("Sxc", sb, q4, i)] = v_ + 1
                v_ += 1
        for ft in range(FT):
            M[("Sh", sb, ft)] = v_ + 1
            v_ += 1
        for gd in range(64):
            M[("Sev", sb, gd // DT, gd % DT)] = v_ + 1
            v_ += 1
        for q4 in range(4):
            ga_use[(sb % 2, q4 % 2)] += 32
            M[("Sga", sb, q4)] = ga_use[(sb % 2, q4 % 2)]
        for q4 in range(4):
            sc_use += 16
            M[("Sscat", sb, q4)] = sc_use

    pbT2 = [psS[0].ap()[:, 0:64].bitcast(BF16), psS[0].ap()[:, BLK:BLK + 64].bitcast(BF16)]
    pbT3 = [pbBc.ap()[:, 0:64].bitcast(BF16), pbR.ap()[:, 0:64].bitcast(BF16)]

    def selv(gg):
        return sel_sb.rearrange("p (c o) -> p c o", o=1)[:, gg, :]

    for sb in range(NSB):
        sp2 = sb % 2

        def _(g, sb=sb, sp2=sp2):
            from concourse.bass import IndirectOffsetOnAxis
            if sb == 0:
                g.wait_ge(sem["dRg"], M["Rg"])
                g.wait_ge(sem["cAgk"], 1)
                g.wait_ge(sem["dRo0"], C["dRo0"])
                g.wait_ge(sem["dRo1"], C["dRo1"])
            for q4 in range(4):
                if sb > 0:
                    g.wait_ge(sem["sA"], M[("Ssc", sb - 1, q4, DT - 1)])
                    assert inc("dSc", g.indirect_dma_start(
                        partT.ap(), IndirectOffsetOnAxis(selv((sb - 1) * 4 + q4), 0),
                        eo_stage, None, compute_op=ALU.add)) == M[("Sscat", sb - 1, q4)]
                gg = sb * 4 + q4
                if gg >= 2:
                    pq = divmod(gg - 2, 4)
                    g.wait_ge(sem["sP"], M[("Str", pq[0], pq[1], DT - 1)])
                if sb >= 2 and q4 == 0:
                    g.wait_ge(sem["sA"], M[("Ssc", sb - 2, 3, DT - 1)])
                inc(f"dGa{sp2}{q4 % 2}", g.indirect_dma_start(
                    xg[q4 % 2], None, xnTok.ap(), IndirectOffsetOnAxis(selv(gg), 0)))
                inc(f"dGa{sp2}{q4 % 2}", g.indirect_dma_start(
                    twg_sb[:, sp2 * 4 + q4:sp2 * 4 + q4 + 1], None,
                    twtok.ap(), IndirectOffsetOnAxis(selv(gg), 0)))
                assert C[f"dGa{sp2}{q4 % 2}"] == M[("Sga", sb, q4)]
        block.gpsimd(_)

        def _(t, sb=sb, sp2=sp2):
            if sb == 0:
                t.wait_ge(sem["sV"], M[("B25ev", 63, 3)])
            for q4 in range(4):
                t.wait_ge(sem[f"dGa{sp2}{q4 % 2}"], M[("Sga", sb, q4)])
                for i in range(DT):
                    if sb > 0 or q4 > 0 or i >= 2:
                        pi = (sb, q4, i - 2) if i >= 2 else \
                             ((sb, q4 - 1, DT - 2 + i) if q4 > 0 else (sb - 1, 3, DT - 2 + i))
                        t.wait_ge(sem["sV"], M[("Sxc",) + pi])
                    assert inc("sP", t.transpose(
                        pbT2[i % 2],
                        xg[q4 % 2].rearrange("p (t c) -> p t c", t=DT)[:, i, :],
                        idbf_sb)) == M[("Str", sb, q4, i)]
            xv = xnv[sb % 2]
            for ft in range(FT):
                gb, ub = 0 + ft % 2, 2 + ft % 2
                if ft == 0:
                    t.wait_ge(sem["sV"], M[("Sxc", sb, 3, DT - 1)])
                if sb > 0 or ft >= 2:
                    pft = (sb, ft - 2) if ft >= 2 else (sb - 1, FT - 2 + ft)
                    t.wait_ge(sem["sA"], M[("Ssg",) + pft])
                    t.wait_ge(sem["sV"], M[("Sh",) + pft])
                if sb == 0 and ft == 0:
                    t.wait_ge(sem["dGate"], 16)
                    t.wait_ge(sem["dUp"], 16)
                for i in range(DT):
                    t.matmul(pb[gb][:, :], gv[:, i, ft * HD:(ft + 1) * HD],
                             xv[:, i, :], start=(i == 0), stop=(i == DT - 1))
                for i in range(DT):
                    ins = t.matmul(pb[ub][:, :], uv[:, i, ft * HD:(ft + 1) * HD],
                                   xv[:, i, :], start=(i == 0), stop=(i == DT - 1))
                assert inc("sP", ins) == M[("Sgu", sb, ft)]
            for gd in range(64):
                q4, d = gd // DT, gd % DT
                bank = 5 + gd % 2
                if gd == 0:
                    t.wait_ge(sem["sV"], M[("Sh", sb, FT - 1)])
                    if sb == 0:
                        t.wait_ge(sem["dDown"], 16)
                if sb > 0 or gd >= 2:
                    pgd = gd - 2 if gd >= 2 else 64 - 2 + gd
                    psb = sb if gd >= 2 else sb - 1
                    t.wait_ge(sem["sV"], M[("Sev", psb, pgd // DT, pgd % DT)])
                for ft in range(FT):
                    ins = t.matmul(
                        pb[bank][:, 0:128],
                        dvw[:, ft, d * HD:(d + 1) * HD],
                        hp_t[ft][:, q4 * 128:(q4 + 1) * 128],
                        start=(ft == 0), stop=(ft == FT - 1))
                assert inc("sP", ins) == M[("Sdn", sb, q4, d)]
                etr = gd - 1
                if etr >= 0:
                    eq4, ed = etr // DT, etr % DT
                    t.wait_ge(sem["sV"], M[("Sev", sb, eq4, ed)])
                    if sb > 0 or etr >= 2:
                        pgd = etr - 2 if etr >= 2 else 64 - 2 + etr
                        psb = sb if etr >= 2 else sb - 1
                        t.wait_ge(sem["sA"], M[("Ssc", psb, pgd // DT, pgd % DT)])
                    assert inc("sP", t.transpose(
                        pbT3[etr % 2], eo_bf[etr % 2], idbf_sb)) == M[("Setr", sb, eq4, ed)]
            t.wait_ge(sem["sV"], M[("Sev", sb, 3, DT - 1)])
            t.wait_ge(sem["sA"], M[("Ssc", sb, 3, DT - 3)])
            assert inc("sP", t.transpose(
                pbT3[63 % 2], eo_bf[63 % 2], idbf_sb)) == M[("Setr", sb, 3, DT - 1)]
        block.tensor(_)

        def _(sc, sb=sb, sp2=sp2):
            for ft in range(FT):
                if sb == 0 and ft == 0:
                    for k in range(4):
                        sc.wait_ge(sem[f"dStg{k}"], C[f"dStg{k}"])
                sc.wait_ge(sem["sP"], M[("Sgu", sb, ft)])
                assert inc("sA", sc.activation(
                    sg_t[ft % 2], pb[0 + ft % 2][:, :], AF.Silu)) == M[("Ssg", sb, ft)]
            for gd in range(64):
                q4, d = gd // DT, gd % DT
                sc.wait_ge(sem["sP"], M[("Setr", sb, q4, d)])
                if gd == 0:
                    if sb == 0:
                        sc.wait_ge(sem["sP"], M[("B25pe", 63)])
                    if sb > 0:
                        sc.wait_ge(sem["dSc"], M[("Sscat", sb - 1, 3)])
                elif d == 0:
                    sc.wait_ge(sem["dSc"], M[("Sscat", sb, q4 - 1)])
                assert inc("sA", sc.activation(
                    eo_stage[:, d * 128:(d + 1) * 128], pbT3[d % 2],
                    AF.Identity,
                    scale=twg_sb[:, sp2 * 4 + q4:sp2 * 4 + q4 + 1])) == M[("Ssc", sb, q4, d)]
        block.scalar(_)

        def _(v, sb=sb, sp2=sp2):
            for q4 in range(4):
                for i in range(DT):
                    v.wait_ge(sem["sP"], M[("Str", sb, q4, i)])
                    if sb >= 2 and q4 == 0 and i == 0:
                        v.wait_ge(sem["sP"], M[("Sgu", sb - 2, FT - 1)])
                    assert inc("sV", v.tensor_copy(
                        xnv[sb % 2][:, i, q4 * 128:(q4 + 1) * 128],
                        pbT2[i % 2])) == M[("Sxc", sb, q4, i)]
            for ft in range(FT):
                v.wait_ge(sem["sA"], M[("Ssg", sb, ft)])
                if sb == 0 and ft == 0:
                    for k in range(4):
                        v.wait_ge(sem[f"dStg{k}"], C[f"dStg{k}"])
                if sb >= 1 and ft == 0:
                    v.wait_ge(sem["sP"], M[("Sdn", sb - 1, 3, DT - 1)])
                assert inc("sV", v.tensor_mul(
                    hp_t[ft], pb[2 + ft % 2][:, :], sg_t[ft % 2])) == M[("Sh", sb, ft)]
            for gd in range(64):
                q4, d = gd // DT, gd % DT
                bank = 5 + gd % 2
                v.wait_ge(sem["sP"], M[("Sdn", sb, q4, d)])
                if sb > 0 or gd >= 2:
                    pgd = gd - 2 if gd >= 2 else 64 - 2 + gd
                    psb = sb if gd >= 2 else sb - 1
                    v.wait_ge(sem["sP"], M[("Setr", psb, pgd // DT, pgd % DT)])
                assert inc("sV", v.tensor_copy(
                    eo_bf[gd % 2],
                    pb[bank][:, 0:128])) == M[("Sev", sb, q4, d)]
        block.vector(_)

    def _(g):
        from concourse.bass import IndirectOffsetOnAxis
        for q4 in range(4):
            g.wait_ge(sem["sA"], M[("Ssc", NSB - 1, q4, DT - 1)])
            assert inc("dSc", g.indirect_dma_start(
                partT.ap(), IndirectOffsetOnAxis(selv((NSB - 1) * 4 + q4), 0),
                eo_stage, None, compute_op=ALU.add)) == M[("Sscat", NSB - 1, q4)]
    block.gpsimd(_)

    def _(q):
        for k in range(4):
            q.wait_ge(sem[f"dStg{k}"], C[f"dStg{k}"])
        q.wait_ge(sem["dSc"], C["dSc"])
        inc("cRs", q.collective_compute(
            "ReduceScatter", ALU.add, replica_groups=[list(range(NCORE))],
            ins=[partT.ap()[0:NTOK, :].opt()], outs=[rsoutT.ap().opt()]), 1)
    block.gpsimd(_)

    for it in range(8):
        p = it % 2

        def _(q, it=it, p=p):
            if it == 0:
                q.wait_ge(sem["cRs"], 1)
            if it >= 2:
                q.wait_ge(sem["sV"], M[("Ca", it - 2)])
                q.wait_ge(sem[f"dCo{p}"], M[("Co", it - 2)])
            inc(f"dCl{p}", q.dma_start(
                out=ct_rs[p], in_=rsoutT.ap()[it * 128:(it + 1) * 128, :]))
            ins = q.dma_start(
                out=ct_x[p], in_=xTok[it * 128:(it + 1) * 128, :])
            M[("Cl", it)] = inc(f"dCl{p}", ins)
        block.sync(_)

        def _(v, it=it, p=p):
            v.wait_ge(sem[f"dCl{p}"], M[("Cl", it)])
            if it >= 2:
                v.wait_ge(sem[f"dCo{p}"], M[("Co", it - 2)])
            M[("Ca", it)] = inc("sV", v.tensor_add(ct_o[p], ct_rs[p], ct_x[p]))
        block.vector(_)

        def _(sc, it=it, p=p):
            sc.wait_ge(sem["sV"], M[("Ca", it)])
            ins = sc.dma_start(out=outTok[it * 128:(it + 1) * 128, :], in_=ct_o[p])
            M[("Co", it)] = inc(f"dCo{p}", ins)
        block.scalar(_)

    if debug:
        def _(q):
            for nm, (dst, src) in dbg.items():
                inc("dDbg", q.dma_start(out=dst, in_=src.ap()))
        block.sync(_)

    def _(q):
        for k in SEM_NAMES:
            if (k.startswith("d") or k.startswith("c")) and C[k]:
                q.wait_ge(sem[k], C[k])
    block.sync(_)

    block.__exit__(None, None, None)
    return nc


def prep_inputs(L0_attn_h, L0_attn_Wq, L0_attn_bq, L0_attn_Wk, L0_attn_bk,
                L0_attn_Wv, L0_attn_bv, L0_attn_Wo, L0_attn_bo,
                L0_ffn_gate_w, L0_ffn_gate_proj, L0_ffn_up_proj, L0_ffn_down_proj):
    x = np.ascontiguousarray(np.asarray(L0_attn_h, dtype=np.float32).reshape(NTOK, D).T)
    wqT = np.ascontiguousarray(np.asarray(L0_attn_Wq).T)
    wkT = np.ascontiguousarray(np.asarray(L0_attn_Wk).T)
    wvT = np.ascontiguousarray(np.asarray(L0_attn_Wv).T)
    woT = np.ascontiguousarray(np.asarray(L0_attn_Wo).T)
    gwT = np.ascontiguousarray(np.asarray(L0_ffn_gate_w).T)
    bq = np.asarray(L0_attn_bq); bk = np.asarray(L0_attn_bk)
    bv = np.asarray(L0_attn_bv); bo = np.asarray(L0_attn_bo)
    gp = np.asarray(L0_ffn_gate_proj); up = np.asarray(L0_ffn_up_proj)
    dp = np.asarray(L0_ffn_down_proj)

    tokid_arr = np.ascontiguousarray(
        (np.arange(128)[:, None] * 64 + np.arange(64)[None, :]).astype(np.int32))
    jj = np.arange(BLK)
    dmp_arr = (NSEL + (jj % 64) * 128 + jj // 64).astype(np.float32).reshape(1, BLK)
    padarr = np.full((128, NSEL // 128), NTOK, np.int32)
    idbf_arr = np.eye(128, dtype=BFNP)
    mk = np.zeros((4, 128, BLK), dtype=BFNP)
    for m in range(4):
        i = np.arange(128)[:, None]
        jj = np.arange(BLK)[None, :]
        mk[m] = (128 * m + i <= jj).astype(BFNP)
    idf = np.eye(128, dtype=np.float32)

    in_maps = []
    for c in range(NCORE):
        hsl = slice(2 * c * HD, (2 * c + 2) * HD)
        sel = np.zeros((8, 128), np.float32)
        sel[c, :] = 1.0
        in_maps.append({
            "xT": np.ascontiguousarray(x[:, c * TPC:(c + 1) * TPC]),
            "wqT": np.ascontiguousarray(wqT[:, hsl]),
            "wkT": np.ascontiguousarray(wkT[:, hsl]),
            "wvT": np.ascontiguousarray(wvT[:, hsl]),
            "woT": np.ascontiguousarray(woT[hsl, :]),
            "bqh": np.ascontiguousarray(bq[hsl].reshape(NHC, HD).T),
            "bkh": np.ascontiguousarray(bk[hsl].reshape(NHC, HD).T),
            "bvr": np.ascontiguousarray(bv[hsl].reshape(1, NHC * HD)),
            "bor": np.ascontiguousarray(bo.reshape(1, D)),
            "xTok": np.ascontiguousarray(
                np.asarray(L0_attn_h, dtype=np.float32).reshape(NTOK, D)[c * TPC:(c + 1) * TPC]),
            "tokids": tokid_arr,
            "dmp512": dmp_arr,
            "pad8192": padarr,
            "idbf": idbf_arr,
            "gwT": gwT,
            "gTd": np.ascontiguousarray(gp[c].T),
            "uTd": np.ascontiguousarray(up[c].T),
            "dTd": np.ascontiguousarray(dp[c].T),
            "selb": sel,
            "masks": mk,
            "idf32": idf,
        })
    return in_maps


_CACHE = {}


def kernel(**inputs):
    if "nc" not in _CACHE:
        _CACHE["nc"] = build_program(debug=False)
    nc = _CACHE["nc"]
    in_maps = prep_inputs(**inputs)
    res = run_bass_kernel_spmd(nc, in_maps, core_ids=list(range(NCORE)))
    shards = [r["outTok"] for r in res.results]
    out = np.concatenate(shards, axis=0)
    return np.ascontiguousarray(out).reshape(2, S, D).astype(np.float32)


# version 13 (final)
# speedup vs baseline: 1.4152x; 1.1014x over previous
import sys

import numpy as np

if "/opt/trn_rl_repo" not in sys.path:
    sys.path.insert(0, "/opt/trn_rl_repo")

import ml_dtypes

import concourse.bass as bass
import concourse.mybir as mybir
from concourse.bass_utils import run_bass_kernel_spmd

F32 = mybir.dt.float32
BF16 = mybir.dt.bfloat16
AF = mybir.ActivationFunctionType
ALU = mybir.AluOpType
AX = mybir.AxisListType
BFNP = ml_dtypes.bfloat16

NCORE = 8
D = 2048
NTOK = 8192
TPC = NTOK // NCORE
S = 4096
HD = 128
NHC = 2
DT = D // 128
F = 1408
FT = F // 128
BLK = 512
NB = NTOK // BLK
QBN = S // BLK
SCALE = float(1.0 / np.sqrt(128.0))
KB = 1024

SEM_NAMES = ["sP", "sV", "sA", "sPl",
             "dPre", "dX", "dAgxA", "dAgxB", "dAgx20", "dAgx21", "dAgt", "dMask",
             "dXn0", "dXn1", "dOtW0", "dOtW1", "dStg0", "dStg1", "dStg2", "dStg3",
             "dOtl0", "dOtl1", "dGa00", "dGa01", "dGa10", "dGa11", "dSc",
             "dR0", "dR1", "dRo0", "dRo1", "dRb0", "dRb1", "dRt0", "dRt1", "dRg",
             "dCl0", "dCl1", "dCo0", "dCo1", "dDbg",
             "dQkv", "dGate", "dUp", "dDown", "dWo",
             "cAgx", "cAgxB", "cAgt", "cAgk", "cRs"]
NSEL = 2560
NSB = NSEL // BLK
PADIDX = NTOK


def build_program(debug=False, upto=None):
    nc = bass.Bass("TRN2", target_bir_lowering=False, debug=False)

    def din(name, shape, dt=F32):
        return nc.dram_tensor(name, shape, dt, kind="ExternalInput").ap()

    xT = din("xT", [D, TPC])
    wqT = din("wqT", [D, NHC * HD])
    wkT = din("wkT", [D, NHC * HD])
    wvT = din("wvT", [D, NHC * HD])
    woT = din("woT", [NHC * HD, D])
    bqh = din("bqh", [HD, NHC])
    bkh = din("bkh", [HD, NHC])
    bvr = din("bvr", [1, NHC * HD])
    bor = din("bor", [1, D])
    xTok = din("xTok", [TPC, D])
    tokids = din("tokids", [128, 64], mybir.dt.int32)
    pad8192 = din("pad8192", [128, NSEL // 128], mybir.dt.int32)
    idbf = din("idbf", [128, 128], BF16)
    gwT = din("gwT", [D, 8])
    gTd = din("gTd", [D, F])
    uTd = din("uTd", [D, F])
    dTd = din("dTd", [F, D])
    selb = din("selb", [8, 128])
    dmp512 = din("dmp512", [1, BLK])
    masks = din("masks", [4, 128, BLK], BF16)
    idf32 = din("idf32", [128, 128])

    outTok = nc.dram_tensor("outTok", [TPC, D], F32, kind="ExternalOutput").ap()

    agin_x = [nc.dram_tensor(f"agin_x{h}", [D, BLK], BF16) for h in range(2)]
    agout_x = [nc.dram_tensor(f"agout_x{h}", [NCORE * D, BLK], BF16) for h in range(2)]
    agin_x2 = nc.dram_tensor("agin_x2", [TPC, D], BF16)
    xnTok = nc.dram_tensor("xnTok", [NTOK + 128, D], BF16)
    agin_t = nc.dram_tensor("agin_t", [8, TPC], F32)
    agout_t = nc.dram_tensor("agout_t", [NCORE * 8, TPC], F32)
    twtok = nc.dram_tensor("twtok", [NTOK + 128, 1], F32)
    seld = nc.dram_tensor("seld", [NSEL + NTOK, 1], mybir.dt.int32)
    tgtd = nc.dram_tensor("tgtd", [NTOK, 1], mybir.dt.int32)
    partT = nc.dram_tensor("partT", [NTOK + 128, D], F32)
    rsoutT = nc.dram_tensor("rsoutT", [TPC, D], F32)

    dbg = {}
    if debug:
        for nm, t in [("d_agx", agout_x[0]), ("d_agt", agout_t),
                      ("d_sel", seld), ("d_twtok", twtok),
                      ("d_part", partT), ("d_rs", rsoutT)]:
            dbg[nm] = (nc.dram_tensor(nm, list(t.ap().shape), t.ap().dtype,
                                      kind="ExternalOutput").ap(), t)

    ARENA = 207 * KB
    arena = nc.alloc_sbuf_tensor("arena", [128, ARENA], mybir.dt.uint8)

    I32 = mybir.dt.int32

    def cv(off, nelem, dt, parts=128):
        nb = {F32: 4, BF16: 2, I32: 4}[dt]
        assert off % 64 == 0 and off + nelem * nb <= ARENA, f"bad carve {off}"
        return arena.ap()[0:parts, off:off + nelem * nb].bitcast(dt)

    o = 0
    onesm = cv(o, 128, F32); o += 512
    ones_col_bf = cv(o, 1, BF16); o += 64
    ones_row = cv(o, 128, F32, parts=1); o += 512
    neg8 = cv(o, 1, F32); o += 64
    eps128 = cv(o, 1, F32); o += 64
    bq_sb = cv(o, 2, F32); o += 64
    bk_sb = cv(o, 2, F32); o += 64
    sel_sb = cv(o, NSEL // 128, I32); o += 128
    twg_sb = cv(o, 8, F32); o += 64
    bv_bc = cv(o, NHC * HD, BF16); o += 512
    bvr_sb = cv(o, NHC * HD, F32, parts=1); o += 1024
    idf32_sb = cv(o, 128, F32); o += 512
    idbf_sb = cv(o, 128, BF16); o += 256
    otl = [cv(o + i * 256, 128, BF16) for i in range(4)]; o += 1024
    gwT_sb = cv(o, DT * 8, F32); o += 512
    selb_sb = cv(o, 128, F32, parts=8); o += 512
    assert o <= 6 * KB

    xnb = [cv(6 * KB + i * 16 * KB, DT * BLK, BF16) for i in range(2)]
    pt_t = [cv(6 * KB + i * KB, BLK, BF16) for i in range(4)]
    pt_pair = [cv(6 * KB + i * 2 * KB, 2 * BLK, BF16) for i in range(2)]
    mask_sb = [cv(10 * KB + i * KB, BLK, BF16) for i in range(4)]
    otv = [cv(14 * KB + i * KB, BLK, BF16) for i in range(2)]
    rec_bc = cv(16 * KB, BLK, BF16)
    rrow = [cv(17 * KB + 2048 * i, BLK, F32, parts=1) for i in range(2)]

    QT_sb = cv(38 * KB, NHC * NTOK, BF16)
    KT_sb = cv(70 * KB, NHC * NTOK, BF16)
    V_sb = cv(102 * KB, 64 * NHC * HD, BF16)
    xblk = cv(38 * KB, DT * BLK, F32)
    xn32 = cv(70 * KB, DT * BLK, F32)
    sqt = [cv(102 * KB + i * 2 * KB, BLK, F32) for i in range(2)]
    rstd_t = cv(106 * KB, BLK, F32)
    twt = [cv(108 * KB + i * 2 * KB, BLK, F32, parts=8) for i in range(2)]
    uT_sb = cv(38 * KB, DT * F, BF16)
    dT_sb = cv(82 * KB, FT * D, BF16)
    wo_sb = cv(126 * KB, NHC * D, BF16)
    xtk = [cv(112 * KB + i * 4 * KB, D, BF16) for i in range(2)]
    ct_rs = [cv(38 * KB + i * 8 * KB, D, F32) for i in range(2)]
    ct_x = [cv(54 * KB + i * 8 * KB, D, F32) for i in range(2)]
    ct_o = [cv(70 * KB + i * 8 * KB, D, F32) for i in range(2)]

    gT_sb = cv(134 * KB, DT * F, BF16)
    twtile2 = [cv(134 * KB + i * 2 * KB, BLK, F32, parts=8) for i in range(2)]
    twseg = [cv(138 * KB + i * 2 * KB, BLK, F32, parts=1) for i in range(2)]
    tgtseg = [cv(142 * KB + i * 2 * KB, BLK, I32, parts=1) for i in range(2)]
    maskseg = cv(146 * KB, BLK, F32, parts=1)
    posseg = cv(148 * KB, BLK, F32, parts=1)
    t1seg = cv(150 * KB, BLK, F32, parts=1)
    u1seg = cv(152 * KB, BLK, F32, parts=1)
    tgtfseg = cv(154 * KB, BLK, F32, parts=1)
    ones512r = cv(156 * KB, BLK, F32, parts=1)
    zeros512r = cv(170 * KB, BLK, F32, parts=1)
    baserow = cv(158 * KB, 1, F32, parts=1)
    zpad = cv(159 * KB, D, BF16)
    zpad_f32 = cv(159 * KB, 1, F32)
    ot_bf = cv(142 * KB, 2 * NTOK, BF16)
    tok_sb = cv(174 * KB, 64, I32)
    tgt_sb = cv(174 * KB + 256, 64, I32)
    dmp_sb = cv(175 * KB, BLK, F32, parts=1)

    wq_sb = cv(178 * KB, DT * NHC * HD, BF16)
    wk_sb = cv(186 * KB, DT * NHC * HD, BF16)
    wv_sb = cv(194 * KB, DT * NHC * HD, BF16)
    borow = cv(178 * KB, D, F32, parts=1)
    bo_bc = cv(186 * KB, D, BF16)
    stg25 = [cv(190 * KB + i * 2 * KB, BLK, F32) for i in range(4)]
    xg = [cv(198 * KB + i * 4 * KB, D, BF16) for i in range(2)]
    hp_t = [cv(178 * KB + i * KB, BLK, BF16) for i in range(FT)]
    sg_t = [cv(188 * KB, BLK, BF16), cv(206 * KB, BLK, BF16)]
    eo_bf = [cv(189 * KB + i * 256, 128, BF16) for i in range(2)]
    eo_stage = cv(126 * KB, D, F32)

    psS = [nc.alloc_psum_tensor(f"psS{i}", [128, 2 * BLK], F32) for i in range(2)]
    pbO = [nc.alloc_psum_tensor(f"pbO{i}", [128, BLK], F32) for i in range(2)]
    pbBc = nc.alloc_psum_tensor("pbBc", [128, BLK], F32)
    pbR = nc.alloc_psum_tensor("pbR", [128, BLK], F32)
    pb = [psS[0].ap()[:, 0:BLK], psS[0].ap()[:, BLK:2 * BLK],
          psS[1].ap()[:, 0:BLK], psS[1].ap()[:, BLK:2 * BLK],
          pbBc.ap(), pbO[0].ap(), pbO[1].ap(), pbR.ap()]

    C = {k: 0 for k in SEM_NAMES}
    sem = {k: nc.semaphore(k).__enter__() for k in SEM_NAMES}

    def inc(name, ins, n=None):
        if n is None:
            n = 1 if name in ("sP", "sV", "sA") or name.startswith("c") else 16
        ins.then_inc(sem[name], n)
        C[name] += n
        return C[name]

    M = {}
    block = nc.Block().__enter__()

    wqv = wq_sb.rearrange("p (t h) -> p t h", t=DT)
    wkv = wk_sb.rearrange("p (t h) -> p t h", t=DT)
    wvv = wv_sb.rearrange("p (t h) -> p t h", t=DT)
    gwv = gwT_sb.rearrange("p (t e) -> p t e", t=DT)
    gv = gT_sb.rearrange("p (t f) -> p t f", t=DT)
    uv = uT_sb.rearrange("p (t f) -> p t f", t=DT)
    dvw = dT_sb.rearrange("p (t d) -> p t d", t=FT)
    wov = wo_sb.rearrange("p (h d) -> p h d", h=NHC)
    xnv = [bb.rearrange("p (t q) -> p t q", t=DT) for bb in xnb]

    def _(q):
        inc("dPre", q.dma_start(out=idf32_sb, in_=idf32))
        inc("dPre", q.dma_start(out=bq_sb, in_=bqh))
        inc("dPre", q.dma_start(out=bk_sb, in_=bkh))
        inc("dPre", q.dma_start(out=idbf_sb, in_=idbf))
        inc("dPre", q.dma_start(out=tok_sb, in_=tokids))
        inc("dPre", q.dma_start(out=bvr_sb, in_=bvr))
        inc("dPre", q.dma_start(out=gwv, in_=gwT.rearrange("(t p) e -> p t e", p=128)))
        inc("dPre", q.dma_start(out=selb_sb, in_=selb))
        inc("dPre", q.dma_start(out=dmp_sb, in_=dmp512))
    block.sync(_)
    PRE_ALL = C["dPre"]

    def _(q):
        inc("dQkv", q.dma_start(out=wqv, in_=wqT.rearrange("(t p) h -> p t h", p=128)))
        inc("dQkv", q.dma_start(out=wkv, in_=wkT.rearrange("(t p) h -> p t h", p=128)))
        inc("dQkv", q.dma_start(out=wvv, in_=wvT.rearrange("(t p) h -> p t h", p=128)))
    block.gpsimd(_)

    def _(v):
        v.memset(onesm, 1.0)
        v.memset(ones_col_bf, 1.0)
        v.memset(ones_row, 1.0)
        v.memset(neg8, -8.0)
        M["consts"] = inc("sV", v.memset(eps128, 1e-5))
    block.vector(_)

    def _(t):
        t.wait_ge(sem["dPre"], PRE_ALL)
        t.wait_ge(sem["sV"], M["consts"])
        M["bv_pe"] = inc("sP", t.matmul(pb[7][:, 0:NHC * HD], ones_row, bvr_sb,
                                        start=True, stop=True))
    block.tensor(_)

    def _(v):
        v.wait_ge(sem["sP"], M["bv_pe"])
        M["bv"] = inc("sV", v.tensor_copy(bv_bc, pb[7][:, 0:NHC * HD]))
    block.vector(_)

    for j in range(2):
        def _(q, j=j):
            if j > 0:
                q.wait_ge(sem["sP"], M[("Arpe", j - 1, 3)])
            inc("dX", q.dma_start(
                out=xblk.rearrange("p (t q) -> p t q", t=DT),
                in_=xT.rearrange("(t p) q -> p t q", p=128)[:, :, j * BLK:(j + 1) * BLK]))
        block.sync(_)

        def _(t, j=j):
            t.wait_ge(sem["dX"], (j + 1) * 16)
            t.wait_ge(sem["sV"], M["consts"])
            if j > 0:
                t.wait_ge(sem["sV"], M[("Axn1", j - 1)])
            for i in range(DT):
                ins = t.matmul(pb[0][:, :], onesm,
                               xblk[:, i * BLK:(i + 1) * BLK],
                               start=(i == 0), stop=(i == DT - 1))
            M[("Asum", j)] = inc("sP", ins)
        block.tensor(_)

        for i in range(DT):
            def _(sc, j=j, i=i):
                if i == 0:
                    sc.wait_ge(sem["dX"], (j + 1) * 16)
                    if j > 0:
                        sc.wait_ge(sem["sV"], M[("Atw", j - 1, 3)])
                        sc.wait_ge(sem["sP"], M[("AtwT", j - 1, 3)])
                if i >= 2:
                    sc.wait_ge(sem["sP"], M[("Asqpe", j, i - 2)])
                M[("Asq", j, i)] = inc("sA", sc.activation(
                    sqt[i % 2], xblk[:, i * BLK:(i + 1) * BLK], AF.Square))
            block.scalar(_)

            def _(t, j=j, i=i):
                t.wait_ge(sem["sA"], M[("Asq", j, i)])
                M[("Asqpe", j, i)] = inc("sP", t.matmul(
                    pb[1][:, :], onesm, sqt[i % 2],
                    start=(i == 0), stop=(i == DT - 1)))
            block.tensor(_)

        def _(v, j=j):
            v.wait_ge(sem["sP"], M[("Asum", j)])
            v.wait_ge(sem["sP"], M[("Asqpe", j, DT - 1)])
            w = inc("sV", v.tensor_scalar_mul(sqt[0], pb[0][:, :], 1.0 / D))
            w = inc("sV", v.tensor_scalar_mul(sqt[1], pb[1][:, :], 1.0 / D))
            v.wait_ge(sem["sV"], w)
            w = inc("sV", v.tensor_mul(sqt[0], sqt[0], sqt[0]))
            v.wait_ge(sem["sV"], w)
            M[("Avar", j)] = inc("sV", v.tensor_sub(sqt[1], sqt[1], sqt[0]))
        block.vector(_)

        def _(sc, j=j):
            sc.wait_ge(sem["sV"], M[("Avar", j)])
            M[("Astd", j)] = inc("sA", sc.activation(rstd_t, sqt[1], AF.Sqrt,
                                                     bias=eps128, scale=1.0))
        block.scalar(_)

        def _(v, j=j):
            v.wait_ge(sem["sA"], M[("Astd", j)])
            M[("Arstd", j)] = inc("sV", v.reciprocal(rstd_t, rstd_t))
            v.wait_ge(sem["sV"], M[("Arstd", j)])
            for i in range(DT):
                ins = v.scalar_tensor_tensor(xn32[:, i * BLK:(i + 1) * BLK],
                                             pb[0][:, :], -1.0 / D,
                                             xblk[:, i * BLK:(i + 1) * BLK],
                                             ALU.mult, ALU.add)
            w = inc("sV", ins)
            v.wait_ge(sem["sV"], w)
            for i in range(DT):
                ins = v.tensor_mul(xn32[:, i * BLK:(i + 1) * BLK],
                                   xn32[:, i * BLK:(i + 1) * BLK], rstd_t)
            M[("Axn1", j)] = inc("sV", ins)
            v.wait_ge(sem["sV"], M[("Axn1", j)])
            for i in range(DT):
                ins = v.tensor_copy(xnb[j][:, i * BLK:(i + 1) * BLK],
                                    xn32[:, i * BLK:(i + 1) * BLK])
            M[("Axncp", j)] = inc("sV", ins)
        block.vector(_)

        def _(q, j=j):
            q.wait_ge(sem["sV"], M[("Axncp", j)])
            inc("dAgxA" if j == 0 else "dAgxB", q.dma_start(
                out=agin_x[j].ap().rearrange("(t p) q -> p t q", p=128),
                in_=xnv[j]))
        block.sync(_)

        pbT = [pbO[1].ap()[:, 0:64].bitcast(BF16), pbR.ap()[:, 0:64].bitcast(BF16)]
        for tt4 in range(4):
            for i in range(DT):
                M[("Axtk", j, tt4, i)] = C["sP"] + i + 1
                M[("Axtke", j, tt4, i)] = C["sV"] + i + 1

            def _(t, j=j, tt4=tt4):
                if tt4 == 0:
                    t.wait_ge(sem["sV"], M[("Axncp", j)])
                for i in range(DT):
                    if j > 0 or tt4 > 0 or i >= 2:
                        pi = (j, tt4, i - 2) if i >= 2 else                              ((j, tt4 - 1, DT - 2 + i) if tt4 > 0 else (j - 1, 3, DT - 2 + i))
                        t.wait_ge(sem["sV"], M[("Axtke",) + pi])
                    ins = t.transpose(pbT[i % 2],
                                      xnv[j][:, i, tt4 * 128:(tt4 + 1) * 128], idbf_sb)
                    assert inc("sP", ins) == M[("Axtk", j, tt4, i)]
            block.tensor(_)

            def _(v, j=j, tt4=tt4):
                if tt4 == 0:
                    v.wait_ge(sem["sV"], M[("Axncp", j)])
                for i in range(DT):
                    v.wait_ge(sem["sP"], M[("Axtk", j, tt4, i)])
                    if i == 0 and (j > 0 or tt4 >= 2):
                        pt_ = (j, tt4 - 2) if tt4 >= 2 else (j - 1, 2 + tt4)
                        v.wait_ge(sem[f"dAgx2{tt4 % 2}"], M[("Axtkd",) + pt_])
                    ins = v.tensor_copy(xtk[tt4 % 2][:, i * 128:(i + 1) * 128], pbT[i % 2])
                    assert inc("sV", ins) == M[("Axtke", j, tt4, i)]
            block.vector(_)

            def _(q, j=j, tt4=tt4):
                q.wait_ge(sem["sV"], M[("Axtke", j, tt4, DT - 1)])
                ins = q.dma_start(out=agin_x2.ap()[j * BLK + tt4 * 128:j * BLK + (tt4 + 1) * 128, :],
                                  in_=xtk[tt4 % 2])
                M[("Axtkd", j, tt4)] = inc(f"dAgx2{tt4 % 2}", ins)
            block.sync(_)

        for tt in range(4):
            def _(t, j=j, tt=tt):
                if tt == 0:
                    t.wait_ge(sem["dPre"], PRE_ALL)
                    t.wait_ge(sem["sV"], M[("Axn1", j)])
                prev = (j, tt - 1) if tt > 0 else ((j - 1, 3) if j > 0 else None)
                if prev is not None:
                    t.wait_ge(sem["sV"], M[("ArL", prev[0], prev[1])])
                for i in range(DT):
                    ins = t.matmul(pb[4][0:128, 0:8],
                                   xn32[:, i * BLK + tt * 128:i * BLK + tt * 128 + 128],
                                   gwv[:, i, :], start=(i == 0), stop=(i == DT - 1))
                M[("Arpe", j, tt)] = inc("sP", ins)
            block.tensor(_)

            def _(v, j=j, tt=tt):
                L = sqt[0][:, 0:8]; m1 = sqt[0][:, 8:9]; k1 = sqt[0][:, 16:24]
                L2 = sqt[0][:, 24:32]; m2 = sqt[0][:, 32:33]; k2 = sqt[0][:, 40:48]
                dd = sqt[0][:, 48:49]
                v.wait_ge(sem["sP"], M[("Arpe", j, tt)])
                M[("ArL", j, tt)] = w = inc("sV", v.tensor_copy(L, pb[4][0:128, 0:8]))
                v.wait_ge(sem["sV"], w)
                w = inc("sV", v.reduce_max(m1, L, axis=AX.X))
                v.wait_ge(sem["sV"], w)
                w = inc("sV", v.tensor_scalar(k1, L, m1, None, ALU.is_equal))
                v.wait_ge(sem["sV"], w)
                w = inc("sV", v.scalar_tensor_tensor(L2, k1, -1e30, L, ALU.mult, ALU.add))
                v.wait_ge(sem["sV"], w)
                w = inc("sV", v.reduce_max(m2, L2, axis=AX.X))
                v.wait_ge(sem["sV"], w)
                v.tensor_scalar(k2, L2, m2, None, ALU.is_equal)
                M[("Ar1", j, tt)] = inc("sV", v.tensor_sub(dd, m2, m1))
            block.vector(_)

            def _(sc, j=j, tt=tt):
                sc.wait_ge(sem["sV"], M[("Ar1", j, tt)])
                M[("Ar2", j, tt)] = inc("sA", sc.activation(
                    sqt[0][:, 49:50], sqt[0][:, 48:49], AF.Exp))
            block.scalar(_)

            def _(v, j=j, tt=tt):
                k1 = sqt[0][:, 16:24]; k2 = sqt[0][:, 40:48]
                ee = sqt[0][:, 49:50]; rr = sqt[0][:, 50:51]; w2 = sqt[0][:, 51:52]
                tw = sqt[1][:, 0:8]
                v.wait_ge(sem["sA"], M[("Ar2", j, tt)])
                w = inc("sV", v.tensor_scalar_add(rr, ee, 1.0))
                v.wait_ge(sem["sV"], w)
                w = inc("sV", v.reciprocal(rr, rr))
                v.wait_ge(sem["sV"], w)
                w = inc("sV", v.tensor_mul(w2, ee, rr))
                v.wait_ge(sem["sV"], w)
                prev = (j, tt - 1) if tt > 0 else ((j - 1, 3) if j > 0 else None)
                if prev is not None:
                    v.wait_ge(sem["sP"], M[("AtwT", prev[0], prev[1])])
                w = inc("sV", v.tensor_scalar(tw, k1, rr, None, ALU.mult))
                v.wait_ge(sem["sV"], w)
                M[("Atw", j, tt)] = inc("sV", v.scalar_tensor_tensor(
                    tw, k2, w2, tw, ALU.mult, ALU.add))
            block.vector(_)

            def _(t, j=j, tt=tt):
                t.wait_ge(sem["sV"], M[("Atw", j, tt)])
                M[("AtwT", j, tt)] = inc("sP", t.transpose(
                    pb[5][0:8, 0:128], sqt[1][:, 0:8], idf32_sb))
            block.tensor(_)

            def _(v, j=j, tt=tt):
                v.wait_ge(sem["sP"], M[("AtwT", j, tt)])
                M[("Atwc", j, tt)] = inc("sV", v.tensor_copy(
                    twt[j][0:8, tt * 128:(tt + 1) * 128], pb[5][0:8, 0:128]))
            block.vector(_)

        def _(q, j=j):
            q.wait_ge(sem["sV"], M[("Atwc", j, 3)])
            inc("dAgt", q.dma_start(out=agin_t.ap()[0:8, j * BLK:(j + 1) * BLK],
                                    in_=twt[j][0:8, :]))
        block.sync(_)

    if upto == "A":
        dbgA = {}
        for nm, shape, dt in [("t_xblk", [128, DT * BLK], F32),
                              ("t_xn32", [128, DT * BLK], F32),
                              ("t_srow", [128, BLK], F32),
                              ("t_agx", [D, TPC], BF16),
                              ("t_agt", [8, TPC], F32)]:
            dbgA[nm] = nc.dram_tensor(nm, shape, dt, kind="ExternalOutput").ap()

        def _(q):
            q.wait_ge(sem["sV"], M[("Axncp", 1)])
            q.wait_ge(sem["sV"], M[("Atwc", 1, 3)])
            q.wait_ge(sem["dAgx"], C["dAgx"])
            q.wait_ge(sem["dAgt"], C["dAgt"])
            inc("dDbg", q.dma_start(out=dbgA["t_xblk"], in_=xblk))
            inc("dDbg", q.dma_start(out=dbgA["t_xn32"], in_=xn32))
            inc("dDbg", q.dma_start(out=dbgA["t_srow"], in_=rstd_t))
            inc("dDbg", q.dma_start(out=dbgA["t_agx"], in_=agin_x.ap()))
            inc("dDbg", q.dma_start(out=dbgA["t_agt"], in_=agin_t.ap()))
            q.wait_ge(sem["dDbg"], C["dDbg"])
            q.wait_ge(sem["dQkv"], C["dQkv"])
            q.wait_ge(sem["dGate"], C["dGate"])
        block.sync(_)
        block.__exit__(None, None, None)
        return nc

    def _(q):
        q.wait_ge(sem["dAgt"], 32)
        inc("cAgt", q.collective_compute(
            "AllGather", ALU.bypass, replica_groups=[list(range(NCORE))],
            ins=[agin_t.ap().opt()], outs=[agout_t.ap().opt()]), 1)
        q.wait_ge(sem["dAgxA"], 16)
        inc("cAgx", q.collective_compute(
            "AllGather", ALU.bypass, replica_groups=[list(range(NCORE))],
            ins=[agin_x[0].ap().opt()], outs=[agout_x[0].ap().opt()]), 1)
        q.wait_ge(sem["dAgxB"], 16)
        inc("cAgxB", q.collective_compute(
            "AllGather", ALU.bypass, replica_groups=[list(range(NCORE))],
            ins=[agin_x[1].ap().opt()], outs=[agout_x[1].ap().opt()]), 1)
        q.wait_ge(sem["dAgx20"], 4 * 16)
        q.wait_ge(sem["dAgx21"], 4 * 16)
        inc("cAgk", q.collective_compute(
            "AllGather", ALU.bypass, replica_groups=[list(range(NCORE))],
            ins=[agin_x2.ap().opt()], outs=[xnTok.ap()[0:NTOK, :].opt()]), 1)
    block.gpsimd(_)

    def xn_load(q, tb):
        p = tb % 2
        r, half = tb // 2, tb % 2
        src = agout_x[half].ap()[r * D:(r + 1) * D, :]
        inc(f"dXn{p}", q.dma_start(out=xnv[p], in_=src.rearrange("(t p) q -> p t q", p=128)))

    def _(q):
        q.wait_ge(sem["cAgx"], 1)
        xn_load(q, 0)
        M[("B1x", 0)] = C["dXn0"]
        q.wait_ge(sem["cAgxB"], 1)
        xn_load(q, 1)
        M[("B1x", 1)] = C["dXn1"]
    block.sync(_)

    for blk in range(NB):
        rp = blk % 2
        r, half = blk // 2, blk % 2

        def _(q, blk=blk, rp=rp, r=r, half=half):
            if blk == 0:
                q.wait_ge(sem["cAgt"], 1)
            if blk >= 2:
                q.wait_ge(sem["sP"], M[("Rpe", blk - 2)])
            ins = q.dma_start(out=twtile2[rp],
                              in_=agout_t.ap()[r * 8:(r + 1) * 8, half * BLK:(half + 1) * BLK])
            M[("Rld", blk)] = inc(f"dR{rp}", ins)
        block.sync(_)

        def _(t, blk=blk, rp=rp):
            t.wait_ge(sem[f"dR{rp}"], M[("Rld", blk)])
            if blk == 0:
                t.wait_ge(sem["sV"], M[("Axtke", 1, 3, DT - 1)])
            if blk >= 1:
                t.wait_ge(sem["sV"], M[("Rcp", blk - 1)])
            M[("Rpe", blk)] = inc("sP", t.matmul(
                pbO[1].ap()[:, :], selb_sb, twtile2[rp], start=True, stop=True))
        block.tensor(_)

        def _(v, blk=blk, rp=rp):
            if blk == 0:
                v.memset(ones512r, 1.0)
                v.memset(zeros512r, 0.0)
                v.memset(baserow, 0.0)
                v.memset(zpad, 0.0)
                v.wait_ge(sem["dPre"], PRE_ALL)
            v.wait_ge(sem["sP"], M[("Rpe", blk)])
            if blk >= 2:
                v.wait_ge(sem[f"dRo{rp}"], M[("Rod", blk - 2)])
            M[("Rcp", blk)] = inc("sV", v.tensor_copy(twseg[rp], pbO[1].ap()[0:1, :]))
            w = M[("Rcp", blk)]
            v.wait_ge(sem["sV"], w)
            w = inc("sV", v.tensor_tensor(maskseg, twseg[rp], zeros512r, ALU.is_gt))
            v.wait_ge(sem["sV"], w)
            w = inc("sV", v.tensor_tensor_scan(posseg, ones512r, maskseg, 0.0,
                                               ALU.mult, ALU.add))
            v.wait_ge(sem["sV"], w)
            w = inc("sV", v.tensor_scalar(posseg, posseg, baserow, None, ALU.add))
            v.wait_ge(sem["sV"], w)
            w = inc("sV", v.tensor_copy(baserow, posseg[0:1, BLK - 1:BLK]))
            v.wait_ge(sem["sV"], w)
            w = inc("sV", v.tensor_sub(t1seg, posseg, ones512r))
            v.wait_ge(sem["sV"], w)
            w = inc("sV", v.tensor_mul(t1seg, t1seg, maskseg))
            v.wait_ge(sem["sV"], w)
            w = inc("sV", v.tensor_sub(u1seg, ones512r, maskseg))
            v.wait_ge(sem["sV"], w)
            w = inc("sV", v.tensor_scalar(posseg, dmp_sb, float(blk * 8), None,
                                          ALU.add))
            v.wait_ge(sem["sV"], w)
            w = inc("sV", v.tensor_mul(u1seg, u1seg, posseg))
            v.wait_ge(sem["sV"], w)
            w = inc("sV", v.tensor_add(tgtfseg, u1seg, t1seg))
            v.wait_ge(sem["sV"], w)
            if blk >= 2:
                v.wait_ge(sem[f"dRt{rp}"], M[("Rtd", blk - 2)])
            M[("Rch", blk)] = inc("sV", v.tensor_copy(tgtseg[rp], tgtfseg))
        block.vector(_)

        def _(q, blk=blk, rp=rp):
            q.wait_ge(sem["sV"], M[("Rcp", blk)])
            ins = q.dma_start(out=twtok.ap()[blk * BLK:(blk + 1) * BLK, 0:1],
                              in_=twseg[rp])
            M[("Rod", blk)] = inc(f"dRo{rp}", ins)
            q.wait_ge(sem["sV"], M[("Rch", blk)])
            ins = q.dma_start(out=tgtd.ap()[blk * BLK:(blk + 1) * BLK, 0:1],
                              in_=tgtseg[rp])
            M[("Rtd", blk)] = inc(f"dRt{rp}", ins)
        block.sync(_)

    for tb in range(NB):
        p = tb % 2

        if tb >= 2:
            def _(q, tb=tb, p=p):
                q.wait_ge(sem["sP"], M[("B1pe", tb - 2)])
                xn_load(q, tb)
                M[("B1x", tb)] = C[f"dXn{p}"]
            block.sync(_)

        def _(t, tb=tb, p=p):
            t.wait_ge(sem[f"dXn{p}"], M[("B1x", tb)])
            if tb == 0:
                t.wait_ge(sem["dQkv"], 48)
            if tb >= 1:
                t.wait_ge(sem["sA"], M[("B1evA", tb - 1)])
                t.wait_ge(sem["sV"], M[("B1evV", tb - 1)])
            xv = xnv[p]
            for h in range(NHC):
                for i in range(DT):
                    ins = t.matmul(pb[0 + h][:, :], wqv[:, i, h * HD:(h + 1) * HD],
                                   xv[:, i, :], start=(i == 0), stop=(i == DT - 1))
                M[("B1q", tb, h)] = inc("sP", ins)
                for i in range(DT):
                    ins = t.matmul(pb[2 + h][:, :], wkv[:, i, h * HD:(h + 1) * HD],
                                   xv[:, i, :], start=(i == 0), stop=(i == DT - 1))
                M[("B1k", tb, h)] = inc("sP", ins)
            for s4 in range(4):
                bank, half = 4 + s4 // 2, s4 % 2
                for i in range(DT):
                    ins = t.matmul(pb[bank][:, half * 256:(half + 1) * 256],
                                   xv[:, i, s4 * 128:(s4 + 1) * 128],
                                   wvv[:, i, :], start=(i == 0), stop=(i == DT - 1))
                if s4 % 2 == 1:
                    M[("B1v", tb, s4 // 2)] = inc("sP", ins)
            M[("B1pe", tb)] = C["sP"]
        block.tensor(_)

        def _(sc, tb=tb):
            if tb == 0:
                sc.wait_ge(sem["dPre"], PRE_ALL)
            for h in range(NHC):
                sc.wait_ge(sem["sP"], M[("B1q", tb, h)])
                sc.activation(QT_sb[:, h * NTOK + tb * BLK:h * NTOK + (tb + 1) * BLK],
                              pb[0 + h][:, :], AF.Identity, bias=bq_sb[:, h:h + 1],
                              scale=1.0)
                sc.wait_ge(sem["sP"], M[("B1k", tb, h)])
                ins = sc.activation(KT_sb[:, h * NTOK + tb * BLK:h * NTOK + (tb + 1) * BLK],
                                    pb[2 + h][:, :], AF.Identity, bias=bk_sb[:, h:h + 1],
                                    scale=1.0)
            M[("B1evA", tb)] = inc("sA", ins)
        block.scalar(_)

        def _(v, tb=tb):
            if tb == 0:
                v.wait_ge(sem["sV"], M["bv"])
                v.wait_ge(sem["dAgt"], 32)
                v.wait_ge(sem["dAgx20"], 64)
                v.wait_ge(sem["dAgx21"], 64)
            for s4 in range(4):
                bank, half = 4 + s4 // 2, s4 % 2
                g = tb * 4 + s4
                v.wait_ge(sem["sP"], M[("B1v", tb, s4 // 2)])
                ins = v.tensor_add(V_sb[:, g * 256:(g + 1) * 256],
                                   pb[bank][:, half * 256:(half + 1) * 256], bv_bc)
            M[("B1evV", tb)] = inc("sV", ins)
        block.vector(_)

    def _(q):
        q.wait_ge(sem["sP"], M[("B1pe", 14)])
        for m in range(4):
            inc("dMask", q.dma_start(out=mask_sb[m], in_=masks[m]))
    block.sync(_)

    def _(g):
        from concourse.bass import IndirectOffsetOnAxis
        g.wait_ge(sem["sV"], M[("Rcp", 0)])
        inc("dRb1", g.dma_start(out=xnTok.ap()[NTOK:NTOK + 128, :], in_=zpad))
        inc("dRb1", g.dma_start(out=twtok.ap()[NTOK:NTOK + 128, 0:1], in_=zpad_f32))
        g.wait_ge(sem["sP"], M[("B1pe", NB - 1)])
        g.wait_ge(sem["cAgk"], 1)
        inc("dRb0", g.dma_start(out=seld.ap()[0:NSEL, :],
                                in_=pad8192.rearrange("p c -> (p c)").unsqueeze(1)))
        g.wait_ge(sem["dRt0"], 8 * 16)
        g.wait_ge(sem["dRt1"], 8 * 16)
        inc("dRb0", g.dma_start(out=tgt_sb,
                                in_=tgtd.ap().rearrange("(p c) o -> p (c o)", p=128)))
        inc("dWo", g.dma_start(out=borow, in_=bor))
        g.wait_ge(sem["dPre"], PRE_ALL)
        g.wait_ge(sem["dRb0"], 2 * 16)
        for c in range(64):
            ins = g.indirect_dma_start(
                seld.ap(),
                IndirectOffsetOnAxis(tgt_sb[:, c:c + 1], 0),
                tok_sb[:, c:c + 1], None)
            inc("dRg", ins)
    block.gpsimd(_)
    M["RgScat"] = C["dRg"]

    baseP, baseA, baseV = C["sP"], C["sA"], C["sV"]
    sMp, pvMp, expMp = {}, {}, {}
    bcM, maskM = {}, {}
    rrowM, recM, recbcM, otvM = {}, {}, {}, {}
    p_, a_, v_ = baseP, baseA, baseV
    gq = 0
    gp = 0
    for u in range(4):
        for qb in range(QBN):
            nk = 4 * qb + 4
            np_ = nk // 2
            order = [("s", 0)] + ([("s", 1)] if np_ > 1 else [])
            for pk in range(np_):
                if pk + 2 < np_:
                    order.append(("s", pk + 2))
                order.append(("pv", pk))
            for kind, pk in order:
                p_ += 1
                if kind == "s":
                    sMp[gp + pk] = p_
                else:
                    pvMp[gp + pk] = p_
            for pk in range(np_):
                expMp[gp + pk] = a_ + 1
                a_ += 1
            for kt in range(4 * qb, nk):
                maskM[(gq, kt)] = v_ + 1
                v_ += 1
            bcM[gq] = p_ + 1
            p_ += 1
            rrowM[gq] = v_ + 1
            recM[gq] = v_ + 2
            recbcM[gq] = v_ + 3
            otvM[gq] = v_ + 4
            v_ += 4
            gp += np_
            gq += 1

    gq = 0
    gp = 0
    for u in range(4):
        b, hh = u // 2, u % 2
        qoff = hh * NTOK + b * S
        for qb in range(QBN):
            nk = 4 * qb + 4
            np_ = nk // 2
            g0, gq0 = gp, gq

            def _(sc, u=u, qb=qb, nk=nk, np_=np_, g0=g0):
                for pk in range(np_):
                    gg = g0 + pk
                    sc.wait_ge(sem["sP"], sMp[gg])
                    if gg >= 2:
                        sc.wait_ge(sem["sP"], pvMp[gg - 2])
                    ins = sc.activation(pt_pair[gg % 2], psS[gg % 2].ap()[:, :],
                                        AF.Exp, bias=neg8, scale=SCALE)
                    assert inc("sA", ins) == expMp[gg]
            block.scalar(_)

            def _(v, u=u, qb=qb, nk=nk, np_=np_, g0=g0, gq0=gq0):
                if u == 0 and qb == 0:
                    v.wait_ge(sem["dMask"], 64)
                for kt in range(4 * qb, nk):
                    pk = kt // 2
                    gg = g0 + pk
                    v.wait_ge(sem["sA"], expMp[gg])
                    half = pt_pair[gg % 2][:, (kt % 2) * BLK:(kt % 2 + 1) * BLK]
                    ins = v.tensor_mul(half, half, mask_sb[kt - 4 * qb])
                    assert inc("sV", ins) == maskM[(gq0, kt)]
                v.wait_ge(sem["sP"], pvMp[g0 + np_ - 1])
                if gq0 >= 2:
                    v.wait_ge(sem["sP"], bcM[gq0 - 2])
                assert inc("sV", v.tensor_copy(rrow[gq0 % 2], pb[7][0:1, :])) == rrowM[gq0]
                v.wait_ge(sem["sV"], rrowM[gq0])
                assert inc("sV", v.reciprocal(rrow[gq0 % 2], rrow[gq0 % 2])) == recM[gq0]
                v.wait_ge(sem["sP"], bcM[gq0])
                assert inc("sV", v.tensor_copy(rec_bc, pb[4][:, :])) == recbcM[gq0]
                v.wait_ge(sem["sV"], recbcM[gq0])
                if gq0 == 0:
                    v.wait_ge(sem["dRt0"], 8 * 16)
                    v.wait_ge(sem["dRt1"], 8 * 16)
                    v.wait_ge(sem["dRb1"], 2 * 16)
                ocol = (u % 2) * NTOK + (u // 2) * S + qb * BLK
                ins = v.tensor_mul(ot_bf[:, ocol:ocol + BLK],
                                   pbO[gq0 % 2].ap()[:, :], rec_bc)
                assert inc("sV", ins) == otvM[gq0]
            block.vector(_)

            def _(t, u=u, qb=qb, nk=nk, np_=np_, g0=g0, gq0=gq0, hh=hh, b=b, qoff=qoff):
                if u == 0 and qb == 0:
                    t.wait_ge(sem["sA"], M[("B1evA", NB - 1)])
                    t.wait_ge(sem["sV"], M[("B1evV", NB - 1)])
                if gq0 >= 2:
                    t.wait_ge(sem["sV"], otvM[gq0 - 2])
                if gq0 >= 1:
                    t.wait_ge(sem["sV"], rrowM[gq0 - 1])

                def s_pair(pk):
                    gg = g0 + pk
                    if gg >= 2:
                        t.wait_ge(sem["sA"], expMp[gg - 2])
                    for half in range(2):
                        kt = 2 * pk + half
                        ins = t.matmul(psS[gg % 2].ap()[:, half * BLK:(half + 1) * BLK],
                                       KT_sb[:, qoff + kt * HD:qoff + (kt + 1) * HD],
                                       QT_sb[:, qoff + qb * BLK:qoff + (qb + 1) * BLK],
                                       start=True, stop=True)
                    assert inc("sP", ins) == sMp[gg]

                def pv_block(pk):
                    gg = g0 + pk
                    t.wait_ge(sem["sA"], expMp[gg])
                    for half in range(2):
                        kt = 2 * pk + half
                        if kt >= 4 * qb:
                            t.wait_ge(sem["sV"], maskM[(gq0, kt)])
                    for half in range(2):
                        kt = 2 * pk + half
                        pth = pt_pair[gg % 2][:, half * BLK:(half + 1) * BLK]
                        t.matmul(pbO[gq0 % 2].ap()[:, :],
                                 V_sb[:, (b * 32 + kt) * 256 + hh * HD:(b * 32 + kt) * 256 + (hh + 1) * HD],
                                 pth, start=(kt == 0), stop=(kt == nk - 1))
                        ins = t.matmul(pb[7][0:1, :], ones_col_bf, pth,
                                       start=(kt == 0), stop=(kt == nk - 1))
                    assert inc("sP", ins) == pvMp[gg]

                s_pair(0)
                if np_ > 1:
                    s_pair(1)
                for pk in range(np_):
                    if pk + 2 < np_:
                        s_pair(pk + 2)
                    pv_block(pk)
                t.wait_ge(sem["sV"], recM[gq0])
                if gq0 >= 1:
                    t.wait_ge(sem["sV"], recbcM[gq0 - 1])
                ins = t.matmul(pb[4][:, :], ones_row, rrow[gq0 % 2], start=True, stop=True)
                assert inc("sP", ins) == bcM[gq0]
            block.tensor(_)

            gp += np_
            gq += 1

    M["B2peEnd"] = C["sP"]
    M["B2vEnd"] = C["sV"]
    M["B2aEnd"] = C["sA"]

    def _(q):
        q.wait_ge(sem["sP"], M["B2peEnd"])
        inc("dWo", q.dma_start(out=wov, in_=woT.rearrange("(h p) d -> p h d", p=128)))
        inc("dUp", q.dma_start(out=uv, in_=uTd.rearrange("(t p) f -> p t f", p=128)))
        inc("dDown", q.dma_start(out=dvw, in_=dTd.rearrange("(t p) d -> p t d", p=128)))
        q.wait_ge(sem["dRg"], M["RgScat"])
        inc("dRg", q.dma_start(out=sel_sb,
                               in_=seld.ap()[0:NSEL, :].rearrange("(p c) o -> p (c o)", p=128)))
        M["Rg"] = C["dRg"]
    block.gpsimd(_)

    for blk in range(4):
        def _(t, blk=blk):
            if blk == 0:
                t.wait_ge(sem["sV"], recbcM[31])
                t.wait_ge(sem["dWo"], 32)
            if blk > 0:
                t.wait_ge(sem["sV"], M[("bobc2", blk - 1)])
            M[("bobc2pe", blk)] = inc("sP", t.matmul(
                pb[4][:, :], ones_row, borow[0:1, blk * BLK:(blk + 1) * BLK],
                start=True, stop=True))
        block.tensor(_)

        def _(v, blk=blk):
            v.wait_ge(sem["sP"], M[("bobc2pe", blk)])
            M[("bobc2", blk)] = inc("sV", v.tensor_scalar_mul(
                bo_bc[:, blk * BLK:(blk + 1) * BLK], pb[4][:, :], 1.0 / NCORE))
        block.vector(_)

    _stg_base = {i: C[f"dStg{i}"] for i in range(4)}
    for k in range(256):
        _stg_base[k % 4] += 16
        M[("B25st", k)] = _stg_base[k % 4]

    for tt in range(64):
        b = tt // 32
        qb25 = (tt % 32) // 4
        po = tt % 2

        def _(t, tt=tt, b=b, qb25=qb25):
            t.wait_ge(sem["sV"], otvM[(2 * b + 1) * QBN + qb25])
            if tt == 0:
                t.wait_ge(sem["sA"], M["B2aEnd"])
            for blk in range(4):
                if tt >= 1:
                    t.wait_ge(sem["sV"], M[("B25ev", tt - 1, blk)])
                for hh in range(NHC):
                    ins = t.matmul(pb[blk][:, :],
                                   ot_bf[:, hh * NTOK + tt * 128:hh * NTOK + (tt + 1) * 128],
                                   wov[:, hh, blk * BLK:(blk + 1) * BLK],
                                   start=(hh == 0), stop=(hh == NHC - 1))
            M[("B25pe", tt)] = inc("sP", ins)
        block.tensor(_)

        def _(v, tt=tt):
            v.wait_ge(sem["sP"], M[("B25pe", tt)])
            if tt == 0:
                v.wait_ge(sem["sV"], M[("bobc2", 3)])
            for blk in range(4):
                sp = (4 * tt + blk) % 4
                if 4 * tt + blk >= 4:
                    v.wait_ge(sem[f"dStg{sp}"], M[("B25st", 4 * tt + blk - 4)])
                ins = v.tensor_add(stg25[sp], pb[blk][:, :],
                                   bo_bc[:, blk * BLK:(blk + 1) * BLK])
                M[("B25ev", tt, blk)] = inc("sV", ins)
        block.vector(_)

        def _(sc, tt=tt):
            for blk in range(4):
                sp = (4 * tt + blk) % 4
                sc.wait_ge(sem["sV"], M[("B25ev", tt, blk)])
                ins = sc.dma_start(
                    out=partT.ap()[tt * 128:(tt + 1) * 128, blk * BLK:(blk + 1) * BLK],
                    in_=stg25[sp])
                assert inc(f"dStg{sp}", ins) == M[("B25st", 4 * tt + blk)]
        block.scalar(_)

    def _(q):
        q.wait_ge(sem["sP"], M[("B25pe", 63)])
        q.wait_ge(sem["sV"], M["B2vEnd"])
        q.wait_ge(sem["dR0"], 8 * 16)
        q.wait_ge(sem["dR1"], 8 * 16)
        q.wait_ge(sem["dRo0"], 8 * 16)
        q.wait_ge(sem["dRo1"], 8 * 16)
        q.wait_ge(sem["dRb0"], 2 * 16)
        q.wait_ge(sem["dRb1"], 2 * 16)
        inc("dGate", q.dma_start(out=gv, in_=gTd.rearrange("(t p) f -> p t f", p=128)))
    block.gpsimd(_)

    baseP3, baseA3, baseV3 = C["sP"], C["sA"], C["sV"]
    ga_use = {(a, b): C[f"dGa{a}{b}"] for a in range(2) for b in range(2)}
    sc_use = C["dSc"]
    p_, a_, v_ = baseP3, baseA3, baseV3
    for sb in range(NSB):
        for q4 in range(4):
            for i in range(DT):
                M[("Str", sb, q4, i)] = p_ + 1
                p_ += 1
        for ft in range(FT):
            M[("Sgu", sb, ft)] = p_ + 1
            p_ += 1
        for gd in range(64):
            M[("Sdn", sb, gd // DT, gd % DT)] = p_ + 1
            p_ += 1
            if gd > 0:
                M[("Setr", sb, (gd - 1) // DT, (gd - 1) % DT)] = p_ + 1
                p_ += 1
        M[("Setr", sb, 3, DT - 1)] = p_ + 1
        p_ += 1
        for ft in range(FT):
            M[("Ssg", sb, ft)] = a_ + 1
            a_ += 1
        for gd in range(64):
            M[("Ssc", sb, gd // DT, gd % DT)] = a_ + 1
            a_ += 1
        for q4 in range(4):
            for i in range(DT):
                M[("Sxc", sb, q4, i)] = v_ + 1
                v_ += 1
        for ft in range(FT):
            M[("Sh", sb, ft)] = v_ + 1
            v_ += 1
        for gd in range(64):
            M[("Sev", sb, gd // DT, gd % DT)] = v_ + 1
            v_ += 1
        for q4 in range(4):
            ga_use[(sb % 2, q4 % 2)] += 32
            M[("Sga", sb, q4)] = ga_use[(sb % 2, q4 % 2)]
        for q4 in range(4):
            sc_use += 16
            M[("Sscat", sb, q4)] = sc_use

    pbT2 = [psS[0].ap()[:, 0:64].bitcast(BF16), psS[0].ap()[:, BLK:BLK + 64].bitcast(BF16)]
    pbT3 = [pbBc.ap()[:, 0:64].bitcast(BF16), pbR.ap()[:, 0:64].bitcast(BF16)]

    def selv(gg):
        return sel_sb.rearrange("p (c o) -> p c o", o=1)[:, gg, :]

    for sb in range(NSB):
        sp2 = sb % 2

        def _(g, sb=sb, sp2=sp2):
            from concourse.bass import IndirectOffsetOnAxis
            if sb == 0:
                g.wait_ge(sem["dRg"], M["Rg"])
                g.wait_ge(sem["cAgk"], 1)
                g.wait_ge(sem["dRo0"], C["dRo0"])
                g.wait_ge(sem["dRo1"], C["dRo1"])
            for q4 in range(4):
                if sb > 0:
                    g.wait_ge(sem["sA"], M[("Ssc", sb - 1, q4, DT - 1)])
                    assert inc("dSc", g.indirect_dma_start(
                        partT.ap(), IndirectOffsetOnAxis(selv((sb - 1) * 4 + q4), 0),
                        eo_stage, None, compute_op=ALU.add)) == M[("Sscat", sb - 1, q4)]
                gg = sb * 4 + q4
                if gg >= 2:
                    pq = divmod(gg - 2, 4)
                    g.wait_ge(sem["sP"], M[("Str", pq[0], pq[1], DT - 1)])
                if sb >= 2 and q4 == 0:
                    g.wait_ge(sem["sA"], M[("Ssc", sb - 2, 3, DT - 1)])
                inc(f"dGa{sp2}{q4 % 2}", g.indirect_dma_start(
                    xg[q4 % 2], None, xnTok.ap(), IndirectOffsetOnAxis(selv(gg), 0)))
                inc(f"dGa{sp2}{q4 % 2}", g.indirect_dma_start(
                    twg_sb[:, sp2 * 4 + q4:sp2 * 4 + q4 + 1], None,
                    twtok.ap(), IndirectOffsetOnAxis(selv(gg), 0)))
                assert C[f"dGa{sp2}{q4 % 2}"] == M[("Sga", sb, q4)]
        block.gpsimd(_)

        def _(t, sb=sb, sp2=sp2):
            if sb == 0:
                t.wait_ge(sem["sV"], M[("B25ev", 63, 3)])
            for q4 in range(4):
                t.wait_ge(sem[f"dGa{sp2}{q4 % 2}"], M[("Sga", sb, q4)])
                for i in range(DT):
                    if sb > 0 or q4 > 0 or i >= 2:
                        pi = (sb, q4, i - 2) if i >= 2 else \
                             ((sb, q4 - 1, DT - 2 + i) if q4 > 0 else (sb - 1, 3, DT - 2 + i))
                        t.wait_ge(sem["sV"], M[("Sxc",) + pi])
                    assert inc("sP", t.transpose(
                        pbT2[i % 2],
                        xg[q4 % 2].rearrange("p (t c) -> p t c", t=DT)[:, i, :],
                        idbf_sb)) == M[("Str", sb, q4, i)]
            xv = xnv[sb % 2]
            for ft in range(FT):
                gb, ub = 0 + ft % 2, 2 + ft % 2
                if ft == 0:
                    t.wait_ge(sem["sV"], M[("Sxc", sb, 3, DT - 1)])
                if sb > 0 or ft >= 2:
                    pft = (sb, ft - 2) if ft >= 2 else (sb - 1, FT - 2 + ft)
                    t.wait_ge(sem["sA"], M[("Ssg",) + pft])
                    t.wait_ge(sem["sV"], M[("Sh",) + pft])
                if sb == 0 and ft == 0:
                    t.wait_ge(sem["dGate"], 16)
                    t.wait_ge(sem["dUp"], 16)
                for i in range(DT):
                    t.matmul(pb[gb][:, :], gv[:, i, ft * HD:(ft + 1) * HD],
                             xv[:, i, :], start=(i == 0), stop=(i == DT - 1))
                for i in range(DT):
                    ins = t.matmul(pb[ub][:, :], uv[:, i, ft * HD:(ft + 1) * HD],
                                   xv[:, i, :], start=(i == 0), stop=(i == DT - 1))
                assert inc("sP", ins) == M[("Sgu", sb, ft)]
            for gd in range(64):
                q4, d = gd // DT, gd % DT
                bank = 5 + gd % 2
                if gd == 0:
                    t.wait_ge(sem["sV"], M[("Sh", sb, FT - 1)])
                    if sb == 0:
                        t.wait_ge(sem["dDown"], 16)
                if sb > 0 or gd >= 2:
                    pgd = gd - 2 if gd >= 2 else 64 - 2 + gd
                    psb = sb if gd >= 2 else sb - 1
                    t.wait_ge(sem["sV"], M[("Sev", psb, pgd // DT, pgd % DT)])
                for ft in range(FT):
                    ins = t.matmul(
                        pb[bank][:, 0:128],
                        dvw[:, ft, d * HD:(d + 1) * HD],
                        hp_t[ft][:, q4 * 128:(q4 + 1) * 128],
                        start=(ft == 0), stop=(ft == FT - 1))
                assert inc("sP", ins) == M[("Sdn", sb, q4, d)]
                etr = gd - 1
                if etr >= 0:
                    eq4, ed = etr // DT, etr % DT
                    t.wait_ge(sem["sV"], M[("Sev", sb, eq4, ed)])
                    if sb > 0 or etr >= 2:
                        pgd = etr - 2 if etr >= 2 else 64 - 2 + etr
                        psb = sb if etr >= 2 else sb - 1
                        t.wait_ge(sem["sA"], M[("Ssc", psb, pgd // DT, pgd % DT)])
                    assert inc("sP", t.transpose(
                        pbT3[etr % 2], eo_bf[etr % 2], idbf_sb)) == M[("Setr", sb, eq4, ed)]
            t.wait_ge(sem["sV"], M[("Sev", sb, 3, DT - 1)])
            t.wait_ge(sem["sA"], M[("Ssc", sb, 3, DT - 3)])
            assert inc("sP", t.transpose(
                pbT3[63 % 2], eo_bf[63 % 2], idbf_sb)) == M[("Setr", sb, 3, DT - 1)]
        block.tensor(_)

        def _(sc, sb=sb, sp2=sp2):
            for ft in range(FT):
                if sb == 0 and ft == 0:
                    for k in range(4):
                        sc.wait_ge(sem[f"dStg{k}"], C[f"dStg{k}"])
                sc.wait_ge(sem["sP"], M[("Sgu", sb, ft)])
                assert inc("sA", sc.activation(
                    sg_t[ft % 2], pb[0 + ft % 2][:, :], AF.Silu)) == M[("Ssg", sb, ft)]
            for gd in range(64):
                q4, d = gd // DT, gd % DT
                sc.wait_ge(sem["sP"], M[("Setr", sb, q4, d)])
                if gd == 0:
                    if sb == 0:
                        sc.wait_ge(sem["sP"], M[("B25pe", 63)])
                    if sb > 0:
                        sc.wait_ge(sem["dSc"], M[("Sscat", sb - 1, 3)])
                elif d == 0:
                    sc.wait_ge(sem["dSc"], M[("Sscat", sb, q4 - 1)])
                assert inc("sA", sc.activation(
                    eo_stage[:, d * 128:(d + 1) * 128], pbT3[d % 2],
                    AF.Identity,
                    scale=twg_sb[:, sp2 * 4 + q4:sp2 * 4 + q4 + 1])) == M[("Ssc", sb, q4, d)]
        block.scalar(_)

        def _(v, sb=sb, sp2=sp2):
            for q4 in range(4):
                for i in range(DT):
                    v.wait_ge(sem["sP"], M[("Str", sb, q4, i)])
                    if sb >= 2 and q4 == 0 and i == 0:
                        v.wait_ge(sem["sP"], M[("Sgu", sb - 2, FT - 1)])
                    assert inc("sV", v.tensor_copy(
                        xnv[sb % 2][:, i, q4 * 128:(q4 + 1) * 128],
                        pbT2[i % 2])) == M[("Sxc", sb, q4, i)]
            for ft in range(FT):
                v.wait_ge(sem["sA"], M[("Ssg", sb, ft)])
                if sb == 0 and ft == 0:
                    for k in range(4):
                        v.wait_ge(sem[f"dStg{k}"], C[f"dStg{k}"])
                if sb >= 1 and ft == 0:
                    v.wait_ge(sem["sP"], M[("Sdn", sb - 1, 3, DT - 1)])
                assert inc("sV", v.tensor_mul(
                    hp_t[ft], pb[2 + ft % 2][:, :], sg_t[ft % 2])) == M[("Sh", sb, ft)]
            for gd in range(64):
                q4, d = gd // DT, gd % DT
                bank = 5 + gd % 2
                v.wait_ge(sem["sP"], M[("Sdn", sb, q4, d)])
                if sb > 0 or gd >= 2:
                    pgd = gd - 2 if gd >= 2 else 64 - 2 + gd
                    psb = sb if gd >= 2 else sb - 1
                    v.wait_ge(sem["sP"], M[("Setr", psb, pgd // DT, pgd % DT)])
                assert inc("sV", v.tensor_copy(
                    eo_bf[gd % 2],
                    pb[bank][:, 0:128])) == M[("Sev", sb, q4, d)]
        block.vector(_)

    def _(g):
        from concourse.bass import IndirectOffsetOnAxis
        for q4 in range(4):
            g.wait_ge(sem["sA"], M[("Ssc", NSB - 1, q4, DT - 1)])
            assert inc("dSc", g.indirect_dma_start(
                partT.ap(), IndirectOffsetOnAxis(selv((NSB - 1) * 4 + q4), 0),
                eo_stage, None, compute_op=ALU.add)) == M[("Sscat", NSB - 1, q4)]
    block.gpsimd(_)

    def _(q):
        for k in range(4):
            q.wait_ge(sem[f"dStg{k}"], C[f"dStg{k}"])
        q.wait_ge(sem["dSc"], C["dSc"])
        inc("cRs", q.collective_compute(
            "ReduceScatter", ALU.add, replica_groups=[list(range(NCORE))],
            ins=[partT.ap()[0:NTOK, :].opt()], outs=[rsoutT.ap().opt()]), 1)
    block.gpsimd(_)

    PE_END = C["sP"]
    for it in range(8):
        p = it % 2

        def _(q, it=it, p=p):
            if it == 0:
                q.wait_ge(sem["cRs"], 1)
            if it >= 2:
                q.wait_ge(sem["sV"], M[("Ca", it - 2)])
                q.wait_ge(sem[f"dCo{p}"], M[("Co", it - 2)])
            ins = q.dma_start(
                out=ct_rs[p], in_=rsoutT.ap()[it * 128:(it + 1) * 128, :])
            M[("Cl", it)] = inc(f"dCl{p}", ins)
        block.sync(_)

        def _(g, it=it, p=p):
            if it == 0:
                g.wait_ge(sem["sP"], PE_END)
            if it >= 2:
                g.wait_ge(sem["sV"], M[("Ca", it - 2)])
            ins = g.dma_start(
                out=ct_x[p], in_=xTok[it * 128:(it + 1) * 128, :])
            M[("Cx", it)] = inc(f"dOtW{p}", ins)
        block.gpsimd(_)

        def _(v, it=it, p=p):
            v.wait_ge(sem[f"dCl{p}"], M[("Cl", it)])
            v.wait_ge(sem[f"dOtW{p}"], M[("Cx", it)])
            if it >= 2:
                v.wait_ge(sem[f"dCo{p}"], M[("Co", it - 2)])
            M[("Ca", it)] = inc("sV", v.tensor_add(ct_o[p], ct_rs[p], ct_x[p]))
        block.vector(_)

        def _(sc, it=it, p=p):
            sc.wait_ge(sem["sV"], M[("Ca", it)])
            ins = sc.dma_start(out=outTok[it * 128:(it + 1) * 128, :], in_=ct_o[p])
            M[("Co", it)] = inc(f"dCo{p}", ins)
        block.scalar(_)

    if debug:
        def _(q):
            for nm, (dst, src) in dbg.items():
                inc("dDbg", q.dma_start(out=dst, in_=src.ap()))
        block.sync(_)

    def _(q):
        for k in SEM_NAMES:
            if (k.startswith("d") or k.startswith("c")) and C[k]:
                q.wait_ge(sem[k], C[k])
    block.sync(_)

    block.__exit__(None, None, None)
    return nc


def prep_inputs(L0_attn_h, L0_attn_Wq, L0_attn_bq, L0_attn_Wk, L0_attn_bk,
                L0_attn_Wv, L0_attn_bv, L0_attn_Wo, L0_attn_bo,
                L0_ffn_gate_w, L0_ffn_gate_proj, L0_ffn_up_proj, L0_ffn_down_proj):
    x = np.ascontiguousarray(np.asarray(L0_attn_h, dtype=np.float32).reshape(NTOK, D).T)
    wqT = np.ascontiguousarray(np.asarray(L0_attn_Wq).T)
    wkT = np.ascontiguousarray(np.asarray(L0_attn_Wk).T)
    wvT = np.ascontiguousarray(np.asarray(L0_attn_Wv).T)
    woT = np.ascontiguousarray(np.asarray(L0_attn_Wo).T)
    gwT = np.ascontiguousarray(np.asarray(L0_ffn_gate_w).T)
    bq = np.asarray(L0_attn_bq); bk = np.asarray(L0_attn_bk)
    bv = np.asarray(L0_attn_bv); bo = np.asarray(L0_attn_bo)
    gp = np.asarray(L0_ffn_gate_proj); up = np.asarray(L0_ffn_up_proj)
    dp = np.asarray(L0_ffn_down_proj)

    tokid_arr = np.ascontiguousarray(
        (np.arange(128)[:, None] * 64 + np.arange(64)[None, :]).astype(np.int32))
    jj = np.arange(BLK)
    dmp_arr = (NSEL + (jj % 64) * 128 + jj // 64).astype(np.float32).reshape(1, BLK)
    padarr = np.full((128, NSEL // 128), NTOK, np.int32)
    idbf_arr = np.eye(128, dtype=BFNP)
    mk = np.zeros((4, 128, BLK), dtype=BFNP)
    for m in range(4):
        i = np.arange(128)[:, None]
        jj = np.arange(BLK)[None, :]
        mk[m] = (128 * m + i <= jj).astype(BFNP)
    idf = np.eye(128, dtype=np.float32)

    in_maps = []
    for c in range(NCORE):
        hsl = slice(2 * c * HD, (2 * c + 2) * HD)
        sel = np.zeros((8, 128), np.float32)
        sel[c, :] = 1.0
        in_maps.append({
            "xT": np.ascontiguousarray(x[:, c * TPC:(c + 1) * TPC]),
            "wqT": np.ascontiguousarray(wqT[:, hsl]),
            "wkT": np.ascontiguousarray(wkT[:, hsl]),
            "wvT": np.ascontiguousarray(wvT[:, hsl]),
            "woT": np.ascontiguousarray(woT[hsl, :]),
            "bqh": np.ascontiguousarray(bq[hsl].reshape(NHC, HD).T),
            "bkh": np.ascontiguousarray(bk[hsl].reshape(NHC, HD).T),
            "bvr": np.ascontiguousarray(bv[hsl].reshape(1, NHC * HD)),
            "bor": np.ascontiguousarray(bo.reshape(1, D)),
            "xTok": np.ascontiguousarray(
                np.asarray(L0_attn_h, dtype=np.float32).reshape(NTOK, D)[c * TPC:(c + 1) * TPC]),
            "tokids": tokid_arr,
            "dmp512": dmp_arr,
            "pad8192": padarr,
            "idbf": idbf_arr,
            "gwT": gwT,
            "gTd": np.ascontiguousarray(gp[c].T),
            "uTd": np.ascontiguousarray(up[c].T),
            "dTd": np.ascontiguousarray(dp[c].T),
            "selb": sel,
            "masks": mk,
            "idf32": idf,
        })
    return in_maps


_CACHE = {}


def kernel(**inputs):
    if "nc" not in _CACHE:
        _CACHE["nc"] = build_program(debug=False)
    nc = _CACHE["nc"]
    in_maps = prep_inputs(**inputs)
    res = run_bass_kernel_spmd(nc, in_maps, core_ids=list(range(NCORE)))
    shards = [r["outTok"] for r in res.results]
    out = np.concatenate(shards, axis=0)
    return np.ascontiguousarray(out).reshape(2, S, D).astype(np.float32)
